# version 01
# baseline (speedup 1.0000x reference)
import numpy as np
import ml_dtypes
import concourse.bass as bass
import concourse.mybir as mybir
from concourse.bass_utils import run_bass_kernel_spmd

F32 = mybir.dt.float32
BF16 = mybir.dt.bfloat16
AF = mybir.ActivationFunctionType
ALU = mybir.AluOpType
AX = mybir.AxisListType
MAGIC = 12582912.0

D = 2048
T = 1024
HALF = 512
D_IN = 4096
NH = 64
DS = 64
CONVD = 6144
NPROJ = 10304
MOE_I = 768
NEXP = 32
NCHK = 16
CHI = 128


def apx(t, off, dims):
    a = t if isinstance(t, bass.AP) else t.ap()
    return bass.AP(a.tensor, a.offset + off, [list(d) for d in dims])


class _Pending:
    __slots__ = ("entry", "em")

    def __init__(self, em, entry):
        self.em = em
        self.entry = entry

    def inc(self, semname, amt=1):
        toks = self.entry[4]
        if semname in toks:
            return (semname, toks[semname])
        sem = self.em.sems[semname]
        sem.n += amt
        self.entry[3].append((sem.h, amt))
        toks[semname] = sem.n
        return (semname, sem.n)


class _EngRec:
    def __init__(self, em, name):
        self._em = em
        self._name = name

    def __getattr__(self, meth):
        em = self._em
        name = self._name

        def call(*a, **kw):
            if meth == "transpose":
                assert a[0].tensor.dtype == a[1].tensor.dtype, \
                    ("transpose dtype", a[0].tensor.dtype, a[1].tensor.dtype)
            entry = [meth, a, kw, [], {}]
            em.streams[name].append(entry)
            return _Pending(em, entry)
        return call


class Sem:
    def __init__(self, handle):
        self.h = handle
        self.n = 0


class Em:
    ENGS = ("t", "v", "s", "q", "g")

    def __init__(self, nc, sems, dma_pool):
        self.nc = nc
        self.streams = {e: [] for e in self.ENGS}
        self.sems = sems
        self.dma_pool = dma_pool
        self.memo = {}
        for e in self.ENGS:
            setattr(self, e, _EngRec(self, e))

    def eng(self, name):
        return getattr(self, name)

    def dma_key(self, key):
        if key not in self.sems:
            if isinstance(key, str) and key in ("dw", "dx", "do", "dsh"):
                pass
            if not self.dma_pool:
                raise RuntimeError("dma sem pool exhausted")
            self.sems[key] = self.dma_pool.pop()
        return key

    def dma(self, eng, out, in_, sem="dw", **kw):
        key = self.dma_key(sem)
        return self.eng(eng).dma_start(out=out, in_=in_, **kw).inc(key, 16)

    def wait(self, eng, tok):
        if tok is None:
            return
        name, val = tok
        key = (eng, name)
        if self.memo.get(key, -1) >= val:
            return
        self.memo[key] = val
        self.streams[eng].append(["wait_ge", (self.sems[name].h, val), {}, [], {}])

    def tok(self, semname):
        return (semname, self.sems[semname].n)

    def barrier(self, engs=ENGS):
        tagmap = {"t": "st", "v": "sv", "s": "ss", "q": "sq", "g": "sg"}
        toks = []
        for e in engs:
            self.wait(e, (tagmap[e], self.sems[tagmap[e]].n))
            toks.append(self.eng(e).nop().inc(tagmap[e]))
        for e in engs:
            for tk in toks:
                self.wait(e, tk)

    def replay(self, engname, engine):
        for idx, (meth, a, kw, incs, _toks) in enumerate(self.streams[engname]):
            try:
                ins = getattr(engine, meth)(*a, **kw)
            except Exception as ex:
                raise RuntimeError("replay fail %s[%d] %s kw=%s: %s"
                                   % (engname, idx, meth, kw, ex)) from ex
            for h, amt in incs:
                ins.then_inc(h, amt)


class KB:
    def __init__(self, dbg_names=()):
        self.nc = bass.Bass("TRN2", target_bir_lowering=False)
        self.dbgreq = set(dbg_names)
        self.dbg = {}
        self.sem_ctxs = []
        sems = {}
        for n in ["st", "sv", "ss", "sq", "sg", "do"]:
            cm = self.nc.semaphore(n)
            sems[n] = Sem(cm.__enter__())
            self.sem_ctxs.append(cm)
        dma_pool = []
        for i in range(90):
            cm = self.nc.semaphore("dp%d" % i)
            dma_pool.append(Sem(cm.__enter__()))
            self.sem_ctxs.append(cm)
        self.em = Em(self.nc, sems, dma_pool)
        self.bank_free = [None] * 8
        self.slot_tok = {}
        self._uid = 0
        self._io()

    def _io(self):
        nc = self.nc

        def din(name, shape, dt=F32):
            return nc.dram_tensor(name, list(shape), dt, kind="ExternalInput").ap()
        self.x_in = din("x_in", (T, D))
        self.x_front = din("x_front", (T, D))
        self.w_in = din("w_in", (D, NPROJ))
        self.conv_w = din("conv_w", (CONVD, 4))
        self.conv_b = din("conv_b", (CONVD,))
        self.fr_conv_b = din("fr_conv_b", (CONVD,))
        self.dt_bias = din("dt_bias", (NH,))
        self.a_log = din("a_log", (NH,))
        self.d_vec = din("d_vec", (NH,))
        self.w_out = din("w_out", (D_IN, D))
        self.ffn_gamma = din("ffn_gamma", (D,))
        self.ffn_w = din("ffn_w", (D, D))
        self.l1_B = din("l1_B", (1024, D))
        self.l1_C = din("l1_C", (D, 1024))
        self.l1_D = din("l1_D", (D,))
        self.gate_w = din("gate_w", (NEXP, D))
        self.moe_wg = din("moe_wg", (NEXP, D, MOE_I))
        self.moe_wu = din("moe_wu", (NEXP, D, MOE_I))
        self.moe_wd = din("moe_wd", (NEXP, MOE_I, D))
        self.cwg = din("cwg", (NCHK, D, CHI))
        self.cwu = din("cwu", (NCHK, D, CHI))
        self.cwd = din("cwd", (NCHK, CHI, D))
        self.ident_f = din("ident_f", (128, 128))
        self.ident_b = din("ident_b", (128, 128), BF16)
        self.ones_f = din("ones_f", (128, 128))
        self.triu_neg_d = din("triu_neg", (128, 128))
        self.oh8_d = din("oh8", (8, 8 * 128))
        self.oh32_d = din("oh32", (32, 32 * 128), BF16)
        self.oh16s_d = din("oh16s", (16, 16 * 128), BF16)
        self.nw_bcast_d = din("nw_bcast", (128, D_IN))
        self.out = nc.dram_tensor("out", [T, D], F32, kind="ExternalOutput").ap()
        self.xn_dram = nc.dram_tensor("xn_scr", [16 * 128 * T], BF16, kind="Internal").ap()
        self.cum_dram = nc.dram_tensor("cum_scr", [NH * 128], F32, kind="Internal").ap()
        self.ec_dram = nc.dram_tensor("ec_scr", [NH * 128], F32, kind="Internal").ap()
        self.sz_dram = nc.dram_tensor("sz_scr", [T * D_IN], BF16, kind="Internal").ap()
        self.c64_dram = nc.dram_tensor("c64_scr", [64 * NH * HALF], BF16, kind="Internal").ap()
        self.b64_dram = nc.dram_tensor("b64_scr", [64 * 16 * HALF], BF16, kind="Internal").ap()

    def tap(self, name, ap_, tok=None, eng="q"):
        if name not in self.dbgreq:
            return
        shp = list(ap_.shape)
        o = self.nc.dram_tensor("dbg_" + name, shp, ap_.tensor.dtype, kind="ExternalOutput").ap()
        self.dbg[name] = "dbg_" + name
        if tok is not None:
            self.em.wait(eng, tok)
        ttap = self.em.dma(eng, o, ap_, "do")
        for e in Em.ENGS:
            self.em.wait(e, ttap)

    def etok(self, eng, sem):
        em = self.em
        for entry in reversed(em.streams[eng]):
            if entry[0] not in ("wait_ge", "nop"):
                return _Pending(em, entry).inc(sem)
        return None

    def vtok(self):
        return self.etok("v", "sv")

    def stok(self):
        return self.etok("s", "ss")

    def ttok(self):
        return self.etok("t", "st")

    def vf(self):
        self.em.wait("v", self.vtok())

    def sf(self):
        self.em.wait("s", self.stok())

    def use_bank(self, i):
        self.em.wait("t", self.bank_free[i])

    def sb(self, nm, shape, dt):
        self._uid += 1
        return self.nc.sbuf_tensor("%s_u%d" % (nm, self._uid), shape, dt)

    def sguard(self, eng, key):
        for tok in self.slot_tok.get(key, ()):
            self.em.wait(eng, tok)

    def sset(self, key, *toks):
        self.slot_tok[key] = [t for t in toks if t is not None]

    def bank_done(self, i, tok):
        self.bank_free[i] = tok

    def tr16(self, src, outs, ident, banks):
        em, pb = self.em, self.pb
        for fq in range(4):
            bi = banks[fq % 2]
            self.use_bank(bi)
            last = None
            for k in range(4):
                f = fq * 4 + k
                last = em.t.transpose(pb[bi][:, k * 128:(k + 1) * 128],
                                      src[:, f * 128:f * 128 + 128], ident).inc("st")
            em.wait("v", last)
            evsrc = pb[bi][:, 0:512]
            if len(outs[fq].shape) == 3:
                evsrc = evsrc.rearrange("p (a b) -> p a b", a=4)
            tv = em.v.tensor_copy(outs[fq], evsrc).inc("sv")
            self.bank_done(bi, tv)

    def fm_out_ap(self, dst, fq, tok_off, stride):
        return apx(dst, (fq * 4) * stride + tok_off,
                   [[16 * stride, 128], [stride, 4], [1, 128]])

    def build(self, until="all"):
        nc, em = self.nc, self.em
        self.idf = nc.alloc_sbuf_tensor("idf", [128, 128], F32)
        self.idb = nc.alloc_sbuf_tensor("idb", [128, 128], BF16)
        self.onef = nc.alloc_sbuf_tensor("onef", [128, 128], F32)
        self.oh8 = nc.alloc_sbuf_tensor("oh8_sb", [8, 8 * 128], F32)
        self.prm = nc.alloc_sbuf_tensor("prm", [128, 512], F32)
        self.convp = nc.alloc_sbuf_tensor("convp", [128, 288], F32)
        self.gdelta = nc.alloc_sbuf_tensor("gdelta", [128, 16 * T], BF16)
        self.pb = [nc.alloc_psum_tensor("pb%d" % i, [128, 512], F32) for i in range(6)]
        self.pb.append(nc.alloc_psum_tensor("pbb0", [128, 1024], BF16))
        self.pb.append(nc.alloc_psum_tensor("pbb1", [128, 1024], BF16))
        self.front_limit = "all"
        if until in ("front_ln", "front_winx", "front_dt", "front_tok", "front_S", "front_dtA", "front_dtB", "front_dtC"):
            self.front_limit = until[6:]
            until = "front"
        order = ["consts", "ffn", "front", "l0", "l1"]
        stop = len(order) if until == "all" else order.index(until) + 1
        todo = set(order[:stop])
        self.consts()
        if "ffn" in todo:
            self.l0_ln_ffn()
            em.barrier()
        if "front" in todo or "l0" in todo:
            with nc.sbuf_tensor("H_ar", [128, NH * DS + 512], F32) as H_ar, \
                 nc.sbuf_tensor("xpre", [128, 48], BF16) as xpre:
                self.H = H_ar
                self.xpre = xpre
                em.v.memset(xpre[:], 0.0)
                self.vf()
                if "front" in todo:
                    self.front()
                    em.barrier()
                if "l0" in todo:
                    self.l0_attn()
        em.barrier()
        if "l1" in todo:
            self.l1()
            em.barrier()
        self.outstage()
        self.finish()
        return self

    def consts(self):
        em, prm, pb = self.em, self.prm, self.pb
        em.dma("q", self.idf[:], self.ident_f, "consts")
        em.dma("q", self.idb[:], self.ident_b, "consts")
        em.dma("q", self.onef[:], self.ones_f, "consts")
        em.dma("q", self.oh8[:], self.oh8_d, "consts")
        for f in range(16):
            em.dma("q", prm[:, f:f + 1], self.ffn_gamma[f * 128:(f + 1) * 128, None], "consts")
        em.dma("q", prm[0:NH, 240:241], self.dt_bias[:, None], "consts")
        em.dma("q", prm[0:NH, 241:242], self.a_log[:, None], "consts")
        em.dma("q", prm[0:NH, 243:244], self.d_vec[:, None], "consts")
        em.dma("q", self.convp[:, 0:192].rearrange("p (c k) -> p c k", c=48),
               apx(self.conv_w, 0, [[4, 128], [128 * 4, 48], [1, 4]]), "consts")
        em.dma("q", apx(prm, 352, [[512, 48], [1, 128]]), 
               apx(self.conv_b, 0, [[128, 48], [1, 128]]), "cvb1")
        tok = em.tok("consts")
        tokb = em.tok("cvb1")
        for e in Em.ENGS:
            em.wait(e, tok)
            em.wait(e, tokb)
        t1 = em.t.transpose(pb[0][:, 0:48], prm[0:48, 352:480], self.idf[0:48, 0:48]).inc("st")
        em.wait("v", t1)
        tv = em.v.tensor_copy(self.convp[:, 192:240], pb[0][:, 0:48]).inc("sv")
        em.wait("q", tv)
        em.dma("q", apx(prm, 352, [[512, 48], [1, 128]]),
               apx(self.fr_conv_b, 0, [[128, 48], [1, 128]]), "cvb2")
        tok = em.tok("cvb2")
        em.wait("t", tok)
        t1 = em.t.transpose(pb[1][:, 0:48], prm[0:48, 352:480], self.idf[0:48, 0:48]).inc("st")
        em.wait("v", t1)
        tv = em.v.tensor_copy(self.convp[:, 240:288], pb[1][:, 0:48]).inc("sv")
        em.wait("t", tv)
        self.bank_done(0, tv)
        self.bank_done(1, tv)
        em.v.memset(prm[:, 504:505], 1e-5)
        em.v.memset(prm[:, 505:506], 1e-6)
        em.v.memset(prm[:, 506:507], 1.0)
        self.vf()
        em.s.activation(prm[0:NH, 244:245], prm[0:NH, 241:242], AF.Exp)
        self.sf()
        tk = em.s.mul(prm[0:NH, 242:243], prm[0:NH, 244:245], -1.0).inc("ss")
        em.wait("v", tk)
        em.wait("t", tk)
        t1 = em.t.transpose(pb[0][0:1, 0:NH], prm[0:NH, 243:244], self.idf[0:NH, 0:NH]).inc("st")
        em.wait("s", t1)
        t2 = em.s.copy(prm[0:1, 352:352 + NH], pb[0][0:1, 0:NH]).inc("ss")
        em.wait("t", t2)
        t3 = em.t.matmul(pb[1][:, 0:NH], self.onef[0:1, :], prm[0:1, 352:352 + NH],
                         start=True, stop=True).inc("st")
        em.wait("v", t3)
        t4 = em.v.tensor_copy(prm[:, 96:96 + NH], pb[1][:, 0:NH]).inc("sv")
        em.wait("t", t4)
        em.barrier()

    def l0_ln_ffn(self):
        nc, em, pb, prm = self.nc, self.em, self.pb, self.prm
        ln_mean, ln_istd, ln_var = prm[:, 16:24], prm[:, 24:32], prm[:, 32:40]
        cA, cB, cC = prm[:, 40:48], prm[:, 48:56], prm[:, 56:64]
        inv_as, rmsf = prm[:, 64:72], prm[:, 72:80]
        with nc.sbuf_tensor("xq", [128, 16 * T], BF16) as xq, \
             nc.sbuf_tensor("xtile", [128, 2 * D], F32) as xtile, \
             nc.sbuf_tensor("scr", [128, D], F32) as scr:
            with nc.sbuf_tensor("xn32", [128, 16 * T], F32) as xn32:
                for c in range(8):
                    buf = xtile[:, (c % 2) * D:(c % 2) * D + D]
                    self.sguard("q", ("xt", c % 2))
                    tl = em.dma("q", buf, self.x_in[c * 128:(c + 1) * 128, :], ("dxt", c % 2))
                    em.wait("v", tl)
                    em.wait("s", tl)
                    self.sguard("s", "scrw")
                    self.sguard("v", "scrw")
                    em.v.reduce_sum(cA[:, c:c + 1], buf, axis=AX.X)
                    ts = em.s.activation(scr[:], buf, AF.Square, accum_out=cB[:, c:c + 1]).inc("ss")
                    self.vf()
                    em.wait("v", ts)
                    em.v.tensor_scalar_mul(ln_mean[:, c:c + 1], cA[:, c:c + 1], 1.0 / D)
                    self.vf()
                    em.v.tensor_mul(cC[:, c:c + 1], ln_mean[:, c:c + 1], ln_mean[:, c:c + 1])
                    self.vf()
                    tv = em.v.tensor_scalar(ln_var[:, c:c + 1], cB[:, c:c + 1], 1.0 / D,
                                            cC[:, c:c + 1], ALU.mult, ALU.subtract).inc("sv")
                    em.wait("s", tv)
                    em.s.activation(cC[:, c:c + 1], ln_var[:, c:c + 1], AF.Ln, bias=self.prm[:, 504:505])
                    self.sf()
                    ti = em.s.activation(ln_istd[:, c:c + 1], cC[:, c:c + 1], AF.Exp, scale=-0.5).inc("ss")
                    em.wait("v", ti)
                    tn = em.v.tensor_scalar(scr[:], buf, ln_mean[:, c:c + 1], ln_istd[:, c:c + 1],
                                            ALU.subtract, ALU.mult).inc("sv")
                    self.sset(("xt", c % 2), tn, ts)
                    em.wait("t", tn)
                    self.tr16(scr, [self.fm_out_ap(xn32, fq, c * 128, T) for fq in range(4)],
                              self.idf[:], (0, 1))
                    self.sset("scrw", self.ttok())
                self.tap("xn32", xn32[:], self.vtok())
                afold = xtile[:, 0:T]
                self.vf()
                em.wait("v", self.ttok())
                for f in range(16):
                    tgt = afold if f == 0 else scr[:, 0:T]
                    em.v.tensor_scalar(tgt, xn32[:, f * T:(f + 1) * T], prm[:, f:f + 1],
                                       None, ALU.mult)
                    self.vf()
                    em.v.scalar_tensor_tensor(tgt, tgt, -1.0, tgt, ALU.mult, ALU.max)
                    self.vf()
                    if f > 0:
                        em.v.tensor_max(afold, afold, scr[:, 0:T])
                        self.vf()
                em.wait("t", self.vtok())
                for c in range(8):
                    self.use_bank(2)
                    tt = em.t.transpose(pb[2][:, 0:128], afold[:, c * 128:(c + 1) * 128],
                                        self.idf[:]).inc("st")
                    em.wait("v", tt)
                    tv = em.v.reduce_max(cA[:, c:c + 1], pb[2][:, 0:128], axis=AX.X).inc("sv")
                    self.bank_done(2, tv)
                self.vf()
                em.v.tensor_scalar_add(cC, ln_var, 1e-5)
                self.vf()
                em.v.reciprocal(prm[:, 44:52] if False else cB, cC)
                self.vf()
                em.v.tensor_mul(cB, ln_var, cB)
                tv = self.vtok()
                self.vf()
                em.wait("s", tv)
                em.s.activation(cC, cB, AF.Ln, bias=self.prm[:, 505:506])
                self.sf()
                ts = em.s.activation(rmsf, cC, AF.Exp, scale=-0.5).inc("ss")
                em.wait("v", ts)
                em.v.tensor_tensor(cB, rmsf, cA, ALU.mult)
                self.vf()
                em.v.tensor_scalar_add(cB, cB, 1e-5)
                self.vf()
                em.v.tensor_scalar_mul(inv_as, cB, 1.0 / 127.0)
                em.v.reciprocal(cC, cB)
                self.vf()
                em.v.tensor_scalar_mul(cC, cC, 127.0)
                self.vf()
                em.v.tensor_tensor(cB, rmsf, cC, ALU.mult)
                self.vf()
                ccb = xtile[:, 0:T]
                ivb = xtile[:, T:2 * T]
                self.tap("prmq", prm[:, 0:96], self.vtok())
                self.bcast8(cB, ccb)
                self.bcast8(inv_as, ivb)
                self.tap("ccb", ccb, self.vtok())
                self.tap("ivb", ivb, self.vtok())
                for f in range(16):
                    self.sguard("v", "scrw")
                    em.v.tensor_scalar(scr[:, 0:T], xn32[:, f * T:(f + 1) * T], prm[:, f:f + 1],
                                       None, ALU.mult)
                    self.vf()
                    em.v.tensor_tensor(scr[:, 0:T], scr[:, 0:T], ccb, ALU.mult)
                    self.vf()
                    em.v.tensor_scalar_add(scr[:, 0:T], scr[:, 0:T], MAGIC)
                    self.vf()
                    em.v.tensor_scalar_sub(xq[:, f * T:(f + 1) * T], scr[:, 0:T], MAGIC)
                    self.vf()
                self.tap("xq", xq[:], self.vtok())
                tv = self.vtok()
                em.wait("g", tv)
                for f in range(16):
                    em.dma("g", apx(self.xn_dram, f * 128 * T, [[T, 128], [1, T]]),
                           xn32[:, f * T:(f + 1) * T], "xnout")
                self.xnout_tok = em.tok("xnout")
            em.wait("q", self.xnout_tok)
            em.wait("q", self.vtok())
            with nc.sbuf_tensor("wtile", [128, 2 * D], F32) as wtile, \
                 nc.sbuf_tensor("wq_n", [128, D], BF16) as wq_n, \
                 nc.sbuf_tensor("wqT", [128, 2 * D], BF16) as wqT:
                wsum, wabs = prm[:, 40:41], prm[:, 41:42]
                for i in range(16):
                    buf = wtile[:, (i % 2) * D:(i % 2) * D + D]
                    self.sguard("q", ("wt", i % 2))
                    tl = em.dma("q", buf, self.ffn_w[i * 128:(i + 1) * 128, :], ("dwt", i % 2))
                    em.wait("v", tl)
                    tr = em.v.reduce_sum(wabs, buf, axis=AX.X, apply_absolute_value=True).inc("sv")
                    self.sset(("wt", i % 2), tr)
                    self.vf()
                    if i == 0:
                        em.v.tensor_copy(wsum, wabs)
                    else:
                        em.v.tensor_add(wsum, wsum, wabs)
                    self.vf()
                tv = self.vtok()
                em.wait("t", tv)
                self.use_bank(2)
                tt = em.t.transpose(pb[2][0:1, 0:128], wsum, self.idf[:]).inc("st")
                em.wait("v", tt)
                em.v.reduce_sum(prm[0:1, 490:491], pb[2][0:1, 0:128], axis=AX.X)
                self.bank_done(2, self.vtok())
                self.vf()
                em.v.tensor_scalar(prm[0:1, 491:492], prm[0:1, 490:491], 1.0 / (D * D), 1e-5,
                                   ALU.mult, ALU.add)
                self.vf()
                em.v.reciprocal(prm[0:1, 490:491], prm[0:1, 491:492])
                tv = self.vtok()
                em.wait("t", tv)
                self.vf()
                self.use_bank(2)
                t1 = em.t.matmul(pb[2][:, 0:2], self.onef[0:1, :], prm[0:1, 490:492],
                                 start=True, stop=True).inc("st")
                em.wait("v", t1)
                tv = em.v.tensor_copy(prm[:, 488:490], pb[2][:, 0:2]).inc("sv")
                self.bank_done(2, tv)
                em.wait("v", tv)
                em.wait("t", tv)
                em.wait("s", tv)
                wscale_b = prm[:, 488:489]
                winv_b = prm[:, 489:490]
                for o in range(16):
                    buf = wtile[:, (o % 2) * D:(o % 2) * D + D]
                    self.sguard("q", ("wt", o % 2))
                    tl = em.dma("q", buf, self.ffn_w[o * 128:(o + 1) * 128, :], ("dwt", o % 2))
                    em.wait("v", tl)
                    em.v.tensor_scalar(buf, buf, wscale_b, MAGIC, ALU.mult, ALU.add)
                    self.vf()
                    em.v.tensor_scalar_sub(buf, buf, MAGIC)
                    self.vf()
                    self.sguard("v", "wqn")
                    tv = em.v.tensor_scalar(wq_n[:], buf, 1.0, -1.0, ALU.min, ALU.max).inc("sv")
                    em.wait("t", tv)
                    self.tr16(wq_n, [wqT[:, fq * 512:(fq + 1) * 512] for fq in range(4)],
                              self.idb[:], (6, 7))
                    self.sset("wqn", self.ttok())
                    em.wait("t", self.vtok())
                    for tt_i in range(2):
                        bi = 2 + tt_i
                        self.use_bank(bi)
                        last = None
                        for f in range(16):
                            lhs = wqT[:, f * 128:(f + 1) * 128]
                            em.t.ldweights(lhs)
                            last = em.t.matmul(pb[bi][:], lhs,
                                               xq[:, f * T + tt_i * 512:f * T + tt_i * 512 + 512],
                                               start=(f == 0), stop=(f == 15)).inc("st")
                        em.wait("v", last)
                        self.sguard("v", ("fsil", o % 2))
                        em.v.tensor_tensor(buf[:, 0:512], pb[bi][:],
                                           ivb[:, tt_i * 512:tt_i * 512 + 512], ALU.mult)
                        tvv = self.vtok()
                        self.vf()
                        em.wait("s", tvv)
                        tsl = em.s.activation(self.gdelta[:, o * T + tt_i * 512:o * T + tt_i * 512 + 512],
                                              buf[:, 0:512], AF.Silu, scale=winv_b).inc("ss")
                        self.bank_done(bi, tsl)
                        self.sset(("wt", o % 2), tsl)
                        self.sset(("fsil", o % 2), tsl)
                self.tap("g0", self.gdelta[:], self.stok(), eng="q")

    def bcast8(self, cols, dst):
        em, pb, prm = self.em, self.pb, self.prm
        em.wait("t", self.vtok())
        self.use_bank(3)
        tt = em.t.transpose(pb[3][0:8, 0:128], cols, self.idf[:]).inc("st")
        em.wait("s", tt)
        ts = em.s.copy(prm[0:8, 352:480], pb[3][0:8, 0:128]).inc("ss")
        self.bank_done(3, ts)
        em.wait("t", ts)
        for c in range(8):
            self.use_bank(3)
            t1 = em.t.matmul(pb[3][:, 0:128], self.oh8[:, c * 128:(c + 1) * 128],
                             prm[0:8, 352:480], start=True, stop=True).inc("st")
            em.wait("v", t1)
            tv = em.v.tensor_copy(dst[:, c * 128:(c + 1) * 128], pb[3][:, 0:128]).inc("sv")
            self.bank_done(3, tv)
        em.wait("t", self.vtok())

    def front(self):
        nc, em = self.nc, self.em
        em.v.memset(self.H[0:64, :], 0.0)
        self.vf()
        with nc.sbuf_tensor("fxn", [128, 16 * HALF], BF16) as fxn, \
             nc.sbuf_tensor("fxB", [128, 16 * HALF], BF16) as fxB, \
             nc.sbuf_tensor("ftail", [128, 144], BF16) as ftail, \
             nc.sbuf_tensor("fdt", [128, 3456], F32) as fdt, \
             nc.sbuf_tensor("ftok", [128, 2 * T], BF16) as ftok, \
             nc.sbuf_tensor("fxdtd", [128, D_IN], BF16) as fxdtd, \
             nc.sbuf_tensor("fwbuf", [128, 2 * 2048], BF16) as fwbuf, \
             nc.sbuf_tensor("fcst", [128, 2 * 516], BF16) as fcst, \
             nc.sbuf_tensor("fdiag", [128, 4 * 128], BF16) as fdiag, \
             nc.sbuf_tensor("fscr", [128, 2 * D], F32) as fscr:
            em.v.memset(ftail[:], 0.0)
            self.vf()
            for fh in range(2):
                h0 = fh * HALF
                em.wait("s", self.vtok())
                em.wait("s", self.ttok())
                em.wait("v", self.ttok())
                for c in range(4):
                    buf = fscr[:, (c % 2) * D:(c % 2) * D + D]
                    ob = ftok[:, 0:D]
                    self.sguard("q", ("fx", c % 2))
                    tl = em.dma("q", buf, self.x_front[h0 + c * 128:h0 + (c + 1) * 128, :], ("dfx", c % 2))
                    em.wait("v", tl)
                    em.wait("s", tl)
                    em.v.reduce_sum(self.prm[:, 40:41], buf, axis=AX.X)
                    ts = em.s.activation(ob, buf, AF.Square, accum_out=self.prm[:, 41:42]).inc("ss")
                    self.vf()
                    em.wait("v", ts)
                    em.v.tensor_scalar_mul(self.prm[:, 42:43], self.prm[:, 40:41], 1.0 / D)
                    self.vf()
                    em.v.tensor_mul(self.prm[:, 43:44], self.prm[:, 42:43], self.prm[:, 42:43])
                    self.vf()
                    tv = em.v.tensor_scalar(self.prm[:, 44:45], self.prm[:, 41:42], 1.0 / D,
                                            self.prm[:, 43:44], ALU.mult, ALU.subtract).inc("sv")
                    em.wait("s", tv)
                    em.s.activation(self.prm[:, 45:46], self.prm[:, 44:45], AF.Ln, bias=self.prm[:, 504:505])
                    self.sf()
                    ti = em.s.activation(self.prm[:, 46:47], self.prm[:, 45:46], AF.Exp,
                                         scale=-0.5).inc("ss")
                    em.wait("v", ti)
                    em.v.tensor_scalar(buf, buf, self.prm[:, 42:43], self.prm[:, 46:47],
                                       ALU.subtract, ALU.mult)
                    self.vf()
                    self.sguard("v", "fwb")
                    tc = em.v.tensor_copy(fwbuf[:, 0:D], buf).inc("sv")
                    self.sset(("fx", c % 2), tc)
                    em.wait("t", tc)
                    self.tr16(fwbuf[:, 0:D], [self.fm_out_ap(fxn, fq, c * 128, HALF) for fq in range(4)],
                              self.idb[:], (6, 7))
                    self.sset("fwb", self.ttok())
                if fh == 1:
                    self.vf()
                    em.v.tensor_copy(self.xpre[:].rearrange("p (f k) -> p f k", f=16),
                                     apx(fxn, 509, [[16 * HALF, 128], [HALF, 16], [1, 3]]))
                    self.vf()
                if self.front_limit == "ln":
                    continue
                self.winx_pass(fxn, fxB, None, fdt, fwbuf, fcst, fdiag, ftail,
                               m_list=list(range(32, 48)), dt_too=True, front=True, h0=h0)
                if self.front_limit == "winx":
                    continue
                for c in range(4):
                    self.sum_chunk_front(fxB, fdt, ftok, fxdtd, c)
                    if self.front_limit in ("dt", "tok", "S"):
                        break
        self.tap("H_init", self.H[0:64, 0:NH * DS], self.vtok())

    def winx_pass(self, xnh, xB_fm, C_stage, dtb, wbuf, cstage, diag, tailbuf,
                  m_list, dt_too, front, h0, pre_src=None):
        nc, em, pb, prm = self.nc, self.em, self.pb, self.prm
        bias_off = 240 if front else 192
        em.wait("t", self.vtok())
        cdma_toks = [None, None]
        for mi, m in enumerate(m_list):
            wb = wbuf[:, (mi % 2) * 2048:(mi % 2) * 2048 + 2048]
            self.sguard("g", (id(wbuf), mi % 2))
            self.sguard("g", "fwb")
            wt = em.dma("g", wb.rearrange("p (f i) -> p f i", f=16),
                        apx(self.w_in, m * 128, [[NPROJ, 128], [128 * NPROJ, 16], [1, 128]]),
                        ("dwin", id(wbuf), mi % 2))
            em.wait("t", wt)
            bi = mi % 2
            self.use_bank(bi)
            if pre_src is not None:
                self.use_bank(4)
            last = None
            for f in range(16):
                lhs = wb[:, f * 128:(f + 1) * 128]
                em.t.ldweights(lhs)
                last = em.t.matmul(pb[bi][:], lhs, xnh[:, f * HALF:(f + 1) * HALF],
                                   start=(f == 0), stop=(f == 15)).inc("st")
                if pre_src is not None:
                    last = em.t.matmul(pb[4][:, 0:3], lhs, pre_src[:, f * 3:(f + 1) * 3],
                                       start=(f == 0), stop=(f == 15)).inc("st")
            self.sset((id(wbuf), mi % 2), last)
            ch0 = m * 128 - D_IN
            cmi = ch0 // 128
            cs = cstage[:, (mi % 2) * 516:(mi % 2) * 516 + 516]
            em.wait("v", last)
            self.sguard("v", (id(cstage), mi % 2))
            if pre_src is not None:
                tpre = em.v.tensor_copy(cs[:, 0:3], pb[4][:, 0:3]).inc("sv")
                self.bank_done(4, tpre)
            else:
                em.v.tensor_copy(cs[:, 0:3], tailbuf[:, cmi * 3:cmi * 3 + 3])
            tv = em.v.tensor_copy(cs[:, 3:515], pb[bi][:]).inc("sv")
            self.bank_done(bi, tv)
            self.vf()
            em.v.tensor_copy(tailbuf[:, cmi * 3:cmi * 3 + 3], cs[:, 512:515])
            self.vf()
            for k in range(4):
                em.v.tensor_scalar(diag[:, k * 128:(k + 1) * 128], self.idb[:],
                                   self.convp[:, cmi * 4 + k:cmi * 4 + k + 1], None, ALU.mult)
            tdg = self.vtok()
            self.vf()
            em.wait("t", tdg)
            bi2 = 2 + mi % 2
            self.use_bank(bi2)
            last = None
            for k in range(4):
                em.t.ldweights(diag[:, k * 128:(k + 1) * 128])
                last = em.t.matmul(pb[bi2][:], diag[:, k * 128:(k + 1) * 128],
                                   cs[:, k:k + 512], start=(k == 0), stop=(k == 3)).inc("st")
            self.sset((id(cstage), mi % 2), last)
            em.wait("s", last)
            bias_col = self.convp[:, bias_off + cmi:bias_off + cmi + 1]
            if ch0 < 2048:
                blk = ch0 // 128
                tv = em.s.activation(xB_fm[:, blk * HALF:(blk + 1) * HALF], pb[bi2][:],
                                     AF.Silu, bias=bias_col).inc("ss")
                self.bank_done(bi2, tv)
            else:
                b = (ch0 - 2048) // 128
                cst = C_stage[:, (mi % 2) * HALF:(mi % 2) * HALF + HALF]
                em.wait("s", cdma_toks[mi % 2])
                tv = em.s.activation(cst, pb[bi2][:], AF.Silu, bias=bias_col).inc("ss")
                self.bank_done(bi2, tv)
                em.wait("g", tv)
                em.dma("g", apx(self.c64_dram, (2 * b) * HALF, [[NH * HALF, 64], [1, 512]]),
                       cst[0:64, :], ("c64o", h0, mi % 2))
                cdma_toks[mi % 2] = em.dma("g", apx(self.c64_dram, (2 * b + 1) * HALF,
                                                    [[NH * HALF, 64], [1, 512]]),
                                           cst[64:128, :], ("c64o", h0, mi % 2))
                if b < 8:
                    pass
        if not front:
            tvv = self.stok()
            em.wait("g", tvv)
            em.dma("g", apx(self.b64_dram, 0, [[16 * HALF, 64], [2 * HALF, 8], [1, 512]]),
                   apx(xB_fm, 8 * HALF, [[16 * HALF, 64], [HALF, 8], [1, 512]]), ("b64o", h0))
            em.dma("g", apx(self.b64_dram, HALF, [[16 * HALF, 64], [2 * HALF, 8], [1, 512]]),
                   apx(xB_fm, 64 * (16 * HALF) + 8 * HALF, [[16 * HALF, 64], [HALF, 8], [1, 512]]),
                   ("b64o", h0))
        if dt_too:
            wb = wbuf[:, 0:16 * 64]
            self.sguard("g", (id(wbuf), 0))
            wt = em.dma("g", wb.rearrange("p (f i) -> p f i", f=16),
                        apx(self.w_in, 10240, [[NPROJ, 128], [128 * NPROJ, 16], [1, 64]]),
                        ("dwin", id(wbuf), 0))
            em.wait("t", wt)
            self.use_bank(4)
            last = None
            for f in range(16):
                lhs = wb[:, f * 64:(f + 1) * 64]
                em.t.ldweights(lhs)
                last = em.t.matmul(pb[4][0:64, 0:512], lhs, xnh[:, f * HALF:(f + 1) * HALF],
                                   start=(f == 0), stop=(f == 15)).inc("st")
            em.wait("v", last)
            ts = em.v.tensor_scalar(dtb[0:64, 0:512], pb[4][0:64, 0:512], prm[0:64, 240:241],
                                    None, ALU.add).inc("sv")
            self.bank_done(4, ts)
            em.wait("s", ts)
            em.s.activation(dtb[0:64, 512:1024], dtb[0:64, 0:512], AF.Abs)
            self.sf()
            em.s.activation(dtb[0:64, 1024:1536], dtb[0:64, 512:1024], AF.Exp, scale=-1.0)
            self.sf()
            em.s.activation(dtb[0:64, 1536:2048], dtb[0:64, 1024:1536], AF.Ln, bias=self.prm[0:64, 506:507])
            self.sf()
            ts = em.s.activation(dtb[0:64, 512:1024], dtb[0:64, 0:512], AF.Relu).inc("ss")
            em.wait("v", ts)
            tv = em.v.tensor_add(dtb[0:64, 2048:2560], dtb[0:64, 512:1024],
                                 dtb[0:64, 1536:2048]).inc("sv")
            self.vf()
            em.wait("s", tv)
            em.wait("t", tv)

    def dt_chunk(self, dtb, c, need_ec, need_toks):
        nc, em, pb, prm = self.nc, self.em, self.pb, self.prm
        lim = getattr(self, "front_limit", "all")
        dtp = dtb[0:64, 2048:2560]
        dA = dtb[0:64, 2560:2688]
        cum = dtb[0:64, 2688:2816]
        dte = dtb[0:64, 2816:2944]
        ec = dtb[0:64, 2944:3072]
        dtd = dtb[0:64, 3072:3200]
        em.v.tensor_scalar(dA, dtp[:, c * 128:(c + 1) * 128], prm[0:64, 242:243], None, ALU.mult)
        self.vf()
        em.v.tensor_tensor_scan(cum, self.onef[0:64, 0:128], dA, 0.0, ALU.mult, ALU.add)
        self.vf()
        em.v.tensor_copy(prm[0:64, 245:246], cum[:, 127:128])
        tv = self.vtok()
        self.vf()
        if lim == "dtA":
            return
        em.wait("s", tv)
        em.s.activation(dte, cum, AF.Exp, bias=prm[0:64, 245:246], scale=-1.0)
        em.s.activation(prm[0:64, 246:247], prm[0:64, 245:246], AF.Exp)
        if need_ec:
            em.s.activation(ec, cum, AF.Exp)
        ts = self.stok()
        self.sf()
        em.wait("v", ts)
        em.v.tensor_mul(dtd, dtp[:, c * 128:(c + 1) * 128], dte)
        tv = self.vtok()
        self.vf()
        if lim == "dtB":
            return
        em.wait("t", tv)
        self.use_bank(5)
        t3 = em.t.transpose(pb[5][:, 0:64], dtd, self.idf[0:64, 0:64]).inc("st")
        if need_toks:
            em.t.transpose(pb[5][:, 64:128], dtp[:, c * 128:(c + 1) * 128], self.idf[0:64, 0:64])
            t3 = em.t.transpose(pb[5][:, 128:192], cum, self.idf[0:64, 0:64]).inc("st")
        em.wait("v", t3)
        em.v.tensor_copy(dtb[:, 3200:3264], pb[5][:, 0:64])
        if need_toks:
            em.v.tensor_copy(dtb[:, 3264:3328], pb[5][:, 64:128])
            em.v.tensor_scalar_mul(dtb[:, 3328:3392], pb[5][:, 128:192], -1.0)
        lastent = None
        for entry in reversed(em.streams["v"]):
            if entry[0] not in ("wait_ge", "nop"):
                lastent = entry
                break
        tv = _Pending(em, lastent).inc("sv")
        em.wait("v", tv)
        self.bank_done(5, tv)
        self.use_bank(5)
        t2 = em.t.transpose(pb[5][0:1, 0:64], prm[0:64, 246:247], self.idf[0:64, 0:64]).inc("st")
        em.wait("s", t2)
        t3 = em.s.copy(prm[0:1, 352:416], pb[5][0:1, 0:64]).inc("ss")
        self.bank_done(5, t3)
        em.wait("t", t3)
        self.use_bank(5)
        t4 = em.t.matmul(pb[5][:, 0:64], self.onef[0:1, :], prm[0:1, 352:416],
                         start=True, stop=True).inc("st")
        em.wait("v", t4)
        tv = em.v.tensor_copy(prm[:, 160:224], pb[5][:, 0:64]).inc("sv")
        self.bank_done(5, tv)
        self.vf()

    def xB_transpose_chunk(self, xB_fm, x_dst, B_dst, c, banks):
        em, pb = self.em, self.pb
        for b8 in range(8):
            bi = banks[b8 % 2]
            self.use_bank(bi)
            em.t.transpose(pb[bi][:, 0:128],
                           xB_fm[:, b8 * HALF + c * 128:b8 * HALF + c * 128 + 128], self.idb[:])
            tb = em.t.transpose(pb[bi][:, 128:256],
                                xB_fm[:, (8 + b8) * HALF + c * 128:(8 + b8) * HALF + c * 128 + 128],
                                self.idb[:]).inc("st")
            em.wait("v", tb)
            em.v.tensor_copy(x_dst[:, b8 * 128:(b8 + 1) * 128], pb[bi][:, 0:128])
            tv = em.v.tensor_copy(B_dst[:, b8 * 128:(b8 + 1) * 128], pb[bi][:, 128:256]).inc("sv")
            self.bank_done(bi, tv)
        self.vf()

    def sum_chunk_front(self, fxB, fdt, ftok, fxdtd, c):
        nc, em, pb, prm = self.nc, self.em, self.pb, self.prm
        self.dt_chunk(fdt, c, need_ec=False, need_toks=False)
        if self.front_limit in ("dt", "dtA", "dtB", "dtC"):
            return
        x_tok = ftok[:, 0:T]
        B_tok = ftok[:, T:2 * T]
        em.wait("t", self.vtok())
        self.xB_transpose_chunk(fxB, x_tok, B_tok, c, (6, 7))
        x_ap = apx(x_tok, 0, [[2 * T, 128], [64, 16], [0, 4], [1, 64]])
        d_ap = apx(fdt, 3200, [[3456, 128], [4, 16], [1, 4], [0, 64]])
        tv = em.v.tensor_tensor(fxdtd[:].rearrange("p (g r q) -> p g r q", g=16, r=4),
                                x_ap, d_ap, ALU.mult).inc("sv")
        em.wait("t", tv)
        if self.front_limit == "tok":
            return
        for grp in range(8):
            bi = 2 + grp % 2
            self.use_bank(bi)
            last = None
            for hh in range(8):
                h = grp * 8 + hh
                xbh = h // 4
                lhs = B_tok[:, xbh * 64:(xbh + 1) * 64]
                em.t.ldweights(lhs)
                last = em.t.matmul(pb[bi][0:64, hh * 64:(hh + 1) * 64], lhs,
                                   fxdtd[:, h * 64:(h + 1) * 64], start=True, stop=True).inc("st")
            em.wait("v", last)
            self.h_update(grp, pb[bi])
            self.bank_done(bi, self.vtok())

    def h_update(self, grp, sbank):
        em = self.em
        Hsl = self.H[0:64, grp * 512:(grp + 1) * 512]
        Hscr = self.H[0:64, NH * DS:NH * DS + 512]
        dec_ap = apx(self.prm, 160 + grp * 8, [[512, 64], [1, 8], [0, 64]])
        em.v.tensor_tensor(Hscr.rearrange("p (a b) -> p a b", a=8),
                           Hsl.rearrange("p (a b) -> p a b", a=8), dec_ap, ALU.mult)
        self.vf()
        em.v.tensor_add(Hsl, Hscr, sbank[0:64, 0:512])
        self.vf()

    def l0_attn(self):
        nc, em = self.nc, self.em
        with nc.sbuf_tensor("nwb", [128, D_IN], BF16) as nwb, \
             nc.sbuf_tensor("a_xB", [128, 16 * HALF], BF16) as a_xB, \
             nc.sbuf_tensor("a_tail", [128, 144], BF16) as a_tail, \
             nc.sbuf_tensor("a_dt", [128, 3456], F32) as a_dt, \
             nc.sbuf_tensor("a_tok", [128, 2048], BF16) as a_tok, \
             nc.sbuf_tensor("a_wbuf", [128, 2 * 2048], BF16) as a_wbuf, \
             nc.sbuf_tensor("a_cst", [128, 2 * 516], BF16) as a_cst, \
             nc.sbuf_tensor("a_Cst", [128, 2 * HALF], BF16) as a_Cst, \
             nc.sbuf_tensor("a_diag", [128, 4 * 128], BF16) as a_diag, \
             nc.sbuf_tensor("a_scr", [128, 512], F32) as a_scr:
            self.nwb = nwb
            self.nwb_tok = em.dma("g", nwb[:], self.nw_bcast_d, "nwb")
            em.v.memset(a_tail[:], 0.0)
            self.vf()
            for hf in range(2):
                self.l0_half(hf, nwb, a_xB, a_tail, a_dt, a_tok, a_wbuf, a_cst,
                             a_Cst, a_diag, a_scr)
                em.barrier()
        self.tap("delta0", self.gdelta[:], self.vtok())

    def l0_half(self, hf, nwb, a_xB, a_tail, a_dt, a_tok, a_wbuf, a_cst, a_Cst,
                a_diag, a_scr):
        nc, em, pb, prm = self.nc, self.em, self.pb, self.prm
        h0 = hf * HALF
        with self.sb("xnh", [128, 16 * HALF], BF16) as xnh, \
             self.sb("zst", [128, HALF], BF16) as zst, \
             self.sb("ztr", [128, 2 * 512], BF16) as ztr:
            em.wait("q", self.xnout_tok)
            tl = em.dma("q", xnh[:].rearrange("p (f t) -> p f t", f=16),
                        apx(self.xn_dram, h0, [[T, 128], [128 * T, 16], [1, HALF]]), "xnh")
            em.wait("t", tl)
            zdma = [None, None]
            for m in range(32):
                wb = a_wbuf[:, (m % 2) * 2048:(m % 2) * 2048 + 2048]
                self.sguard("g", (id(a_wbuf), m % 2))
                wt = em.dma("g", wb.rearrange("p (f i) -> p f i", f=16),
                            apx(self.w_in, m * 128, [[NPROJ, 128], [128 * NPROJ, 16], [1, 128]]),
                            ("dwin", id(a_wbuf), m % 2))
                em.wait("t", wt)
                bi = m % 2
                self.use_bank(bi)
                last = None
                for f in range(16):
                    lhs = wb[:, f * 128:(f + 1) * 128]
                    em.t.ldweights(lhs)
                    last = em.t.matmul(pb[bi][:], lhs, xnh[:, f * HALF:(f + 1) * HALF],
                                       start=(f == 0), stop=(f == 15)).inc("st")
                self.sset((id(a_wbuf), m % 2), last)
                em.wait("s", last)
                self.sguard("s", "zst")
                tv = em.s.activation(zst[:], pb[bi][:], AF.Silu).inc("ss")
                self.bank_done(bi, tv)
                em.wait("t", tv)
                bi2 = 6 + m % 2
                self.use_bank(bi2)
                last = None
                for q in range(4):
                    last = em.t.transpose(pb[bi2][:, q * 128:(q + 1) * 128],
                                          zst[:, q * 128:(q + 1) * 128], self.idb[:]).inc("st")
                self.sset("zst", last)
                em.wait("v", last)
                em.wait("v", zdma[m % 2])
                tv = em.v.tensor_copy(ztr[:, (m % 2) * 512:(m % 2) * 512 + 512], pb[bi2][:, 0:512]).inc("sv")
                self.bank_done(bi2, tv)
                em.wait("g", tv)
                zdma[m % 2] = em.dma("g", apx(self.sz_dram, h0 * D_IN + m * 128,
                                              [[D_IN, 128], [128 * D_IN, 4], [1, 128]]),
                                     ztr[:, (m % 2) * 512:(m % 2) * 512 + 512].rearrange(
                                         "p (q i) -> p q i", q=4), ("szo", h0, m % 2))
            self.winx_pass(xnh, a_xB, a_Cst, a_dt, a_wbuf, a_cst, a_diag, a_tail,
                           m_list=list(range(32, 80)), dt_too=True, front=False, h0=h0,
                           pre_src=(self.xpre if hf == 0 else None))
        self.a_xB_ref = a_xB
        em.wait("t", self.stok())
        stageA_sh = (em.tok(("szo", h0, 0)), em.tok(("szo", h0, 1)),
                     em.tok(("c64o", h0, 0)), em.tok(("c64o", h0, 1)),
                     em.tok(("b64o", h0)))
        for _e in ("q", "g", "v", "s", "t"):
            for _tk in stageA_sh:
                em.wait(_e, _tk)
        with self.sb("szc", [128, D_IN], BF16) as szc, \
             self.sb("C64", [64, 2048], BF16) as C64, \
             self.sb("B64", [64, 16 * 128], BF16) as B64, \
             self.sb("Hbf", [64, NH * DS], BF16) as Hbf, \
             self.sb("xdt", [128, 512], BF16) as xdt, \
             self.sb("xdtd", [128, 512], BF16) as xdtd, \
             self.sb("Rq", [128, 2048], F32) as Rq, \
             self.sb("mexp", [128, 2048], BF16) as mexp, \
             self.sb("eCb", [64, 2048], BF16) as eCb, \
             self.sb("Ceq", [64, 2048], BF16) as Ceq, \
             self.sb("GM", [128, 1024], BF16) as GM, \
             self.sb("yz", [128, 1024], F32) as yz, \
             self.sb("sqs", [128, 1024], F32) as sqs, \
             self.sb("ntok", [128, 1024], BF16) as ntok, \
             self.sb("nfm", [128, 32 * 256], BF16) as nfm:
            for c in range(4):
                self.l0_chunk(hf, c, stageA_sh, a_dt, a_tok, szc, C64, B64, Hbf,
                              xdt, xdtd, Rq, mexp, eCb, Ceq, GM, yz, sqs, ntok, nfm)
                if c % 2 == 1:
                    self.l0_wout(hf, c // 2, nfm, a_scr)

    def l0_chunk(self, hf, c, stageA_sh, a_dt, a_tok, szc, C64, B64, Hbf,
                 xdt, xdtd, Rq, mexp, eCb, Ceq, GM, yz, sqs, ntok, nfm):
        nc, em, pb, prm = self.nc, self.em, self.pb, self.prm
        h0 = hf * HALF
        em.wait("g", em.tok("rq") if "rq" in em.sems else None)
        em.wait("g", em.tok("ecb") if "ecb" in em.sems else None)
        self.dt_chunk(a_dt, c, need_ec=True, need_toks=True)
        tv = self.vtok()
        ts = self.stok()
        em.wait("g", ts)
        em.dma("g", apx(self.cum_dram, 0, [[128, 64], [1, 128]]), a_dt[0:64, 2688:2816], "cumo")
        em.dma("g", apx(self.ec_dram, 0, [[128, 64], [1, 128]]), a_dt[0:64, 2944:3072], "eco")
        cumec_tok = (em.tok("cumo"), em.tok("eco"))
        for tk in stageA_sh:
            em.wait("q", tk)
        em.wait("q", self.vtok())
        em.wait("q", self.ttok())
        tsz = em.dma("q", szc[:], apx(self.sz_dram, (h0 + c * 128) * D_IN,
                                      [[D_IN, 128], [1, D_IN]]), "szc")
        tb64 = em.dma("q", B64[:], apx(self.b64_dram, c * 128,
                                       [[16 * HALF, 64], [HALF, 16], [1, 128]]), "b64l")
        em.wait("v", self.ttok())
        em.v.tensor_copy(Hbf[:], self.H[0:64, 0:NH * DS])
        self.vf()
        em.wait("t", self.vtok())
        self.xB_transpose_chunk(self.a_xB_ref, a_tok[:, 0:T], a_tok[:, T:2 * T], c, (6, 7))
        em.wait("t", tsz)
        em.wait("t", tb64)
        for qq in range(4):
            em.wait("q", self.ttok())
            em.wait("q", self.vtok())
            tc64 = em.dma("q", C64[:], apx(self.c64_dram, qq * 16 * HALF + c * 128,
                                           [[NH * HALF, 64], [HALF, 16], [1, 128]]), "c64l")
            em.wait("t", tc64)
            em.wait("v", tc64)
            em.wait("q", self.stok())
            em.wait("q", cumec_tok[0])
            em.wait("q", cumec_tok[1])
            em.wait("g", cumec_tok[1])
            t1 = em.dma("q", Rq[:].rearrange("p (h i) -> p h i", h=16),
                        apx(self.cum_dram, qq * 16 * 128, [[0, 128], [128, 16], [1, 128]]), "rq")
            em.wait("g", t1)
            t2 = em.dma("g", Rq[:].rearrange("p (h i) -> p h i", h=16),
                        apx(self.triu_neg_d, 0, [[128, 128], [0, 16], [1, 128]]), "rqa",
                        accum_op=ALU.add)
            em.wait("v", t2)
            em.wait("v", self.stok())
            ncol_ap = apx(a_dt, 3328 + qq * 16, [[3456, 128], [1, 16], [0, 128]])
            tv = em.v.tensor_tensor(Rq[:].rearrange("p (h i) -> p h i", h=16),
                                    Rq[:].rearrange("p (h i) -> p h i", h=16),
                                    ncol_ap, ALU.add).inc("sv")
            em.wait("s", tv)
            em.wait("s", self.vtok())
            em.s.activation(mexp[:, 0:1024], Rq[:, 0:1024], AF.Exp)
            ts = em.s.activation(mexp[:, 1024:2048], Rq[:, 1024:2048], AF.Exp).inc("ss")
            t3 = em.dma("g", eCb[:].rearrange("p (h i) -> p h i", h=16),
                        apx(self.ec_dram, qq * 16 * 128, [[0, 64], [128, 16], [1, 128]]), "ecb")
            em.wait("v", t3)
            em.wait("v", self.ttok())
            tce = em.v.tensor_mul(Ceq[:], C64[:], eCb[:]).inc("sv")
            em.wait("t", tce)
            em.wait("v", ts)
            for g2 in range(2):
                grp = qq * 2 + g2
                ybank = 0 + grp % 2
                sbank = 4 + grp % 2
                em.wait("v", self.ttok())
                x_ap = apx(a_tok, 2 * grp * 64, [[2048, 128], [64, 2], [0, 4], [1, 64]])
                dtp_ap = apx(a_dt, 3264 + grp * 8, [[3456, 128], [4, 2], [1, 4], [0, 64]])
                dtd_ap = apx(a_dt, 3200 + grp * 8, [[3456, 128], [4, 2], [1, 4], [0, 64]])
                em.v.tensor_tensor(xdt[:].rearrange("p (g r q) -> p g r q", g=2, r=4),
                                   x_ap, dtp_ap, ALU.mult)
                tv = em.v.tensor_tensor(xdtd[:].rearrange("p (g r q) -> p g r q", g=2, r=4),
                                        x_ap, dtd_ap, ALU.mult).inc("sv")
                em.wait("t", tv)
                self.use_bank(ybank)
                self.use_bank(sbank)
                ylast = None
                for half4 in range(2):
                    gtbank = 2 + half4
                    self.use_bank(gtbank)
                    gtlast = None
                    for k in range(4):
                        h = grp * 8 + half4 * 4 + k
                        xbh = h // 4
                        lhs = B64[:, xbh * 128:(xbh + 1) * 128]
                        em.t.ldweights(lhs)
                        lh = h - qq * 16
                        gtlast = em.t.matmul(pb[gtbank][:, k * 128:(k + 1) * 128], lhs,
                                             C64[:, lh * 128:(lh + 1) * 128],
                                             start=True, stop=True).inc("st")
                    em.wait("v", gtlast)
                    self.sguard("v", ("GM", half4))
                    msl = mexp[:, (g2 * 8 + half4 * 4) * 64 * 2:(g2 * 8 + half4 * 4 + 4) * 64 * 2]
                    tv = em.v.tensor_mul(GM[:, half4 * 512:(half4 + 1) * 512],
                                         pb[gtbank][:], msl).inc("sv")
                    self.bank_done(gtbank, tv)
                    em.wait("t", tv)
                    for k in range(4):
                        h = grp * 8 + half4 * 4 + k
                        hq = g2 * 8 + half4 * 4 + k
                        lhs1 = GM[:, half4 * 512 + k * 128:half4 * 512 + (k + 1) * 128]
                        em.t.ldweights(lhs1)
                        em.t.matmul(pb[ybank][:, (half4 * 4 + k) * 64:(half4 * 4 + k + 1) * 64],
                                    lhs1, xdt[:, (half4 * 4 + k) * 64:(half4 * 4 + k + 1) * 64],
                                    start=True, stop=False)
                        lhs2 = Ceq[:, hq * 128:(hq + 1) * 128]
                        em.t.ldweights(lhs2)
                        ylast = em.t.matmul(pb[ybank][:, (half4 * 4 + k) * 64:(half4 * 4 + k + 1) * 64],
                                            lhs2, Hbf[:, h * 64:(h + 1) * 64],
                                            start=False, stop=True).inc("st")
                    self.sset(("GM", half4), ylast)
                slast = None
                for hh in range(8):
                    h = grp * 8 + hh
                    xbh = h // 4
                    lhs = a_tok[:, T + xbh * 64:T + (xbh + 1) * 64]
                    em.t.ldweights(lhs)
                    slast = em.t.matmul(pb[sbank][0:64, hh * 64:(hh + 1) * 64], lhs,
                                        xdtd[:, hh * 64:(hh + 1) * 64], start=True, stop=True).inc("st")
                em.wait("v", ylast)
                xg_ap = apx(a_tok, 2 * grp * 64, [[2048, 128], [64, 2], [0, 4], [1, 64]])
                D_ap = apx(prm, 96 + grp * 8, [[512, 128], [4, 2], [1, 4], [0, 64]])
                xsc = sqs[:, (grp % 2) * 512:(grp % 2) * 512 + 512]
                em.v.tensor_tensor(xsc.rearrange("p (g r q) -> p g r q", g=2, r=4),
                                   xg_ap, D_ap, ALU.mult)
                self.vf()
                em.v.tensor_add(xsc, xsc, pb[ybank][:])
                tyv = self.vtok()
                self.bank_done(ybank, tyv)
                self.vf()
                yhalf = yz[:, (grp % 2) * 512:(grp % 2) * 512 + 512]
                em.v.tensor_mul(yhalf, xsc, szc[:, grp * 512:(grp + 1) * 512])
                self.vf()
                em.wait("v", slast)
                self.h_update(grp, pb[sbank])
                self.bank_done(sbank, self.vtok())
                if grp % 2 == 1:
                    self.l0_rms(hf, c, grp // 2, yz, sqs, ntok, nfm)

    def l0_rms(self, hf, c, qq, yz, sqs, ntok, nfm):
        nc, em, pb, prm = self.nc, self.em, self.pb, self.prm
        em.wait("s", self.vtok())
        em.wait("v", self.nwb_tok)
        ts = em.s.activation(sqs[:], yz[:], AF.Square).inc("ss")
        em.wait("v", ts)
        rms = prm[:, 248:264]
        tv = em.v.tensor_reduce(rms.rearrange("p (h o) -> p h o", h=16),
                                sqs[:].rearrange("p (h q) -> p h q", h=16),
                                AX.X, ALU.add).inc("sv")
        em.wait("s", tv)
        em.s.activation(prm[:, 280:296], rms, AF.Ln, bias=self.prm[:, 504:505], scale=1.0 / DS)
        self.sf()
        ts = em.s.activation(rms, prm[:, 280:296], AF.Exp, scale=-0.5).inc("ss")
        em.wait("v", ts)
        rms_ap = apx(prm, 248, [[512, 128], [1, 16], [0, 64]])
        em.v.tensor_tensor(sqs[:].rearrange("p (h q) -> p h q", h=16),
                           yz[:].rearrange("p (h q) -> p h q", h=16), rms_ap, ALU.mult)
        self.vf()
        self.sguard("v", "ntok")
        tv = em.v.tensor_mul(ntok[:], sqs[:], self.nwb[:, qq * 1024:(qq + 1) * 1024]).inc("sv")
        em.wait("t", tv)
        for fq in range(2):
            bi = 6 + fq % 2
            self.use_bank(bi)
            last = None
            for k in range(4):
                fb = fq * 4 + k
                last = em.t.transpose(pb[bi][:, k * 128:(k + 1) * 128],
                                      ntok[:, fb * 128:fb * 128 + 128], self.idb[:]).inc("st")
            em.wait("v", last)
            out_ap = apx(nfm, (qq * 8 + fq * 4) * 256 + (c % 2) * 128,
                         [[32 * 256, 128], [256, 4], [1, 128]])
            tv = em.v.tensor_copy(out_ap, pb[bi][:, 0:512].rearrange("p (a b) -> p a b", a=4)).inc("sv")
            self.bank_done(bi, tv)
            self.sset("ntok", self.ttok())

    def l0_wout(self, hf, pair, nfm, a_scr):
        nc, em, pb = self.nc, self.em, self.pb
        em.wait("t", self.vtok())
        tok0 = hf * HALF + pair * 256
        with self.sb("wob", [128, 2 * 2048], BF16) as wob:
            for o in range(16):
                bi = 4 + o % 2
                self.use_bank(bi)
                last = None
                for istrip in range(2):
                    wb = wob[:, istrip * 2048:(istrip + 1) * 2048]
                    self.sguard("g", (id(wob), istrip))
                    wt = em.dma("g", wb.rearrange("p (f i) -> p f i", f=16),
                                apx(self.w_out, (istrip * 16 * 128) * D + o * 128,
                                    [[D, 128], [128 * D, 16], [1, 128]]), ("dwob", istrip))
                    em.wait("t", wt)
                    for f in range(16):
                        ii = istrip * 16 + f
                        lhs = wb[:, f * 128:(f + 1) * 128]
                        em.t.ldweights(lhs)
                        last = em.t.matmul(pb[bi][:, 0:256], lhs, nfm[:, ii * 256:(ii + 1) * 256],
                                           start=(ii == 0), stop=(ii == 31)).inc("st")
                    self.sset((id(wob), istrip), last)
                em.wait("v", last)
                gofs = o * T + tok0
                gsl = self.gdelta[:, gofs:gofs + 256]
                em.v.tensor_mul(a_scr[:, 0:256], pb[bi][:, 0:256], gsl)
                tvv = self.vtok()
                self.bank_done(bi, tvv)
                self.vf()
                em.v.tensor_copy(gsl, a_scr[:, 0:256])
                self.vf()

    def l1(self):
        nc, em, pb, prm = self.nc, self.em, self.pb, self.prm
        for f in range(16):
            em.dma("q", prm[:, 224 + f:225 + f], self.l1_D[f * 128:(f + 1) * 128, None], "l1d")
        self.l1d_tok = em.tok("l1d")
        with nc.sbuf_tensor("x1n", [128, 16 * T], BF16) as x1n:
            with nc.sbuf_tensor("x1f", [128, 16 * T], F32) as x1f, \
                 nc.sbuf_tensor("l1scr", [128, 2 * D], F32) as l1scr:
                for c in range(8):
                    buf = l1scr[:, (c % 2) * D:(c % 2) * D + D]
                    self.sguard("q", ("l1x", c % 2))
                    tl = em.dma("q", buf, self.x_in[c * 128:(c + 1) * 128, :], ("dl1x", c % 2))
                    em.wait("t", tl)
                    for fq in range(4):
                        bi = fq % 2
                        self.use_bank(bi)
                        last = None
                        for k in range(4):
                            f = fq * 4 + k
                            last = em.t.transpose(pb[bi][:, k * 128:(k + 1) * 128],
                                                  buf[:, f * 128:f * 128 + 128], self.idf[:]).inc("st")
                        em.wait("v", last)
                        out_ap = self.fm_out_ap(x1f, fq, c * 128, T)
                        g_ap = self.fm_out_ap(self.gdelta, fq, c * 128, T)
                        tv = em.v.tensor_add(out_ap, pb[bi][:].rearrange("p (a b) -> p a b", a=4),
                                             g_ap).inc("sv")
                        self.bank_done(bi, tv)
                        if fq == 3:
                            self.sset(("l1x", c % 2), tv)
                em.wait("t", self.vtok())
                em.wait("s", self.vtok())
                for tt_i in range(2):
                    self.use_bank(2)
                    last = None
                    for f in range(16):
                        last = em.t.matmul(pb[2][0:1, 0:512], self.onef[:, 0:1],
                                           x1f[:, f * T + tt_i * 512:f * T + tt_i * 512 + 512],
                                           start=(f == 0), stop=(f == 15)).inc("st")
                    em.wait("s", last)
                    ts = em.s.mul(l1scr[0:1, tt_i * 512:tt_i * 512 + 512],
                                  pb[2][0:1, 0:512], 1.0 / D).inc("ss")
                    self.bank_done(2, ts)
                for tt_i in range(2):
                    self.use_bank(3)
                    last = None
                    for f in range(16):
                        tq = em.s.activation(l1scr[:, D:D + 512],
                                             x1f[:, f * T + tt_i * 512:f * T + tt_i * 512 + 512],
                                             AF.Square).inc("ss")
                        em.wait("t", tq)
                        last = em.t.matmul(pb[3][0:1, 0:512], self.onef[:, 0:1],
                                           l1scr[:, D:D + 512], start=(f == 0),
                                           stop=(f == 15)).inc("st")
                        em.wait("s", last)
                    ts = em.s.mul(l1scr[0:1, 1024 + tt_i * 512:1024 + tt_i * 512 + 512],
                                  pb[3][0:1, 0:512], 1.0 / D).inc("ss")
                    self.bank_done(3, ts)
                em.wait("v", self.stok())
                em.v.tensor_mul(l1scr[0:1, D:D + T], l1scr[0:1, 0:T], l1scr[0:1, 0:T])
                self.vf()
                em.v.tensor_sub(l1scr[0:1, 1024:1024 + T], l1scr[0:1, 1024:1024 + T],
                                l1scr[0:1, D:D + T])
                tv = self.vtok()
                self.vf()
                em.wait("s", tv)
                em.s.activation(l1scr[0:1, D:D + T], l1scr[0:1, 1024:1024 + T], AF.Ln, bias=self.prm[0:1, 504:505])
                self.sf()
                ts = em.s.activation(l1scr[0:1, 1024:1024 + T], l1scr[0:1, D:D + T],
                                     AF.Exp, scale=-0.5).inc("ss")
                em.wait("t", ts)
                tmm = None
                for half in range(2):
                    self.use_bank(2 + half)
                    em.t.matmul(pb[2 + half][:, 0:512], self.onef[0:1, :],
                                l1scr[0:1, half * 512:half * 512 + 512],
                                start=True, stop=True).inc("st")
                    self.use_bank(4 + half)
                    tmm = em.t.matmul(pb[4 + half][:, 0:512], self.onef[0:1, :],
                                      l1scr[0:1, 1024 + half * 512:1024 + half * 512 + 512],
                                      start=True, stop=True).inc("st")
                em.wait("v", tmm)
                for half in range(2):
                    em.v.tensor_copy(l1scr[:, D + half * 512:D + half * 512 + 512],
                                     pb[2 + half][:, 0:512])
                    em.v.tensor_copy(l1scr[:, D + T + half * 512:D + T + half * 512 + 512],
                                     pb[4 + half][:, 0:512])
                lastent = None
                for entry in reversed(em.streams["v"]):
                    if entry[0] not in ("wait_ge", "nop"):
                        lastent = entry
                        break
                tv = _Pending(em, lastent).inc("sv")
                em.wait("v", tv)
                for bi in (2, 3, 4, 5):
                    self.bank_done(bi, tv)
                meanb = l1scr[:, D:D + T]
                istdb = l1scr[:, D + T:D + 2 * T]
                for f in range(16):
                    em.v.tensor_sub(x1f[:, f * T:(f + 1) * T], x1f[:, f * T:(f + 1) * T], meanb)
                    self.vf()
                    em.v.tensor_mul(x1n[:, f * T:(f + 1) * T], x1f[:, f * T:(f + 1) * T], istdb)
                    self.vf()
                self.tap("x1n", x1n[:], self.vtok())
            with nc.sbuf_tensor("cwfm", [32, T], BF16) as cwfm, \
                 nc.sbuf_tensor("cwsfm", [16, T], BF16) as cwsfm:
                self.routing(x1n, cwfm, cwsfm)
                em.barrier()
                with nc.sbuf_tensor("moe", [128, 16 * T], BF16) as moe:
                    with nc.sbuf_tensor("oh32sb", [32, 32 * 128], BF16) as oh32, \
                         nc.sbuf_tensor("oh16sb", [16, 16 * 128], BF16) as oh16, \
                         nc.sbuf_tensor("a_e", [128, 6 * T], BF16) as a_e, \
                         nc.sbuf_tensor("a_s", [128, T], BF16) as a_s, \
                         nc.sbuf_tensor("wgu", [128, 2 * 4096], BF16) as wgu, \
                         nc.sbuf_tensor("wdb", [128, 2 * 3072], BF16) as wdb, \
                         nc.sbuf_tensor("wsm", [128, 2048 + 2048], BF16) as wsm, \
                         nc.sbuf_tensor("cwb", [128, T], BF16) as cwb, \
                         nc.sbuf_tensor("cwsb", [128, T], BF16) as cwsb, \
                         nc.sbuf_tensor("mscr", [128, 2 * 512], BF16) as mscr:
                        em.dma("q", oh32[:], self.oh32_d, "oh32")
                        tl = em.dma("q", oh16[:], self.oh16s_d, "oh32")
                        em.wait("t", tl)
                        for e in range(NEXP):
                            self.moe_expert(e, x1n, cwfm, cwsfm, moe, a_e, a_s, wgu,
                                            wdb, wsm, cwb, cwsb, mscr, oh32, oh16)
                        self.tap("moe", moe[:], self.vtok())
                        em.wait("s", self.vtok())
                        for o in range(16):
                            for tt_i in range(2):
                                sl = moe[:, o * T + tt_i * 512:o * T + tt_i * 512 + 512]
                                em.s.activation(mscr[:, 0:512], sl, AF.Silu)
                                self.sf()
                                em.s.copy(sl, mscr[:, 0:512])
                                self.sf()
                        ts = self.stok()
                        em.wait("t", ts)
                        em.wait("v", ts)
                    self.l1_attn(x1n, moe)

    def routing(self, x1n, cwfm, cwsfm):
        nc, em, pb, prm = self.nc, self.em, self.pb, self.prm
        with nc.sbuf_tensor("gw", [32, D], BF16) as gw, \
             nc.sbuf_tensor("gwT", [128, 16 * 32], BF16) as gwT, \
             nc.sbuf_tensor("lgt", [32, T], F32) as lgt, \
             nc.sbuf_tensor("rs", [128, 8 * 32], F32) as rs, \
             nc.sbuf_tensor("rw", [128, 14 * 32], F32) as rw:
            em.wait("g", self.vtok())
            em.wait("g", self.stok())
            tl = em.dma("g", gw[:], self.gate_w, "gw")
            em.wait("t", tl)
            for f in range(16):
                bi = 6 + f % 2
                self.use_bank(bi)
                tt = em.t.transpose(pb[bi][:, 0:32], gw[:, f * 128:(f + 1) * 128],
                                    self.idb[0:32, 0:32]).inc("st")
                em.wait("v", tt)
                tv = em.v.tensor_copy(gwT[:, f * 32:(f + 1) * 32], pb[bi][:, 0:32]).inc("sv")
                self.bank_done(bi, tv)
            em.wait("t", self.vtok())
            for tt_i in range(2):
                self.use_bank(2)
                last = None
                for f in range(16):
                    lhs = gwT[:, f * 32:(f + 1) * 32]
                    em.t.ldweights(lhs)
                    last = em.t.matmul(pb[2][0:32, 0:512], lhs,
                                       x1n[:, f * T + tt_i * 512:f * T + tt_i * 512 + 512],
                                       start=(f == 0), stop=(f == 15)).inc("st")
                em.wait("v", last)
                tv = em.v.tensor_copy(lgt[:, tt_i * 512:tt_i * 512 + 512], pb[2][0:32, 0:512]).inc("sv")
                self.bank_done(2, tv)
            em.wait("t", self.vtok())
            for c in range(8):
                self.use_bank(3)
                tt = em.t.transpose(pb[3][:, 0:32], lgt[:, c * 128:(c + 1) * 128],
                                    self.idf[0:32, 0:32]).inc("st")
                em.wait("v", tt)
                self.sguard("v", "rwk")
                lt = rs[:, 0:32]
                cur = rs[:, 32:64]
                msk = rs[:, 64:96]
                ex = rs[:, 96:128]
                cwv = rs[:, 128:160]
                mx = rs[:, 160:165]
                tv = em.v.tensor_copy(lt, pb[3][:, 0:32]).inc("sv")
                self.bank_done(3, tv)
                self.vf()
                em.v.tensor_copy(cur, lt)
                self.vf()
                for j in range(4):
                    em.v.reduce_max(mx[:, j:j + 1], cur, axis=AX.X)
                    self.vf()
                    if j < 3:
                        em.v.tensor_scalar(msk, cur, mx[:, j:j + 1], None, ALU.is_ge)
                        self.vf()
                        em.v.scalar_tensor_tensor(cur, msk, -1e30, cur, ALU.mult, ALU.add)
                        self.vf()
                em.v.tensor_scalar_mul(mx[:, 4:5], mx[:, 0:1], -1.0)
                tv = self.vtok()
                self.vf()
                em.wait("s", tv)
                ts = em.s.activation(ex, lt, AF.Exp, bias=mx[:, 4:5]).inc("ss")
                em.wait("v", ts)
                em.v.tensor_scalar(msk, lt, mx[:, 3:4], None, ALU.is_ge)
                self.vf()
                em.v.tensor_mul(cwv, ex, msk)
                self.vf()
                em.v.reduce_sum(mx[:, 0:1], cwv, axis=AX.X)
                self.vf()
                em.v.reciprocal(mx[:, 1:2], mx[:, 0:1])
                self.vf()
                em.v.tensor_scalar(rw[:, 0:32], cwv, mx[:, 1:2], None, ALU.mult)
                self.vf()
                em.v.tensor_tensor(rw[:, 32:48], apx(rw, 0, [[14 * 32, 128], [2, 16]]),
                                   apx(rw, 1, [[14 * 32, 128], [2, 16]]), ALU.add)
                tv = self.vtok()
                self.vf()
                em.wait("t", tv)
                self.use_bank(3)
                ta = em.t.transpose(pb[3][0:32, 0:128], rw[:, 0:32], self.idf[:]).inc("st")
                em.wait("v", ta)
                tv = em.v.tensor_copy(cwfm[:, c * 128:(c + 1) * 128], pb[3][0:32, 0:128]).inc("sv")
                self.bank_done(3, tv)
                self.use_bank(3)
                ta = em.t.transpose(pb[3][0:16, 0:128], rw[:, 32:48], self.idf[:]).inc("st")
                em.wait("v", ta)
                tv = em.v.tensor_copy(cwsfm[:, c * 128:(c + 1) * 128], pb[3][0:16, 0:128]).inc("sv")
                self.bank_done(3, tv)
                self.sset("rwk", self.ttok())
            self.tap("cw", cwfm[:], self.vtok())

    def moe_expert(self, e, x1n, cwfm, cwsfm, moe, a_e, a_s, wgu, wdb, wsm, cwb,
                   cwsb, mscr, oh32, oh16):
        nc, em, pb = self.nc, self.em, self.pb
        small = (e % 2 == 0)
        k = e // 2
        em.wait("v", self.ttok())
        em.wait("t", self.vtok())
        for tt_i in range(2):
            self.use_bank(tt_i)
            lhs = oh32[:, e * 128:(e + 1) * 128]
            em.t.ldweights(lhs)
            tm = em.t.matmul(pb[tt_i][:], lhs, cwfm[:, tt_i * 512:tt_i * 512 + 512],
                             start=True, stop=True).inc("st")
            em.wait("v", tm)
            tv = em.v.tensor_copy(cwb[:, tt_i * 512:tt_i * 512 + 512], pb[tt_i][:]).inc("sv")
            self.bank_done(tt_i, tv)
            if small:
                self.use_bank(2 + tt_i)
                lhs = oh16[:, k * 128:(k + 1) * 128]
                em.t.ldweights(lhs)
                tm = em.t.matmul(pb[2 + tt_i][:], lhs, cwsfm[:, tt_i * 512:tt_i * 512 + 512],
                                 start=True, stop=True).inc("st")
                em.wait("v", tm)
                tv = em.v.tensor_copy(cwsb[:, tt_i * 512:tt_i * 512 + 512], pb[2 + tt_i][:]).inc("sv")
                self.bank_done(2 + tt_i, tv)
        em.wait("t", self.vtok())
        for third in range(3):
            for proj in range(2):
                wsrc = self.moe_wg if proj == 0 else self.moe_wu
                wb = wgu[:, (proj % 2) * 4096:(proj % 2) * 4096 + 4096]
                self.sguard("g", (id(wgu), proj % 2))
                em.dma("g", wb.rearrange("p (f i) -> p f i", f=16),
                       apx(wsrc, e * D * MOE_I + third * 256,
                           [[MOE_I, 128], [128 * MOE_I, 16], [1, 256]]), ("dwgu", proj % 2))
            wt = (em.tok(("dwgu", 0)), em.tok(("dwgu", 1)))
            em.wait("t", wt[0])
            em.wait("t", wt[1])
            for ic2 in range(2):
                ii = third * 2 + ic2
                for proj in range(2):
                    wb = wgu[:, (proj % 2) * 4096:(proj % 2) * 4096 + 4096]
                    for tt_i in range(2):
                        bi = proj * 2 + tt_i
                        self.use_bank(bi)
                        last = None
                        for f in range(16):
                            lhs = wb[:, f * 256 + ic2 * 128:f * 256 + ic2 * 128 + 128]
                            em.t.ldweights(lhs)
                            last = em.t.matmul(pb[bi][:], lhs,
                                               x1n[:, f * T + tt_i * 512:f * T + tt_i * 512 + 512],
                                               start=(f == 0), stop=(f == 15)).inc("st")
                        self.sset((id(wgu), proj % 2), last)
                for tt_i in range(2):
                    bg = 0 + tt_i
                    bu = 2 + tt_i
                    tmlast = self.ttok()
                    em.wait("s", tmlast)
                    self.sguard("s", (id(mscr), tt_i % 2))
                    ts = em.s.activation(mscr[:, (tt_i % 2) * 512:(tt_i % 2) * 512 + 512],
                                         pb[bg][:], AF.Silu).inc("ss")
                    em.wait("v", ts)
                    em.v.tensor_mul(mscr[:, (tt_i % 2) * 512:(tt_i % 2) * 512 + 512],
                                    mscr[:, (tt_i % 2) * 512:(tt_i % 2) * 512 + 512], pb[bu][:])
                    self.vf()
                    tv = em.v.tensor_mul(a_e[:, ii * T + tt_i * 512:ii * T + tt_i * 512 + 512],
                                         mscr[:, (tt_i % 2) * 512:(tt_i % 2) * 512 + 512],
                                         cwb[:, tt_i * 512:tt_i * 512 + 512]).inc("sv")
                    self.sset((id(mscr), tt_i % 2), tv)
                    self.bank_done(bg, tv)
                    self.bank_done(bu, tv)
        if small:
            for proj in range(2):
                wsrc = self.cwg if proj == 0 else self.cwu
                wb = wsm[:, 0:2048]
                em.wait("g", self.ttok())
                wt = em.dma("g", wb.rearrange("p (f i) -> p f i", f=16),
                            apx(wsrc, k * D * CHI, [[CHI, 128], [128 * CHI, 16], [1, 128]]), "dw")
                em.wait("t", wt)
                for tt_i in range(2):
                    bi = proj * 2 + tt_i
                    self.use_bank(bi)
                    last = None
                    for f in range(16):
                        lhs = wb[:, f * 128:(f + 1) * 128]
                        em.t.ldweights(lhs)
                        last = em.t.matmul(pb[bi][:], lhs,
                                           x1n[:, f * T + tt_i * 512:f * T + tt_i * 512 + 512],
                                           start=(f == 0), stop=(f == 15)).inc("st")
                    if proj == 1:
                        em.wait("s", last)
                        ts = em.s.activation(mscr[:, tt_i * 512:tt_i * 512 + 512],
                                             pb[0 + tt_i][:], AF.Silu).inc("ss")
                        em.wait("v", ts)
                        em.v.tensor_mul(mscr[:, tt_i * 512:tt_i * 512 + 512],
                                        mscr[:, tt_i * 512:tt_i * 512 + 512], pb[2 + tt_i][:])
                        self.vf()
                        tv = em.v.tensor_mul(a_s[:, tt_i * 512:tt_i * 512 + 512],
                                             mscr[:, tt_i * 512:tt_i * 512 + 512],
                                             cwsb[:, tt_i * 512:tt_i * 512 + 512]).inc("sv")
                        self.bank_done(0 + tt_i, tv)
                        self.bank_done(2 + tt_i, tv)
            em.wait("g", self.ttok())
            wt = em.dma("g", wsm[:, 2048:4096].rearrange("p (o j) -> p o j", o=16),
                        apx(self.cwd, k * CHI * D, [[D, 128], [128, 16], [1, 128]]), "wsm2")
            self.small_wd_tok = wt
        em.wait("t", self.vtok())
        for oq in range(4):
            wb = wdb[:, (oq % 2) * 3072:(oq % 2) * 3072 + 3072]
            self.sguard("g", (id(wdb), oq % 2))
            wt = em.dma("g", wb.rearrange("p (i o) -> p i o", i=6),
                        apx(self.moe_wd, e * MOE_I * D + oq * 512,
                            [[D, 128], [128 * D, 6], [1, 512]]), ("dwd", oq % 2))
            em.wait("t", wt)
            if small:
                em.wait("t", self.small_wd_tok)
            for o4 in range(4):
                o = oq * 4 + o4
                for tt_i in range(2):
                    bi = 4 + (o * 2 + tt_i) % 2
                    self.use_bank(bi)
                    last = None
                    for ii in range(6):
                        lhs = wb[:, ii * 512 + o4 * 128:ii * 512 + o4 * 128 + 128]
                        em.t.ldweights(lhs)
                        last = em.t.matmul(pb[bi][:], lhs,
                                           a_e[:, ii * T + tt_i * 512:ii * T + tt_i * 512 + 512],
                                           start=(ii == 0), stop=(not small and ii == 5)).inc("st")
                    if small:
                        lhs = wsm[:, 2048 + o * 128:2048 + (o + 1) * 128]
                        em.t.ldweights(lhs)
                        last = em.t.matmul(pb[bi][:], lhs,
                                           a_s[:, tt_i * 512:tt_i * 512 + 512],
                                           start=False, stop=True).inc("st")
                    em.wait("v", last)
                    self.sset((id(wdb), oq % 2), last)
                    msl = moe[:, o * T + tt_i * 512:o * T + tt_i * 512 + 512]
                    if e == 0:
                        tv = em.v.tensor_copy(msl, pb[bi][:]).inc("sv")
                    elif e % 2 == 1:
                        em.v.tensor_add(mscr[:, 0:512], msl, pb[bi][:])
                        self.vf()
                        tv = em.v.tensor_copy(msl, mscr[:, 0:512]).inc("sv")
                    else:
                        em.v.tensor_add(mscr[:, 0:512], msl, pb[bi][:])
                        self.vf()
                        tv = em.v.tensor_copy(msl, mscr[:, 0:512]).inc("sv")
                    self.bank_done(bi, tv)
                    self.vf()

    def l1_attn(self, x1n, g1):
        nc, em, pb, prm = self.nc, self.em, self.pb, self.prm
        with nc.sbuf_tensor("state", [128, 8 * T], BF16) as state, \
             nc.sbuf_tensor("bnat", [128, 2 * D], BF16) as bnat, \
             nc.sbuf_tensor("btn", [128, 2 * 2048], BF16) as btn, \
             nc.sbuf_tensor("lascr", [128, 3 * 512], F32) as lascr, \
             nc.sbuf_tensor("lbscr", [128, 512], BF16) as lbscr:
            em.wait("g", self.ttok())
            em.wait("g", self.vtok())
            em.wait("g", self.stok())
            for nc8 in range(8):
                bb = bnat[:, (nc8 % 2) * D:(nc8 % 2) * D + D]
                self.sguard("g", (id(bnat), nc8 % 2))
                tl = em.dma("g", bb, self.l1_B[nc8 * 128:(nc8 + 1) * 128, :], ("dbn", nc8 % 2))
                em.wait("t", tl)
                bt = btn[:, (nc8 % 2) * 2048:(nc8 % 2) * 2048 + 2048]
                self.tr16(bb, [bt[:, fq * 512:(fq + 1) * 512] for fq in range(4)],
                          self.idb[:], (6, 7))
                self.sset((id(bnat), nc8 % 2), self.ttok())
                em.wait("t", self.vtok())
                for tt_i in range(2):
                    bi = 2 + tt_i
                    self.use_bank(bi)
                    last = None
                    for f in range(16):
                        lhs = bt[:, f * 128:(f + 1) * 128]
                        em.t.ldweights(lhs)
                        last = em.t.matmul(pb[bi][:], lhs,
                                           x1n[:, f * T + tt_i * 512:f * T + tt_i * 512 + 512],
                                           start=(f == 0), stop=(f == 15)).inc("st")
                    em.wait("s", last)
                    ts = em.s.activation(state[:, nc8 * T + tt_i * 512:nc8 * T + tt_i * 512 + 512],
                                         pb[bi][:], AF.Tanh).inc("ss")
                    self.bank_done(bi, ts)
                    self.sset((id(btn), nc8 % 2), last)
            em.wait("t", self.stok())
            for dch in range(16):
                cb = bnat[:, (dch % 2) * D:(dch % 2) * D + D][:, 0:1024]
                self.sguard("g", (id(bnat), dch % 2))
                tl = em.dma("g", cb, self.l1_C[dch * 128:(dch + 1) * 128, :], ("dbn", dch % 2))
                em.wait("t", tl)
                ct = btn[:, (dch % 2) * 2048:(dch % 2) * 2048 + 1024]
                self.sguard("t", (id(btn), dch % 2))
                for fq in range(2):
                    bi = 6 + fq % 2
                    self.use_bank(bi)
                    last = None
                    for kk in range(4):
                        nb = fq * 4 + kk
                        last = em.t.transpose(pb[bi][:, kk * 128:(kk + 1) * 128],
                                              cb[:, nb * 128:nb * 128 + 128], self.idb[:]).inc("st")
                    em.wait("v", last)
                    tv = em.v.tensor_copy(ct[:, fq * 512:(fq + 1) * 512], pb[bi][:, 0:512]).inc("sv")
                    self.bank_done(bi, tv)
                em.wait("t", self.vtok())
                for tt_i in range(2):
                    bi = 2 + tt_i
                    self.use_bank(bi)
                    last = None
                    for nc8 in range(8):
                        lhs = ct[:, nc8 * 128:(nc8 + 1) * 128]
                        em.t.ldweights(lhs)
                        last = em.t.matmul(pb[bi][:], lhs,
                                           state[:, nc8 * T + tt_i * 512:nc8 * T + tt_i * 512 + 512],
                                           start=(nc8 == 0), stop=(nc8 == 7)).inc("st")
                    em.wait("v", last)
                    em.wait("v", self.l1d_tok)
                    self.sset((id(bnat), dch % 2), last)
                    self.sset((id(btn), dch % 2), last)
                    xsl = x1n[:, dch * T + tt_i * 512:dch * T + tt_i * 512 + 512]
                    gsl = g1[:, dch * T + tt_i * 512:dch * T + tt_i * 512 + 512]
                    dsl = self.gdelta[:, dch * T + tt_i * 512:dch * T + tt_i * 512 + 512]
                    em.v.scalar_tensor_tensor(lascr[:, 0:512], xsl, prm[:, 224 + dch:225 + dch],
                                              pb[bi][:], ALU.mult, ALU.add)
                    tvv = self.vtok()
                    self.bank_done(bi, tvv)
                    self.vf()
                    em.v.tensor_mul(lascr[:, 512:1024], lascr[:, 0:512], gsl)
                    self.vf()
                    em.v.tensor_add(lbscr[:, 0:512], lascr[:, 512:1024], dsl)
                    self.vf()
                    em.v.tensor_copy(dsl, lbscr[:, 0:512])
                    self.vf()
        self.tap("delta1", self.gdelta[:], self.vtok())

    def outstage(self):
        nc, em, pb = self.nc, self.em, self.pb
        with nc.sbuf_tensor("oxt", [128, 2 * D], F32) as oxt, \
             nc.sbuf_tensor("odt", [128, D], BF16) as odt, \
             nc.sbuf_tensor("oot", [128, 2 * D], F32) as oot:
            for c in range(8):
                buf = oxt[:, (c % 2) * D:(c % 2) * D + D]
                self.sguard("q", ("ox", c % 2))
                tl = em.dma("q", buf, self.x_in[c * 128:(c + 1) * 128, :], ("dox", c % 2))
                for fq in range(4):
                    bi = 6 + fq % 2
                    self.use_bank(bi)
                    last = None
                    for k in range(4):
                        f = fq * 4 + k
                        last = em.t.transpose(pb[bi][:, k * 128:(k + 1) * 128],
                                              self.gdelta[:, f * T + c * 128:f * T + c * 128 + 128],
                                              self.idb[:]).inc("st")
                    em.wait("v", last)
                    tv = em.v.tensor_copy(odt[:, fq * 512:(fq + 1) * 512], pb[bi][:, 0:512]).inc("sv")
                    self.bank_done(bi, tv)
                em.wait("v", tl)
                self.vf()
                ob = oot[:, (c % 2) * D:(c % 2) * D + D]
                self.sguard("v", ("oo", c % 2))
                tv = em.v.tensor_add(ob, buf, odt[:]).inc("sv")
                self.vf()
                self.sset(("ox", c % 2), tv)
                em.wait("q", tv)
                td = em.dma("q", self.out[c * 128:(c + 1) * 128, :], ob, ("dout", c % 2))
                self.sset(("oo", c % 2), td)
        em.wait("q", em.tok(("dout", 0)))
        em.wait("q", em.tok(("dout", 1)))
        if "do" in em.sems:
            em.wait("q", em.tok("do"))

    def finish(self):
        em, nc = self.em, self.nc
        with nc.Block() as block:
            @block.tensor
            def _(e):
                em.replay("t", e)

            @block.vector
            def _(e):
                em.replay("v", e)

            @block.scalar
            def _(e):
                em.replay("s", e)

            @block.sync
            def _(e):
                em.replay("q", e)

            @block.gpsimd
            def _(e):
                em.replay("g", e)
        for cm in self.sem_ctxs:
            cm.__exit__(None, None, None)


def make_consts():
    jj, ii = np.meshgrid(np.arange(128), np.arange(128), indexing="ij")
    oh8 = np.zeros((8, 8 * 128), dtype=np.float32)
    for c in range(8):
        oh8[c, c * 128:(c + 1) * 128] = 1.0
    oh32 = np.zeros((32, 32 * 128), dtype=ml_dtypes.bfloat16)
    for c in range(32):
        oh32[c, c * 128:(c + 1) * 128] = 1.0
    oh16s = np.zeros((16, 16 * 128), dtype=ml_dtypes.bfloat16)
    for c in range(16):
        oh16s[c, c * 128:(c + 1) * 128] = 0.05
    return dict(
        ident_f=np.eye(128, dtype=np.float32),
        ident_b=np.eye(128, dtype=ml_dtypes.bfloat16),
        ones_f=np.ones((128, 128), dtype=np.float32),
        triu_neg=np.where(jj <= ii, 0.0, -1e30).astype(np.float32),
        oh8=oh8, oh32=oh32, oh16s=oh16s)


_BUILD_CACHE = {}


def get_build(dbg_names=()):
    key = tuple(sorted(dbg_names))
    if key not in _BUILD_CACHE:
        kb = KB(dbg_names)
        kb.build()
        _BUILD_CACHE[key] = kb
    return _BUILD_CACHE[key]


def make_in_maps(inputs, dbg_names=()):
    consts = make_consts()
    x = np.ascontiguousarray(inputs["L0_attn_u"].reshape(8192, D))
    nw_b = np.ascontiguousarray(
        np.broadcast_to(inputs["L0_attn_norm_weight"][None, :], (128, D_IN)))
    zeros_x = np.zeros((T, D), dtype=np.float32)
    zeros_cb = np.zeros((CONVD,), dtype=np.float32)
    in_maps = []
    for core in range(8):
        b = core // 2
        hh = core % 2
        rows = slice(b * 2048 + hh * T, b * 2048 + (hh + 1) * T)
        fr_rows = slice(b * 2048, b * 2048 + T)
        m = dict(
            x_in=np.ascontiguousarray(x[rows]),
            x_front=(np.ascontiguousarray(x[fr_rows]) if hh == 1 else zeros_x),
            fr_conv_b=(inputs["L0_attn_conv_b"] if hh == 1 else zeros_cb),
            w_in=inputs["L0_attn_W_in"],
            conv_w=inputs["L0_attn_conv_w"],
            conv_b=inputs["L0_attn_conv_b"],
            dt_bias=inputs["L0_attn_dt_bias"],
            a_log=inputs["L0_attn_A_log"],
            d_vec=inputs["L0_attn_D"],
            w_out=inputs["L0_attn_W_out"],
            ffn_gamma=inputs["L0_ffn_gamma"],
            ffn_w=inputs["L0_ffn_W"],
            l1_B=inputs["L1_attn_B"],
            l1_C=inputs["L1_attn_C"],
            l1_D=inputs["L1_attn_D"],
            gate_w=inputs["L1_ffn_gate_w"],
            moe_wg=inputs["L1_ffn_Wg"],
            moe_wu=inputs["L1_ffn_Wu"],
            moe_wd=inputs["L1_ffn_Wd"],
            cwg=inputs["L1_ffn_cWg"],
            cwu=inputs["L1_ffn_cWu"],
            cwd=inputs["L1_ffn_cWd"],
            nw_bcast=nw_b,
            **consts,
        )
        in_maps.append(m)
    return in_maps


def kernel(**inputs):
    kb = get_build()
    in_maps = make_in_maps(inputs)
    res = run_bass_kernel_spmd(kb.nc, in_maps, core_ids=list(range(8)))
    out = np.empty((8192, D), dtype=np.float32)
    for core in range(8):
        b = core // 2
        hh = core % 2
        out[b * 2048 + hh * T:b * 2048 + (hh + 1) * T] = res.results[core]["out"]
    return out.reshape(4, 2048, D)


# version 02
# speedup vs baseline: 1.0427x; 1.0427x over previous
import numpy as np
import ml_dtypes
import concourse.bass as bass
import concourse.mybir as mybir
from concourse.bass_utils import run_bass_kernel_spmd

F32 = mybir.dt.float32
BF16 = mybir.dt.bfloat16
AF = mybir.ActivationFunctionType
ALU = mybir.AluOpType
AX = mybir.AxisListType
MAGIC = 12582912.0

D = 2048
T = 1024
HALF = 512
D_IN = 4096
NH = 64
DS = 64
CONVD = 6144
NPROJ = 10304
MOE_I = 768
NEXP = 32
NCHK = 16
CHI = 128


def apx(t, off, dims):
    a = t if isinstance(t, bass.AP) else t.ap()
    return bass.AP(a.tensor, a.offset + off, [list(d) for d in dims])


class _Pending:
    __slots__ = ("entry", "em")

    def __init__(self, em, entry):
        self.em = em
        self.entry = entry

    def inc(self, semname, amt=1):
        toks = self.entry[4]
        if semname in toks:
            return (semname, toks[semname])
        sem = self.em.sems[semname]
        sem.n += amt
        self.entry[3].append((sem.h, amt))
        toks[semname] = sem.n
        return (semname, sem.n)


class _EngRec:
    def __init__(self, em, name):
        self._em = em
        self._name = name

    def __getattr__(self, meth):
        em = self._em
        name = self._name

        def call(*a, **kw):
            if meth == "transpose":
                assert a[0].tensor.dtype == a[1].tensor.dtype, \
                    ("transpose dtype", a[0].tensor.dtype, a[1].tensor.dtype)
            entry = [meth, a, kw, [], {}]
            em.streams[name].append(entry)
            return _Pending(em, entry)
        return call


class Sem:
    def __init__(self, handle):
        self.h = handle
        self.n = 0


class Em:
    ENGS = ("t", "v", "s", "q", "g")

    def __init__(self, nc, sems, dma_pool):
        self.nc = nc
        self.streams = {e: [] for e in self.ENGS}
        self.sems = sems
        self.dma_pool = dma_pool
        self.memo = {}
        for e in self.ENGS:
            setattr(self, e, _EngRec(self, e))

    def eng(self, name):
        return getattr(self, name)

    def dma_key(self, key):
        if key not in self.sems:
            if isinstance(key, str) and key in ("dw", "dx", "do", "dsh"):
                pass
            if not self.dma_pool:
                raise RuntimeError("dma sem pool exhausted")
            self.sems[key] = self.dma_pool.pop()
        return key

    def dma(self, eng, out, in_, sem="dw", **kw):
        key = self.dma_key(sem)
        return self.eng(eng).dma_start(out=out, in_=in_, **kw).inc(key, 16)

    def wait(self, eng, tok):
        if tok is None:
            return
        name, val = tok
        key = (eng, name)
        if self.memo.get(key, -1) >= val:
            return
        self.memo[key] = val
        self.streams[eng].append(["wait_ge", (self.sems[name].h, val), {}, [], {}])

    def tok(self, semname):
        return (semname, self.sems[semname].n)

    def barrier(self, engs=ENGS):
        tagmap = {"t": "st", "v": "sv", "s": "ss", "q": "sq", "g": "sg"}
        toks = []
        for e in engs:
            self.wait(e, (tagmap[e], self.sems[tagmap[e]].n))
            toks.append(self.eng(e).nop().inc(tagmap[e]))
        for e in engs:
            for tk in toks:
                self.wait(e, tk)

    def replay(self, engname, engine):
        for idx, (meth, a, kw, incs, _toks) in enumerate(self.streams[engname]):
            try:
                ins = getattr(engine, meth)(*a, **kw)
            except Exception as ex:
                raise RuntimeError("replay fail %s[%d] %s kw=%s: %s"
                                   % (engname, idx, meth, kw, ex)) from ex
            for h, amt in incs:
                ins.then_inc(h, amt)


class KB:
    def __init__(self, dbg_names=()):
        self.nc = bass.Bass("TRN2", target_bir_lowering=False)
        self.dbgreq = set(dbg_names)
        self.dbg = {}
        self.sem_ctxs = []
        sems = {}
        for n in ["st", "sv", "ss", "sq", "sg", "do"]:
            cm = self.nc.semaphore(n)
            sems[n] = Sem(cm.__enter__())
            self.sem_ctxs.append(cm)
        dma_pool = []
        for i in range(90):
            cm = self.nc.semaphore("dp%d" % i)
            dma_pool.append(Sem(cm.__enter__()))
            self.sem_ctxs.append(cm)
        self.em = Em(self.nc, sems, dma_pool)
        self.bank_free = [None] * 8
        self.slot_tok = {}
        self._uid = 0
        self._io()

    def _io(self):
        nc = self.nc

        def din(name, shape, dt=F32):
            return nc.dram_tensor(name, list(shape), dt, kind="ExternalInput").ap()
        self.x_in = din("x_in", (T, D))
        self.x_front = din("x_front", (T, D))
        self.w_in = din("w_in", (D, NPROJ))
        self.conv_w = din("conv_w", (CONVD, 4))
        self.conv_b = din("conv_b", (CONVD,))
        self.fr_conv_b = din("fr_conv_b", (CONVD,))
        self.dt_bias = din("dt_bias", (NH,))
        self.a_log = din("a_log", (NH,))
        self.d_vec = din("d_vec", (NH,))
        self.w_out = din("w_out", (D_IN, D))
        self.ffn_gamma = din("ffn_gamma", (D,))
        self.ffn_w = din("ffn_w", (D, D))
        self.l1_B = din("l1_B", (1024, D))
        self.l1_C = din("l1_C", (D, 1024))
        self.l1_D = din("l1_D", (D,))
        self.gate_w = din("gate_w", (NEXP, D))
        self.moe_wg = din("moe_wg", (NEXP, D, MOE_I))
        self.moe_wu = din("moe_wu", (NEXP, D, MOE_I))
        self.moe_wd = din("moe_wd", (NEXP, MOE_I, D))
        self.cwg = din("cwg", (NCHK, D, CHI))
        self.cwu = din("cwu", (NCHK, D, CHI))
        self.cwd = din("cwd", (NCHK, CHI, D))
        self.ident_f = din("ident_f", (128, 128))
        self.ident_b = din("ident_b", (128, 128), BF16)
        self.ones_f = din("ones_f", (128, 128))
        self.triu_neg_d = din("triu_neg", (128, 128))
        self.oh8_d = din("oh8", (8, 8 * 128))
        self.oh32_d = din("oh32", (32, 32 * 128), BF16)
        self.oh16s_d = din("oh16s", (16, 16 * 128), BF16)
        self.nw_bcast_d = din("nw_bcast", (128, D_IN))
        self.out = nc.dram_tensor("out", [T, D], F32, kind="ExternalOutput").ap()
        self.xn_dram = nc.dram_tensor("xn_scr", [16 * 128 * T], BF16, kind="Internal").ap()
        self.cum_dram = nc.dram_tensor("cum_scr", [NH * 128], F32, kind="Internal").ap()
        self.ec_dram = nc.dram_tensor("ec_scr", [NH * 128], F32, kind="Internal").ap()
        self.sz_dram = nc.dram_tensor("sz_scr", [T * D_IN], BF16, kind="Internal").ap()
        self.c64_dram = nc.dram_tensor("c64_scr", [64 * NH * HALF], BF16, kind="Internal").ap()
        self.b64_dram = nc.dram_tensor("b64_scr", [64 * 16 * HALF], BF16, kind="Internal").ap()

    def tap(self, name, ap_, tok=None, eng="q"):
        if name not in self.dbgreq:
            return
        shp = list(ap_.shape)
        o = self.nc.dram_tensor("dbg_" + name, shp, ap_.tensor.dtype, kind="ExternalOutput").ap()
        self.dbg[name] = "dbg_" + name
        if tok is not None:
            self.em.wait(eng, tok)
        ttap = self.em.dma(eng, o, ap_, "do")
        for e in Em.ENGS:
            self.em.wait(e, ttap)

    def etok(self, eng, sem):
        em = self.em
        for entry in reversed(em.streams[eng]):
            if entry[0] not in ("wait_ge", "nop"):
                return _Pending(em, entry).inc(sem)
        return None

    def vtok(self):
        return self.etok("v", "sv")

    def stok(self):
        return self.etok("s", "ss")

    def ttok(self):
        return self.etok("t", "st")

    def vf(self):
        self.em.wait("v", self.vtok())

    def sf(self):
        self.em.wait("s", self.stok())

    def use_bank(self, i):
        self.em.wait("t", self.bank_free[i])

    def sb(self, nm, shape, dt):
        self._uid += 1
        return self.nc.sbuf_tensor("%s_u%d" % (nm, self._uid), shape, dt)

    def sguard(self, eng, key):
        for tok in self.slot_tok.get(key, ()):
            self.em.wait(eng, tok)

    def sset(self, key, *toks):
        self.slot_tok[key] = [t for t in toks if t is not None]

    def bank_done(self, i, tok):
        self.bank_free[i] = tok

    def tr16(self, src, outs, ident, banks):
        em, pb = self.em, self.pb
        for fq in range(4):
            bi = banks[fq % 2]
            self.use_bank(bi)
            last = None
            for k in range(4):
                f = fq * 4 + k
                last = em.t.transpose(pb[bi][:, k * 128:(k + 1) * 128],
                                      src[:, f * 128:f * 128 + 128], ident).inc("st")
            em.wait("v", last)
            evsrc = pb[bi][:, 0:512]
            if len(outs[fq].shape) == 3:
                evsrc = evsrc.rearrange("p (a b) -> p a b", a=4)
            tv = em.v.tensor_copy(outs[fq], evsrc).inc("sv")
            self.bank_done(bi, tv)

    def fm_out_ap(self, dst, fq, tok_off, stride):
        return apx(dst, (fq * 4) * stride + tok_off,
                   [[16 * stride, 128], [stride, 4], [1, 128]])

    def build(self, until="all"):
        nc, em = self.nc, self.em
        self.idf = nc.alloc_sbuf_tensor("idf", [128, 128], F32)
        self.idb = nc.alloc_sbuf_tensor("idb", [128, 128], BF16)
        self.onef = nc.alloc_sbuf_tensor("onef", [128, 128], F32)
        self.oh8 = nc.alloc_sbuf_tensor("oh8_sb", [8, 8 * 128], F32)
        self.prm = nc.alloc_sbuf_tensor("prm", [128, 512], F32)
        self.convp = nc.alloc_sbuf_tensor("convp", [128, 288], F32)
        self.gdelta = nc.alloc_sbuf_tensor("gdelta", [128, 16 * T], BF16)
        self.pb = [nc.alloc_psum_tensor("pb%d" % i, [128, 512], F32) for i in range(6)]
        self.pb.append(nc.alloc_psum_tensor("pbb0", [128, 1024], BF16))
        self.pb.append(nc.alloc_psum_tensor("pbb1", [128, 1024], BF16))
        self.front_limit = "all"
        if until in ("front_ln", "front_winx", "front_dt", "front_tok", "front_S", "front_dtA", "front_dtB", "front_dtC"):
            self.front_limit = until[6:]
            until = "front"
        order = ["consts", "ffn", "front", "l0", "l1"]
        stop = len(order) if until == "all" else order.index(until) + 1
        todo = set(order[:stop])
        self.consts()
        if "ffn" in todo:
            self.l0_ln_ffn()
            em.barrier()
        if "front" in todo or "l0" in todo:
            with nc.sbuf_tensor("H_ar", [128, NH * DS + 512], F32) as H_ar, \
                 nc.sbuf_tensor("xpre", [128, 48], BF16) as xpre:
                self.H = H_ar
                self.xpre = xpre
                em.v.memset(xpre[:], 0.0)
                self.vf()
                if "front" in todo:
                    self.front()
                    em.barrier()
                if "l0" in todo:
                    self.l0_attn()
        em.barrier()
        if "l1" in todo:
            self.l1()
            em.barrier()
        self.outstage()
        self.finish()
        return self

    def consts(self):
        em, prm, pb = self.em, self.prm, self.pb
        em.dma("q", self.idf[:], self.ident_f, "consts")
        em.dma("q", self.idb[:], self.ident_b, "consts")
        em.dma("q", self.onef[:], self.ones_f, "consts")
        em.dma("q", self.oh8[:], self.oh8_d, "consts")
        for f in range(16):
            em.dma("q", prm[:, f:f + 1], self.ffn_gamma[f * 128:(f + 1) * 128, None], "consts")
        em.dma("q", prm[0:NH, 240:241], self.dt_bias[:, None], "consts")
        em.dma("q", prm[0:NH, 241:242], self.a_log[:, None], "consts")
        em.dma("q", prm[0:NH, 243:244], self.d_vec[:, None], "consts")
        em.dma("q", self.convp[:, 0:192].rearrange("p (c k) -> p c k", c=48),
               apx(self.conv_w, 0, [[4, 128], [128 * 4, 48], [1, 4]]), "consts")
        em.dma("q", apx(prm, 352, [[512, 48], [1, 128]]), 
               apx(self.conv_b, 0, [[128, 48], [1, 128]]), "cvb1")
        tok = em.tok("consts")
        tokb = em.tok("cvb1")
        for e in Em.ENGS:
            em.wait(e, tok)
            em.wait(e, tokb)
        t1 = em.t.transpose(pb[0][:, 0:48], prm[0:48, 352:480], self.idf[0:48, 0:48]).inc("st")
        em.wait("v", t1)
        tv = em.v.tensor_copy(self.convp[:, 192:240], pb[0][:, 0:48]).inc("sv")
        em.wait("q", tv)
        em.dma("q", apx(prm, 352, [[512, 48], [1, 128]]),
               apx(self.fr_conv_b, 0, [[128, 48], [1, 128]]), "cvb2")
        tok = em.tok("cvb2")
        em.wait("t", tok)
        t1 = em.t.transpose(pb[1][:, 0:48], prm[0:48, 352:480], self.idf[0:48, 0:48]).inc("st")
        em.wait("v", t1)
        tv = em.v.tensor_copy(self.convp[:, 240:288], pb[1][:, 0:48]).inc("sv")
        em.wait("t", tv)
        self.bank_done(0, tv)
        self.bank_done(1, tv)
        em.v.memset(prm[:, 504:505], 1e-5)
        em.v.memset(prm[:, 505:506], 1e-6)
        em.v.memset(prm[:, 506:507], 1.0)
        self.vf()
        em.s.activation(prm[0:NH, 244:245], prm[0:NH, 241:242], AF.Exp)
        self.sf()
        tk = em.s.mul(prm[0:NH, 242:243], prm[0:NH, 244:245], -1.0).inc("ss")
        em.wait("v", tk)
        em.wait("t", tk)
        t1 = em.t.transpose(pb[0][0:1, 0:NH], prm[0:NH, 243:244], self.idf[0:NH, 0:NH]).inc("st")
        em.wait("s", t1)
        t2 = em.s.copy(prm[0:1, 352:352 + NH], pb[0][0:1, 0:NH]).inc("ss")
        em.wait("t", t2)
        t3 = em.t.matmul(pb[1][:, 0:NH], self.onef[0:1, :], prm[0:1, 352:352 + NH],
                         start=True, stop=True).inc("st")
        em.wait("v", t3)
        t4 = em.v.tensor_copy(prm[:, 96:96 + NH], pb[1][:, 0:NH]).inc("sv")
        em.wait("t", t4)
        em.barrier()

    def l0_ln_ffn(self):
        nc, em, pb, prm = self.nc, self.em, self.pb, self.prm
        ln_mean, ln_istd, ln_var = prm[:, 16:24], prm[:, 24:32], prm[:, 32:40]
        cA, cB, cC = prm[:, 40:48], prm[:, 48:56], prm[:, 56:64]
        inv_as, rmsf = prm[:, 64:72], prm[:, 72:80]
        with nc.sbuf_tensor("xq", [128, 16 * T], BF16) as xq, \
             nc.sbuf_tensor("xtile", [128, 2 * D], F32) as xtile, \
             nc.sbuf_tensor("scr", [128, D], F32) as scr:
            with nc.sbuf_tensor("xn32", [128, 16 * T], F32) as xn32:
                for c in range(8):
                    buf = xtile[:, (c % 2) * D:(c % 2) * D + D]
                    self.sguard("q", ("xt", c % 2))
                    tl = em.dma("q", buf, self.x_in[c * 128:(c + 1) * 128, :], ("dxt", c % 2))
                    em.wait("v", tl)
                    em.wait("s", tl)
                    self.sguard("s", "scrw")
                    self.sguard("v", "scrw")
                    em.v.reduce_sum(cA[:, c:c + 1], buf, axis=AX.X)
                    ts = em.s.activation(scr[:], buf, AF.Square, accum_out=cB[:, c:c + 1]).inc("ss")
                    self.vf()
                    em.wait("v", ts)
                    em.v.tensor_scalar_mul(ln_mean[:, c:c + 1], cA[:, c:c + 1], 1.0 / D)
                    self.vf()
                    em.v.tensor_mul(cC[:, c:c + 1], ln_mean[:, c:c + 1], ln_mean[:, c:c + 1])
                    self.vf()
                    tv = em.v.tensor_scalar(ln_var[:, c:c + 1], cB[:, c:c + 1], 1.0 / D,
                                            cC[:, c:c + 1], ALU.mult, ALU.subtract).inc("sv")
                    em.wait("s", tv)
                    em.s.activation(cC[:, c:c + 1], ln_var[:, c:c + 1], AF.Ln, bias=self.prm[:, 504:505])
                    self.sf()
                    ti = em.s.activation(ln_istd[:, c:c + 1], cC[:, c:c + 1], AF.Exp, scale=-0.5).inc("ss")
                    em.wait("v", ti)
                    tn = em.v.tensor_scalar(scr[:], buf, ln_mean[:, c:c + 1], ln_istd[:, c:c + 1],
                                            ALU.subtract, ALU.mult).inc("sv")
                    self.sset(("xt", c % 2), tn, ts)
                    em.wait("t", tn)
                    self.tr16(scr, [self.fm_out_ap(xn32, fq, c * 128, T) for fq in range(4)],
                              self.idf[:], (0, 1))
                    self.sset("scrw", self.ttok())
                self.tap("xn32", xn32[:], self.vtok())
                afold = xtile[:, 0:T]
                self.vf()
                em.wait("v", self.ttok())
                for f in range(16):
                    tgt = afold if f == 0 else scr[:, 0:T]
                    em.v.tensor_scalar(tgt, xn32[:, f * T:(f + 1) * T], prm[:, f:f + 1],
                                       None, ALU.mult)
                    self.vf()
                    em.v.scalar_tensor_tensor(tgt, tgt, -1.0, tgt, ALU.mult, ALU.max)
                    self.vf()
                    if f > 0:
                        em.v.tensor_max(afold, afold, scr[:, 0:T])
                        self.vf()
                em.wait("t", self.vtok())
                for c in range(8):
                    self.use_bank(2)
                    tt = em.t.transpose(pb[2][:, 0:128], afold[:, c * 128:(c + 1) * 128],
                                        self.idf[:]).inc("st")
                    em.wait("v", tt)
                    tv = em.v.reduce_max(cA[:, c:c + 1], pb[2][:, 0:128], axis=AX.X).inc("sv")
                    self.bank_done(2, tv)
                self.vf()
                em.v.tensor_scalar_add(cC, ln_var, 1e-5)
                self.vf()
                em.v.reciprocal(prm[:, 44:52] if False else cB, cC)
                self.vf()
                em.v.tensor_mul(cB, ln_var, cB)
                tv = self.vtok()
                self.vf()
                em.wait("s", tv)
                em.s.activation(cC, cB, AF.Ln, bias=self.prm[:, 505:506])
                self.sf()
                ts = em.s.activation(rmsf, cC, AF.Exp, scale=-0.5).inc("ss")
                em.wait("v", ts)
                em.v.tensor_tensor(cB, rmsf, cA, ALU.mult)
                self.vf()
                em.v.tensor_scalar_add(cB, cB, 1e-5)
                self.vf()
                em.v.tensor_scalar_mul(inv_as, cB, 1.0 / 127.0)
                em.v.reciprocal(cC, cB)
                self.vf()
                em.v.tensor_scalar_mul(cC, cC, 127.0)
                self.vf()
                em.v.tensor_tensor(cB, rmsf, cC, ALU.mult)
                self.vf()
                ccb = xtile[:, 0:T]
                ivb = xtile[:, T:2 * T]
                self.tap("prmq", prm[:, 0:96], self.vtok())
                self.bcast8(cB, ccb)
                self.bcast8(inv_as, ivb)
                self.tap("ccb", ccb, self.vtok())
                self.tap("ivb", ivb, self.vtok())
                for f in range(16):
                    self.sguard("v", "scrw")
                    em.v.tensor_scalar(scr[:, 0:T], xn32[:, f * T:(f + 1) * T], prm[:, f:f + 1],
                                       None, ALU.mult)
                    self.vf()
                    em.v.tensor_tensor(scr[:, 0:T], scr[:, 0:T], ccb, ALU.mult)
                    self.vf()
                    em.v.tensor_scalar_add(scr[:, 0:T], scr[:, 0:T], MAGIC)
                    self.vf()
                    em.v.tensor_scalar_sub(xq[:, f * T:(f + 1) * T], scr[:, 0:T], MAGIC)
                    self.vf()
                self.tap("xq", xq[:], self.vtok())
                tv = self.vtok()
                em.wait("g", tv)
                for f in range(16):
                    em.dma("g", apx(self.xn_dram, f * 128 * T, [[T, 128], [1, T]]),
                           xn32[:, f * T:(f + 1) * T], "xnout")
                self.xnout_tok = em.tok("xnout")
            em.wait("q", self.xnout_tok)
            em.wait("q", self.vtok())
            with nc.sbuf_tensor("wtile", [128, 2 * D], F32) as wtile, \
                 nc.sbuf_tensor("wq_n", [128, D], BF16) as wq_n, \
                 nc.sbuf_tensor("wqT", [128, 2 * D], BF16) as wqT:
                wsum, wabs = prm[:, 40:41], prm[:, 41:42]
                for i in range(16):
                    buf = wtile[:, (i % 2) * D:(i % 2) * D + D]
                    self.sguard("q", ("wt", i % 2))
                    tl = em.dma("q", buf, self.ffn_w[i * 128:(i + 1) * 128, :], ("dwt", i % 2))
                    em.wait("v", tl)
                    tr = em.v.reduce_sum(wabs, buf, axis=AX.X, apply_absolute_value=True).inc("sv")
                    self.sset(("wt", i % 2), tr)
                    self.vf()
                    if i == 0:
                        em.v.tensor_copy(wsum, wabs)
                    else:
                        em.v.tensor_add(wsum, wsum, wabs)
                    self.vf()
                tv = self.vtok()
                em.wait("t", tv)
                self.use_bank(2)
                tt = em.t.transpose(pb[2][0:1, 0:128], wsum, self.idf[:]).inc("st")
                em.wait("v", tt)
                em.v.reduce_sum(prm[0:1, 490:491], pb[2][0:1, 0:128], axis=AX.X)
                self.bank_done(2, self.vtok())
                self.vf()
                em.v.tensor_scalar(prm[0:1, 491:492], prm[0:1, 490:491], 1.0 / (D * D), 1e-5,
                                   ALU.mult, ALU.add)
                self.vf()
                em.v.reciprocal(prm[0:1, 490:491], prm[0:1, 491:492])
                tv = self.vtok()
                em.wait("t", tv)
                self.vf()
                self.use_bank(2)
                t1 = em.t.matmul(pb[2][:, 0:2], self.onef[0:1, :], prm[0:1, 490:492],
                                 start=True, stop=True).inc("st")
                em.wait("v", t1)
                tv = em.v.tensor_copy(prm[:, 488:490], pb[2][:, 0:2]).inc("sv")
                self.bank_done(2, tv)
                em.wait("v", tv)
                em.wait("t", tv)
                em.wait("s", tv)
                wscale_b = prm[:, 488:489]
                winv_b = prm[:, 489:490]
                for o in range(16):
                    buf = wtile[:, (o % 2) * D:(o % 2) * D + D]
                    self.sguard("q", ("wt", o % 2))
                    tl = em.dma("q", buf, self.ffn_w[o * 128:(o + 1) * 128, :], ("dwt", o % 2))
                    em.wait("v", tl)
                    em.v.tensor_scalar(buf, buf, wscale_b, MAGIC, ALU.mult, ALU.add)
                    self.vf()
                    em.v.tensor_scalar_sub(buf, buf, MAGIC)
                    self.vf()
                    self.sguard("v", "wqn")
                    tv = em.v.tensor_scalar(wq_n[:], buf, 1.0, -1.0, ALU.min, ALU.max).inc("sv")
                    em.wait("t", tv)
                    self.tr16(wq_n, [wqT[:, fq * 512:(fq + 1) * 512] for fq in range(4)],
                              self.idb[:], (6, 7))
                    self.sset("wqn", self.ttok())
                    em.wait("t", self.vtok())
                    for tt_i in range(2):
                        bi = 2 + tt_i
                        self.use_bank(bi)
                        last = None
                        for f in range(16):
                            lhs = wqT[:, f * 128:(f + 1) * 128]
                            em.t.ldweights(lhs)
                            last = em.t.matmul(pb[bi][:], lhs,
                                               xq[:, f * T + tt_i * 512:f * T + tt_i * 512 + 512],
                                               start=(f == 0), stop=(f == 15)).inc("st")
                        em.wait("v", last)
                        self.sguard("v", ("fsil", o % 2))
                        em.v.tensor_tensor(buf[:, 0:512], pb[bi][:],
                                           ivb[:, tt_i * 512:tt_i * 512 + 512], ALU.mult)
                        tvv = self.vtok()
                        self.vf()
                        em.wait("s", tvv)
                        tsl = em.s.activation(self.gdelta[:, o * T + tt_i * 512:o * T + tt_i * 512 + 512],
                                              buf[:, 0:512], AF.Silu, scale=winv_b).inc("ss")
                        self.bank_done(bi, tsl)
                        self.sset(("wt", o % 2), tsl)
                        self.sset(("fsil", o % 2), tsl)
                self.tap("g0", self.gdelta[:], self.stok(), eng="q")

    def bcast8(self, cols, dst):
        em, pb, prm = self.em, self.pb, self.prm
        em.wait("t", self.vtok())
        self.use_bank(3)
        tt = em.t.transpose(pb[3][0:8, 0:128], cols, self.idf[:]).inc("st")
        em.wait("s", tt)
        ts = em.s.copy(prm[0:8, 352:480], pb[3][0:8, 0:128]).inc("ss")
        self.bank_done(3, ts)
        em.wait("t", ts)
        for c in range(8):
            self.use_bank(3)
            t1 = em.t.matmul(pb[3][:, 0:128], self.oh8[:, c * 128:(c + 1) * 128],
                             prm[0:8, 352:480], start=True, stop=True).inc("st")
            em.wait("v", t1)
            tv = em.v.tensor_copy(dst[:, c * 128:(c + 1) * 128], pb[3][:, 0:128]).inc("sv")
            self.bank_done(3, tv)
        em.wait("t", self.vtok())

    def front(self):
        nc, em = self.nc, self.em
        em.v.memset(self.H[0:64, :], 0.0)
        self.vf()
        with nc.sbuf_tensor("fxn", [128, 16 * HALF], BF16) as fxn, \
             nc.sbuf_tensor("fxB", [128, 16 * HALF], BF16) as fxB, \
             nc.sbuf_tensor("ftail", [128, 144], BF16) as ftail, \
             nc.sbuf_tensor("fdt", [128, 3456], F32) as fdt, \
             nc.sbuf_tensor("ftok", [128, 2 * T], BF16) as ftok, \
             nc.sbuf_tensor("fxdtd", [128, D_IN], BF16) as fxdtd, \
             nc.sbuf_tensor("fwbuf", [128, 2 * 2048], BF16) as fwbuf, \
             nc.sbuf_tensor("fcst", [128, 2 * 516], BF16) as fcst, \
             nc.sbuf_tensor("fdiag", [128, 4 * 128], BF16) as fdiag, \
             nc.sbuf_tensor("fscr", [128, 2 * D], F32) as fscr:
            em.v.memset(ftail[:], 0.0)
            self.vf()
            for fh in range(2):
                h0 = fh * HALF
                em.wait("s", self.vtok())
                em.wait("s", self.ttok())
                em.wait("v", self.ttok())
                for c in range(4):
                    buf = fscr[:, (c % 2) * D:(c % 2) * D + D]
                    ob = ftok[:, 0:D]
                    self.sguard("q", ("fx", c % 2))
                    tl = em.dma("q", buf, self.x_front[h0 + c * 128:h0 + (c + 1) * 128, :], ("dfx", c % 2))
                    em.wait("v", tl)
                    em.wait("s", tl)
                    em.v.reduce_sum(self.prm[:, 40:41], buf, axis=AX.X)
                    ts = em.s.activation(ob, buf, AF.Square, accum_out=self.prm[:, 41:42]).inc("ss")
                    self.vf()
                    em.wait("v", ts)
                    em.v.tensor_scalar_mul(self.prm[:, 42:43], self.prm[:, 40:41], 1.0 / D)
                    self.vf()
                    em.v.tensor_mul(self.prm[:, 43:44], self.prm[:, 42:43], self.prm[:, 42:43])
                    self.vf()
                    tv = em.v.tensor_scalar(self.prm[:, 44:45], self.prm[:, 41:42], 1.0 / D,
                                            self.prm[:, 43:44], ALU.mult, ALU.subtract).inc("sv")
                    em.wait("s", tv)
                    em.s.activation(self.prm[:, 45:46], self.prm[:, 44:45], AF.Ln, bias=self.prm[:, 504:505])
                    self.sf()
                    ti = em.s.activation(self.prm[:, 46:47], self.prm[:, 45:46], AF.Exp,
                                         scale=-0.5).inc("ss")
                    em.wait("v", ti)
                    em.v.tensor_scalar(buf, buf, self.prm[:, 42:43], self.prm[:, 46:47],
                                       ALU.subtract, ALU.mult)
                    self.vf()
                    self.sguard("v", "fwb")
                    tc = em.v.tensor_copy(fwbuf[:, 0:D], buf).inc("sv")
                    self.sset(("fx", c % 2), tc)
                    em.wait("t", tc)
                    self.tr16(fwbuf[:, 0:D], [self.fm_out_ap(fxn, fq, c * 128, HALF) for fq in range(4)],
                              self.idb[:], (6, 7))
                    self.sset("fwb", self.ttok())
                if fh == 1:
                    self.vf()
                    em.v.tensor_copy(self.xpre[:].rearrange("p (f k) -> p f k", f=16),
                                     apx(fxn, 509, [[16 * HALF, 128], [HALF, 16], [1, 3]]))
                    self.vf()
                if self.front_limit == "ln":
                    continue
                self.winx_pass(fxn, fxB, None, fdt, fwbuf, fcst, fdiag, ftail,
                               m_list=list(range(32, 48)), dt_too=True, front=True, h0=h0)
                if self.front_limit == "winx":
                    continue
                for c in range(4):
                    self.sum_chunk_front(fxB, fdt, ftok, fxdtd, c)
                    if self.front_limit in ("dt", "tok", "S"):
                        break
        self.tap("H_init", self.H[0:64, 0:NH * DS], self.vtok())

    def winx_pass(self, xnh, xB_fm, C_stage, dtb, wbuf, cstage, diag, tailbuf,
                  m_list, dt_too, front, h0, pre_src=None):
        nc, em, pb, prm = self.nc, self.em, self.pb, self.prm
        bias_off = 240 if front else 192
        em.wait("t", self.vtok())
        cdma_toks = [None, None]
        for mi, m in enumerate(m_list):
            wb = wbuf[:, (mi % 2) * 2048:(mi % 2) * 2048 + 2048]
            self.sguard("g", (id(wbuf), mi % 2))
            self.sguard("g", "fwb")
            wt = em.dma("g", wb.rearrange("p (f i) -> p f i", f=16),
                        apx(self.w_in, m * 128, [[NPROJ, 128], [128 * NPROJ, 16], [1, 128]]),
                        ("dwin", id(wbuf), mi % 2))
            em.wait("t", wt)
            bi = mi % 2
            self.use_bank(bi)
            if pre_src is not None:
                self.use_bank(4)
            last = None
            for f in range(16):
                lhs = wb[:, f * 128:(f + 1) * 128]
                em.t.ldweights(lhs)
                last = em.t.matmul(pb[bi][:], lhs, xnh[:, f * HALF:(f + 1) * HALF],
                                   start=(f == 0), stop=(f == 15)).inc("st")
                if pre_src is not None:
                    last = em.t.matmul(pb[4][:, 0:3], lhs, pre_src[:, f * 3:(f + 1) * 3],
                                       start=(f == 0), stop=(f == 15)).inc("st")
            self.sset((id(wbuf), mi % 2), last)
            ch0 = m * 128 - D_IN
            cmi = ch0 // 128
            cs = cstage[:, (mi % 2) * 516:(mi % 2) * 516 + 516]
            em.wait("v", last)
            self.sguard("v", (id(cstage), mi % 2))
            if pre_src is not None:
                tpre = em.v.tensor_copy(cs[:, 0:3], pb[4][:, 0:3]).inc("sv")
                self.bank_done(4, tpre)
            else:
                em.v.tensor_copy(cs[:, 0:3], tailbuf[:, cmi * 3:cmi * 3 + 3])
            tv = em.v.tensor_copy(cs[:, 3:515], pb[bi][:]).inc("sv")
            self.bank_done(bi, tv)
            self.vf()
            em.v.tensor_copy(tailbuf[:, cmi * 3:cmi * 3 + 3], cs[:, 512:515])
            self.vf()
            for k in range(4):
                em.v.tensor_scalar(diag[:, k * 128:(k + 1) * 128], self.idb[:],
                                   self.convp[:, cmi * 4 + k:cmi * 4 + k + 1], None, ALU.mult)
            tdg = self.vtok()
            self.vf()
            em.wait("t", tdg)
            bi2 = 2 + mi % 2
            self.use_bank(bi2)
            last = None
            for k in range(4):
                em.t.ldweights(diag[:, k * 128:(k + 1) * 128])
                last = em.t.matmul(pb[bi2][:], diag[:, k * 128:(k + 1) * 128],
                                   cs[:, k:k + 512], start=(k == 0), stop=(k == 3)).inc("st")
            self.sset((id(cstage), mi % 2), last)
            em.wait("s", last)
            bias_col = self.convp[:, bias_off + cmi:bias_off + cmi + 1]
            if ch0 < 2048:
                blk = ch0 // 128
                tv = em.s.activation(xB_fm[:, blk * HALF:(blk + 1) * HALF], pb[bi2][:],
                                     AF.Silu, bias=bias_col).inc("ss")
                self.bank_done(bi2, tv)
            else:
                b = (ch0 - 2048) // 128
                cst = C_stage[:, (mi % 2) * HALF:(mi % 2) * HALF + HALF]
                em.wait("s", cdma_toks[mi % 2])
                tv = em.s.activation(cst, pb[bi2][:], AF.Silu, bias=bias_col).inc("ss")
                self.bank_done(bi2, tv)
                em.wait("g", tv)
                em.dma("g", apx(self.c64_dram, (2 * b) * HALF, [[NH * HALF, 64], [1, 512]]),
                       cst[0:64, :], ("c64o", h0, mi % 2))
                cdma_toks[mi % 2] = em.dma("g", apx(self.c64_dram, (2 * b + 1) * HALF,
                                                    [[NH * HALF, 64], [1, 512]]),
                                           cst[64:128, :], ("c64o", h0, mi % 2))
                if b < 8:
                    pass
        if not front:
            tvv = self.stok()
            em.wait("g", tvv)
            em.dma("g", apx(self.b64_dram, 0, [[16 * HALF, 64], [2 * HALF, 8], [1, 512]]),
                   apx(xB_fm, 8 * HALF, [[16 * HALF, 64], [HALF, 8], [1, 512]]), ("b64o", h0))
            em.dma("g", apx(self.b64_dram, HALF, [[16 * HALF, 64], [2 * HALF, 8], [1, 512]]),
                   apx(xB_fm, 64 * (16 * HALF) + 8 * HALF, [[16 * HALF, 64], [HALF, 8], [1, 512]]),
                   ("b64o", h0))
        if dt_too:
            wb = wbuf[:, 0:16 * 64]
            self.sguard("g", (id(wbuf), 0))
            wt = em.dma("g", wb.rearrange("p (f i) -> p f i", f=16),
                        apx(self.w_in, 10240, [[NPROJ, 128], [128 * NPROJ, 16], [1, 64]]),
                        ("dwin", id(wbuf), 0))
            em.wait("t", wt)
            self.use_bank(4)
            last = None
            for f in range(16):
                lhs = wb[:, f * 64:(f + 1) * 64]
                em.t.ldweights(lhs)
                last = em.t.matmul(pb[4][0:64, 0:512], lhs, xnh[:, f * HALF:(f + 1) * HALF],
                                   start=(f == 0), stop=(f == 15)).inc("st")
            em.wait("v", last)
            ts = em.v.tensor_scalar(dtb[0:64, 0:512], pb[4][0:64, 0:512], prm[0:64, 240:241],
                                    None, ALU.add).inc("sv")
            self.bank_done(4, ts)
            em.wait("s", ts)
            em.s.activation(dtb[0:64, 512:1024], dtb[0:64, 0:512], AF.Abs)
            self.sf()
            em.s.activation(dtb[0:64, 1024:1536], dtb[0:64, 512:1024], AF.Exp, scale=-1.0)
            self.sf()
            em.s.activation(dtb[0:64, 1536:2048], dtb[0:64, 1024:1536], AF.Ln, bias=self.prm[0:64, 506:507])
            self.sf()
            ts = em.s.activation(dtb[0:64, 512:1024], dtb[0:64, 0:512], AF.Relu).inc("ss")
            em.wait("v", ts)
            tv = em.v.tensor_add(dtb[0:64, 2048:2560], dtb[0:64, 512:1024],
                                 dtb[0:64, 1536:2048]).inc("sv")
            self.vf()
            em.wait("s", tv)
            em.wait("t", tv)

    def dt_chunk(self, dtb, c, need_ec, need_toks):
        nc, em, pb, prm = self.nc, self.em, self.pb, self.prm
        lim = getattr(self, "front_limit", "all")
        dtp = dtb[0:64, 2048:2560]
        dA = dtb[0:64, 2560:2688]
        cum = dtb[0:64, 2688:2816]
        dte = dtb[0:64, 2816:2944]
        ec = dtb[0:64, 2944:3072]
        dtd = dtb[0:64, 3072:3200]
        em.v.tensor_scalar(dA, dtp[:, c * 128:(c + 1) * 128], prm[0:64, 242:243], None, ALU.mult)
        self.vf()
        em.v.tensor_tensor_scan(cum, self.onef[0:64, 0:128], dA, 0.0, ALU.mult, ALU.add)
        self.vf()
        em.v.tensor_copy(prm[0:64, 245:246], cum[:, 127:128])
        tv = self.vtok()
        self.vf()
        if lim == "dtA":
            return
        em.wait("s", tv)
        em.s.activation(dte, cum, AF.Exp, bias=prm[0:64, 245:246], scale=-1.0)
        em.s.activation(prm[0:64, 246:247], prm[0:64, 245:246], AF.Exp)
        if need_ec:
            em.s.activation(ec, cum, AF.Exp)
        ts = self.stok()
        self.sf()
        em.wait("v", ts)
        em.v.tensor_mul(dtd, dtp[:, c * 128:(c + 1) * 128], dte)
        tv = self.vtok()
        self.vf()
        if lim == "dtB":
            return
        em.wait("t", tv)
        self.use_bank(5)
        t3 = em.t.transpose(pb[5][:, 0:64], dtd, self.idf[0:64, 0:64]).inc("st")
        if need_toks:
            em.t.transpose(pb[5][:, 64:128], dtp[:, c * 128:(c + 1) * 128], self.idf[0:64, 0:64])
            t3 = em.t.transpose(pb[5][:, 128:192], cum, self.idf[0:64, 0:64]).inc("st")
        em.wait("v", t3)
        em.v.tensor_copy(dtb[:, 3200:3264], pb[5][:, 0:64])
        if need_toks:
            em.v.tensor_copy(dtb[:, 3264:3328], pb[5][:, 64:128])
            em.v.tensor_scalar_mul(dtb[:, 3328:3392], pb[5][:, 128:192], -1.0)
        lastent = None
        for entry in reversed(em.streams["v"]):
            if entry[0] not in ("wait_ge", "nop"):
                lastent = entry
                break
        tv = _Pending(em, lastent).inc("sv")
        em.wait("v", tv)
        self.bank_done(5, tv)
        self.use_bank(5)
        t2 = em.t.transpose(pb[5][0:1, 0:64], prm[0:64, 246:247], self.idf[0:64, 0:64]).inc("st")
        em.wait("s", t2)
        t3 = em.s.copy(prm[0:1, 352:416], pb[5][0:1, 0:64]).inc("ss")
        self.bank_done(5, t3)
        em.wait("t", t3)
        self.use_bank(5)
        t4 = em.t.matmul(pb[5][:, 0:64], self.onef[0:1, :], prm[0:1, 352:416],
                         start=True, stop=True).inc("st")
        em.wait("v", t4)
        tv = em.v.tensor_copy(prm[:, 160:224], pb[5][:, 0:64]).inc("sv")
        self.bank_done(5, tv)
        self.vf()

    def xB_transpose_chunk(self, xB_fm, x_dst, B_dst, c, banks):
        em, pb = self.em, self.pb
        for b8 in range(8):
            bi = banks[b8 % 2]
            self.use_bank(bi)
            em.t.transpose(pb[bi][:, 0:128],
                           xB_fm[:, b8 * HALF + c * 128:b8 * HALF + c * 128 + 128], self.idb[:])
            tb = em.t.transpose(pb[bi][:, 128:256],
                                xB_fm[:, (8 + b8) * HALF + c * 128:(8 + b8) * HALF + c * 128 + 128],
                                self.idb[:]).inc("st")
            em.wait("v", tb)
            em.v.tensor_copy(x_dst[:, b8 * 128:(b8 + 1) * 128], pb[bi][:, 0:128])
            tv = em.v.tensor_copy(B_dst[:, b8 * 128:(b8 + 1) * 128], pb[bi][:, 128:256]).inc("sv")
            self.bank_done(bi, tv)
        self.vf()

    def sum_chunk_front(self, fxB, fdt, ftok, fxdtd, c):
        nc, em, pb, prm = self.nc, self.em, self.pb, self.prm
        self.dt_chunk(fdt, c, need_ec=False, need_toks=False)
        if self.front_limit in ("dt", "dtA", "dtB", "dtC"):
            return
        x_tok = ftok[:, 0:T]
        B_tok = ftok[:, T:2 * T]
        em.wait("t", self.vtok())
        self.xB_transpose_chunk(fxB, x_tok, B_tok, c, (6, 7))
        x_ap = apx(x_tok, 0, [[2 * T, 128], [64, 16], [0, 4], [1, 64]])
        d_ap = apx(fdt, 3200, [[3456, 128], [4, 16], [1, 4], [0, 64]])
        tv = em.v.tensor_tensor(fxdtd[:].rearrange("p (g r q) -> p g r q", g=16, r=4),
                                x_ap, d_ap, ALU.mult).inc("sv")
        em.wait("t", tv)
        if self.front_limit == "tok":
            return
        for grp in range(8):
            bi = 2 + grp % 2
            self.use_bank(bi)
            last = None
            for hh in range(8):
                h = grp * 8 + hh
                xbh = h // 4
                lhs = B_tok[:, xbh * 64:(xbh + 1) * 64]
                em.t.ldweights(lhs)
                last = em.t.matmul(pb[bi][0:64, hh * 64:(hh + 1) * 64], lhs,
                                   fxdtd[:, h * 64:(h + 1) * 64], start=True, stop=True).inc("st")
            em.wait("v", last)
            self.h_update(grp, pb[bi])
            self.bank_done(bi, self.vtok())

    def h_update(self, grp, sbank):
        em = self.em
        Hsl = self.H[0:64, grp * 512:(grp + 1) * 512]
        Hscr = self.H[0:64, NH * DS:NH * DS + 512]
        dec_ap = apx(self.prm, 160 + grp * 8, [[512, 64], [1, 8], [0, 64]])
        em.v.tensor_tensor(Hscr.rearrange("p (a b) -> p a b", a=8),
                           Hsl.rearrange("p (a b) -> p a b", a=8), dec_ap, ALU.mult)
        self.vf()
        em.v.tensor_add(Hsl, Hscr, sbank[0:64, 0:512])
        self.vf()

    def l0_attn(self):
        nc, em = self.nc, self.em
        with nc.sbuf_tensor("nwb", [128, D_IN], BF16) as nwb, \
             nc.sbuf_tensor("a_xB", [128, 16 * HALF], BF16) as a_xB, \
             nc.sbuf_tensor("a_tail", [128, 144], BF16) as a_tail, \
             nc.sbuf_tensor("a_dt", [128, 3456], F32) as a_dt, \
             nc.sbuf_tensor("a_tok", [128, 2048], BF16) as a_tok, \
             nc.sbuf_tensor("a_wbuf", [128, 2 * 2048], BF16) as a_wbuf, \
             nc.sbuf_tensor("a_cst", [128, 2 * 516], BF16) as a_cst, \
             nc.sbuf_tensor("a_Cst", [128, 2 * HALF], BF16) as a_Cst, \
             nc.sbuf_tensor("a_diag", [128, 4 * 128], BF16) as a_diag, \
             nc.sbuf_tensor("a_scr", [128, 512], F32) as a_scr:
            self.nwb = nwb
            self.nwb_tok = em.dma("g", nwb[:], self.nw_bcast_d, "nwb")
            em.v.memset(a_tail[:], 0.0)
            self.vf()
            for hf in range(2):
                self.l0_half(hf, nwb, a_xB, a_tail, a_dt, a_tok, a_wbuf, a_cst,
                             a_Cst, a_diag, a_scr)
                em.barrier()
        self.tap("delta0", self.gdelta[:], self.vtok())

    def l0_half(self, hf, nwb, a_xB, a_tail, a_dt, a_tok, a_wbuf, a_cst, a_Cst,
                a_diag, a_scr):
        nc, em, pb, prm = self.nc, self.em, self.pb, self.prm
        h0 = hf * HALF
        with self.sb("xnh", [128, 16 * HALF], BF16) as xnh, \
             self.sb("zst", [128, HALF], BF16) as zst, \
             self.sb("ztr", [128, 2 * 512], BF16) as ztr:
            em.wait("q", self.xnout_tok)
            tl = em.dma("q", xnh[:].rearrange("p (f t) -> p f t", f=16),
                        apx(self.xn_dram, h0, [[T, 128], [128 * T, 16], [1, HALF]]), "xnh")
            em.wait("t", tl)
            zdma = [None, None]
            for m in range(32):
                wb = a_wbuf[:, (m % 2) * 2048:(m % 2) * 2048 + 2048]
                self.sguard("g", (id(a_wbuf), m % 2))
                wt = em.dma("g", wb.rearrange("p (f i) -> p f i", f=16),
                            apx(self.w_in, m * 128, [[NPROJ, 128], [128 * NPROJ, 16], [1, 128]]),
                            ("dwin", id(a_wbuf), m % 2))
                em.wait("t", wt)
                bi = m % 2
                self.use_bank(bi)
                last = None
                for f in range(16):
                    lhs = wb[:, f * 128:(f + 1) * 128]
                    em.t.ldweights(lhs)
                    last = em.t.matmul(pb[bi][:], lhs, xnh[:, f * HALF:(f + 1) * HALF],
                                       start=(f == 0), stop=(f == 15)).inc("st")
                self.sset((id(a_wbuf), m % 2), last)
                em.wait("s", last)
                self.sguard("s", "zst")
                tv = em.s.activation(zst[:], pb[bi][:], AF.Silu).inc("ss")
                self.bank_done(bi, tv)
                em.wait("t", tv)
                bi2 = 6 + m % 2
                self.use_bank(bi2)
                last = None
                for q in range(4):
                    last = em.t.transpose(pb[bi2][:, q * 128:(q + 1) * 128],
                                          zst[:, q * 128:(q + 1) * 128], self.idb[:]).inc("st")
                self.sset("zst", last)
                em.wait("v", last)
                em.wait("v", zdma[m % 2])
                tv = em.v.tensor_copy(ztr[:, (m % 2) * 512:(m % 2) * 512 + 512], pb[bi2][:, 0:512]).inc("sv")
                self.bank_done(bi2, tv)
                em.wait("g", tv)
                zdma[m % 2] = em.dma("g", apx(self.sz_dram, h0 * D_IN + m * 128,
                                              [[D_IN, 128], [128 * D_IN, 4], [1, 128]]),
                                     ztr[:, (m % 2) * 512:(m % 2) * 512 + 512].rearrange(
                                         "p (q i) -> p q i", q=4), ("szo", h0, m % 2))
            self.winx_pass(xnh, a_xB, a_Cst, a_dt, a_wbuf, a_cst, a_diag, a_tail,
                           m_list=list(range(32, 80)), dt_too=True, front=False, h0=h0,
                           pre_src=(self.xpre if hf == 0 else None))
        self.a_xB_ref = a_xB
        em.wait("t", self.stok())
        stageA_sh = (em.tok(("szo", h0, 0)), em.tok(("szo", h0, 1)),
                     em.tok(("c64o", h0, 0)), em.tok(("c64o", h0, 1)),
                     em.tok(("b64o", h0)))
        for _e in ("q", "g", "v", "s", "t"):
            for _tk in stageA_sh:
                em.wait(_e, _tk)
        with self.sb("szc", [128, D_IN], BF16) as szc, \
             self.sb("C64", [64, 2048], BF16) as C64, \
             self.sb("B64", [64, 16 * 128], BF16) as B64, \
             self.sb("Hbf", [64, NH * DS], BF16) as Hbf, \
             self.sb("xdt", [128, 512], BF16) as xdt, \
             self.sb("xdtd", [128, 512], BF16) as xdtd, \
             self.sb("Rq", [128, 2048], F32) as Rq, \
             self.sb("mexp", [128, 2048], BF16) as mexp, \
             self.sb("eCb", [64, 2048], BF16) as eCb, \
             self.sb("Ceq", [64, 2048], BF16) as Ceq, \
             self.sb("GM", [128, 1024], BF16) as GM, \
             self.sb("yz", [128, 1024], F32) as yz, \
             self.sb("sqs", [128, 1024], F32) as sqs, \
             self.sb("ntok", [128, 1024], BF16) as ntok, \
             self.sb("nfm", [128, 32 * 256], BF16) as nfm:
            for c in range(4):
                self.l0_chunk(hf, c, stageA_sh, a_dt, a_tok, szc, C64, B64, Hbf,
                              xdt, xdtd, Rq, mexp, eCb, Ceq, GM, yz, sqs, ntok, nfm)
                if c % 2 == 1:
                    self.l0_wout(hf, c // 2, nfm, a_scr)

    def l0_chunk(self, hf, c, stageA_sh, a_dt, a_tok, szc, C64, B64, Hbf,
                 xdt, xdtd, Rq, mexp, eCb, Ceq, GM, yz, sqs, ntok, nfm):
        nc, em, pb, prm = self.nc, self.em, self.pb, self.prm
        h0 = hf * HALF
        em.wait("g", em.tok("rq") if "rq" in em.sems else None)
        em.wait("g", em.tok("ecb") if "ecb" in em.sems else None)
        self.dt_chunk(a_dt, c, need_ec=True, need_toks=True)
        tv = self.vtok()
        ts = self.stok()
        em.wait("g", ts)
        em.dma("g", apx(self.cum_dram, 0, [[128, 64], [1, 128]]), a_dt[0:64, 2688:2816], "cumo")
        em.dma("g", apx(self.ec_dram, 0, [[128, 64], [1, 128]]), a_dt[0:64, 2944:3072], "eco")
        cumec_tok = (em.tok("cumo"), em.tok("eco"))
        for tk in stageA_sh:
            em.wait("q", tk)
        em.wait("q", self.vtok())
        em.wait("q", self.ttok())
        tsz = em.dma("q", szc[:], apx(self.sz_dram, (h0 + c * 128) * D_IN,
                                      [[D_IN, 128], [1, D_IN]]), "szc")
        tb64 = em.dma("q", B64[:], apx(self.b64_dram, c * 128,
                                       [[16 * HALF, 64], [HALF, 16], [1, 128]]), "b64l")
        em.wait("v", self.ttok())
        em.v.tensor_copy(Hbf[:], self.H[0:64, 0:NH * DS])
        self.vf()
        em.wait("t", self.vtok())
        self.xB_transpose_chunk(self.a_xB_ref, a_tok[:, 0:T], a_tok[:, T:2 * T], c, (6, 7))
        em.wait("t", tsz)
        em.wait("t", tb64)
        for qq in range(4):
            em.wait("q", self.ttok())
            em.wait("q", self.vtok())
            tc64 = em.dma("q", C64[:], apx(self.c64_dram, qq * 16 * HALF + c * 128,
                                           [[NH * HALF, 64], [HALF, 16], [1, 128]]), "c64l")
            em.wait("t", tc64)
            em.wait("v", tc64)
            em.wait("q", self.stok())
            em.wait("q", cumec_tok[0])
            em.wait("q", cumec_tok[1])
            em.wait("g", cumec_tok[1])
            t1 = em.dma("q", Rq[:].rearrange("p (h i) -> p h i", h=16),
                        apx(self.cum_dram, qq * 16 * 128, [[0, 128], [128, 16], [1, 128]]), "rq")
            em.wait("g", t1)
            t2 = em.dma("g", Rq[:].rearrange("p (h i) -> p h i", h=16),
                        apx(self.triu_neg_d, 0, [[128, 128], [0, 16], [1, 128]]), "rqa",
                        accum_op=ALU.add)
            em.wait("v", t2)
            em.wait("v", self.stok())
            ncol_ap = apx(a_dt, 3328 + qq * 16, [[3456, 128], [1, 16], [0, 128]])
            tv = em.v.tensor_tensor(Rq[:].rearrange("p (h i) -> p h i", h=16),
                                    Rq[:].rearrange("p (h i) -> p h i", h=16),
                                    ncol_ap, ALU.add).inc("sv")
            em.wait("s", tv)
            em.wait("s", self.vtok())
            em.s.activation(mexp[:, 0:1024], Rq[:, 0:1024], AF.Exp)
            ts = em.s.activation(mexp[:, 1024:2048], Rq[:, 1024:2048], AF.Exp).inc("ss")
            t3 = em.dma("g", eCb[:].rearrange("p (h i) -> p h i", h=16),
                        apx(self.ec_dram, qq * 16 * 128, [[0, 64], [128, 16], [1, 128]]), "ecb")
            em.wait("v", t3)
            em.wait("v", self.ttok())
            tce = em.v.tensor_mul(Ceq[:], C64[:], eCb[:]).inc("sv")
            em.wait("t", tce)
            em.wait("v", ts)
            for g2 in range(2):
                grp = qq * 2 + g2
                ybank = 0 + grp % 2
                sbank = 4 + grp % 2
                em.wait("v", self.ttok())
                x_ap = apx(a_tok, 2 * grp * 64, [[2048, 128], [64, 2], [0, 4], [1, 64]])
                dtp_ap = apx(a_dt, 3264 + grp * 8, [[3456, 128], [4, 2], [1, 4], [0, 64]])
                dtd_ap = apx(a_dt, 3200 + grp * 8, [[3456, 128], [4, 2], [1, 4], [0, 64]])
                em.v.tensor_tensor(xdt[:].rearrange("p (g r q) -> p g r q", g=2, r=4),
                                   x_ap, dtp_ap, ALU.mult)
                tv = em.v.tensor_tensor(xdtd[:].rearrange("p (g r q) -> p g r q", g=2, r=4),
                                        x_ap, dtd_ap, ALU.mult).inc("sv")
                em.wait("t", tv)
                self.use_bank(ybank)
                self.use_bank(sbank)
                ylast = None
                for half4 in range(2):
                    gtbank = 2 + half4
                    self.use_bank(gtbank)
                    gtlast = None
                    for k in range(4):
                        h = grp * 8 + half4 * 4 + k
                        xbh = h // 4
                        lhs = B64[:, xbh * 128:(xbh + 1) * 128]
                        em.t.ldweights(lhs)
                        lh = h - qq * 16
                        gtlast = em.t.matmul(pb[gtbank][:, k * 128:(k + 1) * 128], lhs,
                                             C64[:, lh * 128:(lh + 1) * 128],
                                             start=True, stop=True).inc("st")
                    em.wait("v", gtlast)
                    self.sguard("v", ("GM", half4))
                    msl = mexp[:, (g2 * 8 + half4 * 4) * 64 * 2:(g2 * 8 + half4 * 4 + 4) * 64 * 2]
                    tv = em.v.tensor_mul(GM[:, half4 * 512:(half4 + 1) * 512],
                                         pb[gtbank][:], msl).inc("sv")
                    self.bank_done(gtbank, tv)
                    em.wait("t", tv)
                    for k in range(4):
                        h = grp * 8 + half4 * 4 + k
                        hq = g2 * 8 + half4 * 4 + k
                        lhs1 = GM[:, half4 * 512 + k * 128:half4 * 512 + (k + 1) * 128]
                        em.t.ldweights(lhs1)
                        em.t.matmul(pb[ybank][:, (half4 * 4 + k) * 64:(half4 * 4 + k + 1) * 64],
                                    lhs1, xdt[:, (half4 * 4 + k) * 64:(half4 * 4 + k + 1) * 64],
                                    start=True, stop=False)
                        lhs2 = Ceq[:, hq * 128:(hq + 1) * 128]
                        em.t.ldweights(lhs2)
                        ylast = em.t.matmul(pb[ybank][:, (half4 * 4 + k) * 64:(half4 * 4 + k + 1) * 64],
                                            lhs2, Hbf[:, h * 64:(h + 1) * 64],
                                            start=False, stop=True).inc("st")
                    self.sset(("GM", half4), ylast)
                slast = None
                for hh in range(8):
                    h = grp * 8 + hh
                    xbh = h // 4
                    lhs = a_tok[:, T + xbh * 64:T + (xbh + 1) * 64]
                    em.t.ldweights(lhs)
                    slast = em.t.matmul(pb[sbank][0:64, hh * 64:(hh + 1) * 64], lhs,
                                        xdtd[:, hh * 64:(hh + 1) * 64], start=True, stop=True).inc("st")
                em.wait("v", ylast)
                xg_ap = apx(a_tok, 2 * grp * 64, [[2048, 128], [64, 2], [0, 4], [1, 64]])
                D_ap = apx(prm, 96 + grp * 8, [[512, 128], [4, 2], [1, 4], [0, 64]])
                xsc = sqs[:, (grp % 2) * 512:(grp % 2) * 512 + 512]
                em.v.tensor_tensor(xsc.rearrange("p (g r q) -> p g r q", g=2, r=4),
                                   xg_ap, D_ap, ALU.mult)
                self.vf()
                em.v.tensor_add(xsc, xsc, pb[ybank][:])
                tyv = self.vtok()
                self.bank_done(ybank, tyv)
                self.vf()
                yhalf = yz[:, (grp % 2) * 512:(grp % 2) * 512 + 512]
                em.v.tensor_mul(yhalf, xsc, szc[:, grp * 512:(grp + 1) * 512])
                self.vf()
                em.wait("v", slast)
                self.h_update(grp, pb[sbank])
                self.bank_done(sbank, self.vtok())
                if grp % 2 == 1:
                    self.l0_rms(hf, c, grp // 2, yz, sqs, ntok, nfm)

    def l0_rms(self, hf, c, qq, yz, sqs, ntok, nfm):
        nc, em, pb, prm = self.nc, self.em, self.pb, self.prm
        em.wait("s", self.vtok())
        em.wait("v", self.nwb_tok)
        ts = em.s.activation(sqs[:], yz[:], AF.Square).inc("ss")
        em.wait("v", ts)
        rms = prm[:, 248:264]
        tv = em.v.tensor_reduce(rms.rearrange("p (h o) -> p h o", h=16),
                                sqs[:].rearrange("p (h q) -> p h q", h=16),
                                AX.X, ALU.add).inc("sv")
        em.wait("s", tv)
        em.s.activation(prm[:, 280:296], rms, AF.Ln, bias=self.prm[:, 504:505], scale=1.0 / DS)
        self.sf()
        ts = em.s.activation(rms, prm[:, 280:296], AF.Exp, scale=-0.5).inc("ss")
        em.wait("v", ts)
        rms_ap = apx(prm, 248, [[512, 128], [1, 16], [0, 64]])
        em.v.tensor_tensor(sqs[:].rearrange("p (h q) -> p h q", h=16),
                           yz[:].rearrange("p (h q) -> p h q", h=16), rms_ap, ALU.mult)
        self.vf()
        self.sguard("v", "ntok")
        tv = em.v.tensor_mul(ntok[:], sqs[:], self.nwb[:, qq * 1024:(qq + 1) * 1024]).inc("sv")
        em.wait("t", tv)
        for fq in range(2):
            bi = 6 + fq % 2
            self.use_bank(bi)
            last = None
            for k in range(4):
                fb = fq * 4 + k
                last = em.t.transpose(pb[bi][:, k * 128:(k + 1) * 128],
                                      ntok[:, fb * 128:fb * 128 + 128], self.idb[:]).inc("st")
            em.wait("v", last)
            out_ap = apx(nfm, (qq * 8 + fq * 4) * 256 + (c % 2) * 128,
                         [[32 * 256, 128], [256, 4], [1, 128]])
            tv = em.v.tensor_copy(out_ap, pb[bi][:, 0:512].rearrange("p (a b) -> p a b", a=4)).inc("sv")
            self.bank_done(bi, tv)
            self.sset("ntok", self.ttok())

    def l0_wout(self, hf, pair, nfm, a_scr):
        nc, em, pb = self.nc, self.em, self.pb
        em.wait("t", self.vtok())
        tok0 = hf * HALF + pair * 256
        with self.sb("wob", [128, 2 * 2048], BF16) as wob:
            for o in range(16):
                bi = 4 + o % 2
                self.use_bank(bi)
                last = None
                for istrip in range(2):
                    wb = wob[:, istrip * 2048:(istrip + 1) * 2048]
                    self.sguard("g", (id(wob), istrip))
                    wt = em.dma("g", wb.rearrange("p (f i) -> p f i", f=16),
                                apx(self.w_out, (istrip * 16 * 128) * D + o * 128,
                                    [[D, 128], [128 * D, 16], [1, 128]]), ("dwob", istrip))
                    em.wait("t", wt)
                    for f in range(16):
                        ii = istrip * 16 + f
                        lhs = wb[:, f * 128:(f + 1) * 128]
                        em.t.ldweights(lhs)
                        last = em.t.matmul(pb[bi][:, 0:256], lhs, nfm[:, ii * 256:(ii + 1) * 256],
                                           start=(ii == 0), stop=(ii == 31)).inc("st")
                    self.sset((id(wob), istrip), last)
                em.wait("v", last)
                gofs = o * T + tok0
                gsl = self.gdelta[:, gofs:gofs + 256]
                em.v.tensor_mul(a_scr[:, 0:256], pb[bi][:, 0:256], gsl)
                tvv = self.vtok()
                self.bank_done(bi, tvv)
                self.vf()
                em.v.tensor_copy(gsl, a_scr[:, 0:256])
                self.vf()

    def l1(self):
        nc, em, pb, prm = self.nc, self.em, self.pb, self.prm
        for f in range(16):
            em.dma("q", prm[:, 224 + f:225 + f], self.l1_D[f * 128:(f + 1) * 128, None], "l1d")
        self.l1d_tok = em.tok("l1d")
        with nc.sbuf_tensor("x1n", [128, 16 * T], BF16) as x1n:
            with nc.sbuf_tensor("x1f", [128, 16 * T], F32) as x1f, \
                 nc.sbuf_tensor("l1scr", [128, 2 * D], F32) as l1scr:
                for c in range(8):
                    buf = l1scr[:, (c % 2) * D:(c % 2) * D + D]
                    self.sguard("q", ("l1x", c % 2))
                    tl = em.dma("q", buf, self.x_in[c * 128:(c + 1) * 128, :], ("dl1x", c % 2))
                    em.wait("t", tl)
                    for fq in range(4):
                        bi = fq % 2
                        self.use_bank(bi)
                        last = None
                        for k in range(4):
                            f = fq * 4 + k
                            last = em.t.transpose(pb[bi][:, k * 128:(k + 1) * 128],
                                                  buf[:, f * 128:f * 128 + 128], self.idf[:]).inc("st")
                        em.wait("v", last)
                        out_ap = self.fm_out_ap(x1f, fq, c * 128, T)
                        g_ap = self.fm_out_ap(self.gdelta, fq, c * 128, T)
                        tv = em.v.tensor_add(out_ap, pb[bi][:].rearrange("p (a b) -> p a b", a=4),
                                             g_ap).inc("sv")
                        self.bank_done(bi, tv)
                        if fq == 3:
                            self.sset(("l1x", c % 2), tv)
                em.wait("t", self.vtok())
                em.wait("s", self.vtok())
                for tt_i in range(2):
                    self.use_bank(2)
                    last = None
                    for f in range(16):
                        last = em.t.matmul(pb[2][0:1, 0:512], self.onef[:, 0:1],
                                           x1f[:, f * T + tt_i * 512:f * T + tt_i * 512 + 512],
                                           start=(f == 0), stop=(f == 15)).inc("st")
                    em.wait("s", last)
                    ts = em.s.mul(l1scr[0:1, tt_i * 512:tt_i * 512 + 512],
                                  pb[2][0:1, 0:512], 1.0 / D).inc("ss")
                    self.bank_done(2, ts)
                for tt_i in range(2):
                    self.use_bank(3)
                    last = None
                    for f in range(16):
                        tq = em.s.activation(l1scr[:, D:D + 512],
                                             x1f[:, f * T + tt_i * 512:f * T + tt_i * 512 + 512],
                                             AF.Square).inc("ss")
                        em.wait("t", tq)
                        last = em.t.matmul(pb[3][0:1, 0:512], self.onef[:, 0:1],
                                           l1scr[:, D:D + 512], start=(f == 0),
                                           stop=(f == 15)).inc("st")
                        em.wait("s", last)
                    ts = em.s.mul(l1scr[0:1, 1024 + tt_i * 512:1024 + tt_i * 512 + 512],
                                  pb[3][0:1, 0:512], 1.0 / D).inc("ss")
                    self.bank_done(3, ts)
                em.wait("v", self.stok())
                em.v.tensor_mul(l1scr[0:1, D:D + T], l1scr[0:1, 0:T], l1scr[0:1, 0:T])
                self.vf()
                em.v.tensor_sub(l1scr[0:1, 1024:1024 + T], l1scr[0:1, 1024:1024 + T],
                                l1scr[0:1, D:D + T])
                tv = self.vtok()
                self.vf()
                em.wait("s", tv)
                em.s.activation(l1scr[0:1, D:D + T], l1scr[0:1, 1024:1024 + T], AF.Ln, bias=self.prm[0:1, 504:505])
                self.sf()
                ts = em.s.activation(l1scr[0:1, 1024:1024 + T], l1scr[0:1, D:D + T],
                                     AF.Exp, scale=-0.5).inc("ss")
                em.wait("t", ts)
                tmm = None
                for half in range(2):
                    self.use_bank(2 + half)
                    em.t.matmul(pb[2 + half][:, 0:512], self.onef[0:1, :],
                                l1scr[0:1, half * 512:half * 512 + 512],
                                start=True, stop=True).inc("st")
                    self.use_bank(4 + half)
                    tmm = em.t.matmul(pb[4 + half][:, 0:512], self.onef[0:1, :],
                                      l1scr[0:1, 1024 + half * 512:1024 + half * 512 + 512],
                                      start=True, stop=True).inc("st")
                em.wait("v", tmm)
                for half in range(2):
                    em.v.tensor_copy(l1scr[:, D + half * 512:D + half * 512 + 512],
                                     pb[2 + half][:, 0:512])
                    em.v.tensor_copy(l1scr[:, D + T + half * 512:D + T + half * 512 + 512],
                                     pb[4 + half][:, 0:512])
                lastent = None
                for entry in reversed(em.streams["v"]):
                    if entry[0] not in ("wait_ge", "nop"):
                        lastent = entry
                        break
                tv = _Pending(em, lastent).inc("sv")
                em.wait("v", tv)
                for bi in (2, 3, 4, 5):
                    self.bank_done(bi, tv)
                meanb = l1scr[:, D:D + T]
                istdb = l1scr[:, D + T:D + 2 * T]
                for f in range(16):
                    em.v.tensor_sub(x1f[:, f * T:(f + 1) * T], x1f[:, f * T:(f + 1) * T], meanb)
                    self.vf()
                    em.v.tensor_mul(x1n[:, f * T:(f + 1) * T], x1f[:, f * T:(f + 1) * T], istdb)
                    self.vf()
                self.tap("x1n", x1n[:], self.vtok())
            with nc.sbuf_tensor("cwfm", [32, T], BF16) as cwfm, \
                 nc.sbuf_tensor("cwsfm", [16, T], BF16) as cwsfm:
                self.routing(x1n, cwfm, cwsfm)
                em.barrier()
                with nc.sbuf_tensor("moe", [128, 16 * T], BF16) as moe:
                    with nc.sbuf_tensor("oh32sb", [32, 32 * 128], BF16) as oh32, \
                         nc.sbuf_tensor("oh16sb", [16, 16 * 128], BF16) as oh16, \
                         nc.sbuf_tensor("a_e", [128, 6 * T], BF16) as a_e, \
                         nc.sbuf_tensor("a_s", [128, T], BF16) as a_s, \
                         nc.sbuf_tensor("wgu", [128, 2 * 4096], BF16) as wgu, \
                         nc.sbuf_tensor("wdb", [128, 2 * 3072], BF16) as wdb, \
                         nc.sbuf_tensor("wsm", [128, 2048 + 2048], BF16) as wsm, \
                         nc.sbuf_tensor("cwb", [128, T], BF16) as cwb, \
                         nc.sbuf_tensor("cwsb", [128, T], BF16) as cwsb, \
                         nc.sbuf_tensor("mscr", [128, 2 * 512], BF16) as mscr:
                        em.dma("q", oh32[:], self.oh32_d, "oh32")
                        tl = em.dma("q", oh16[:], self.oh16s_d, "oh32")
                        em.wait("t", tl)
                        for e in range(NEXP):
                            self.moe_expert(e, x1n, cwfm, cwsfm, moe, a_e, a_s, wgu,
                                            wdb, wsm, cwb, cwsb, mscr, oh32, oh16)
                        self.tap("moe", moe[:], self.vtok())
                        em.wait("s", self.vtok())
                        for o in range(16):
                            for tt_i in range(2):
                                sl = moe[:, o * T + tt_i * 512:o * T + tt_i * 512 + 512]
                                em.s.activation(mscr[:, 0:512], sl, AF.Silu)
                                self.sf()
                                em.s.copy(sl, mscr[:, 0:512])
                                self.sf()
                        ts = self.stok()
                        em.wait("t", ts)
                        em.wait("v", ts)
                    self.l1_attn(x1n, moe)

    def routing(self, x1n, cwfm, cwsfm):
        nc, em, pb, prm = self.nc, self.em, self.pb, self.prm
        with nc.sbuf_tensor("gw", [32, D], BF16) as gw, \
             nc.sbuf_tensor("gwT", [128, 16 * 32], BF16) as gwT, \
             nc.sbuf_tensor("lgt", [32, T], F32) as lgt, \
             nc.sbuf_tensor("rs", [128, 8 * 32], F32) as rs, \
             nc.sbuf_tensor("rw", [128, 14 * 32], F32) as rw:
            em.wait("g", self.vtok())
            em.wait("g", self.stok())
            tl = em.dma("g", gw[:], self.gate_w, "gw")
            em.wait("t", tl)
            for f in range(16):
                bi = 6 + f % 2
                self.use_bank(bi)
                tt = em.t.transpose(pb[bi][:, 0:32], gw[:, f * 128:(f + 1) * 128],
                                    self.idb[0:32, 0:32]).inc("st")
                em.wait("v", tt)
                tv = em.v.tensor_copy(gwT[:, f * 32:(f + 1) * 32], pb[bi][:, 0:32]).inc("sv")
                self.bank_done(bi, tv)
            em.wait("t", self.vtok())
            for tt_i in range(2):
                self.use_bank(2)
                last = None
                for f in range(16):
                    lhs = gwT[:, f * 32:(f + 1) * 32]
                    em.t.ldweights(lhs)
                    last = em.t.matmul(pb[2][0:32, 0:512], lhs,
                                       x1n[:, f * T + tt_i * 512:f * T + tt_i * 512 + 512],
                                       start=(f == 0), stop=(f == 15)).inc("st")
                em.wait("v", last)
                tv = em.v.tensor_copy(lgt[:, tt_i * 512:tt_i * 512 + 512], pb[2][0:32, 0:512]).inc("sv")
                self.bank_done(2, tv)
            em.wait("t", self.vtok())
            for c in range(8):
                self.use_bank(3)
                tt = em.t.transpose(pb[3][:, 0:32], lgt[:, c * 128:(c + 1) * 128],
                                    self.idf[0:32, 0:32]).inc("st")
                em.wait("v", tt)
                self.sguard("v", "rwk")
                lt = rs[:, 0:32]
                cur = rs[:, 32:64]
                msk = rs[:, 64:96]
                ex = rs[:, 96:128]
                cwv = rs[:, 128:160]
                mx = rs[:, 160:165]
                tv = em.v.tensor_copy(lt, pb[3][:, 0:32]).inc("sv")
                self.bank_done(3, tv)
                self.vf()
                em.v.tensor_copy(cur, lt)
                self.vf()
                for j in range(4):
                    em.v.reduce_max(mx[:, j:j + 1], cur, axis=AX.X)
                    self.vf()
                    if j < 3:
                        em.v.tensor_scalar(msk, cur, mx[:, j:j + 1], None, ALU.is_ge)
                        self.vf()
                        em.v.scalar_tensor_tensor(cur, msk, -1e30, cur, ALU.mult, ALU.add)
                        self.vf()
                em.v.tensor_scalar_mul(mx[:, 4:5], mx[:, 0:1], -1.0)
                tv = self.vtok()
                self.vf()
                em.wait("s", tv)
                ts = em.s.activation(ex, lt, AF.Exp, bias=mx[:, 4:5]).inc("ss")
                em.wait("v", ts)
                em.v.tensor_scalar(msk, lt, mx[:, 3:4], None, ALU.is_ge)
                self.vf()
                em.v.tensor_mul(cwv, ex, msk)
                self.vf()
                em.v.reduce_sum(mx[:, 0:1], cwv, axis=AX.X)
                self.vf()
                em.v.reciprocal(mx[:, 1:2], mx[:, 0:1])
                self.vf()
                em.v.tensor_scalar(rw[:, 0:32], cwv, mx[:, 1:2], None, ALU.mult)
                self.vf()
                em.v.tensor_tensor(rw[:, 32:48], apx(rw, 0, [[14 * 32, 128], [2, 16]]),
                                   apx(rw, 1, [[14 * 32, 128], [2, 16]]), ALU.add)
                tv = self.vtok()
                self.vf()
                em.wait("t", tv)
                self.use_bank(3)
                ta = em.t.transpose(pb[3][0:32, 0:128], rw[:, 0:32], self.idf[:]).inc("st")
                em.wait("v", ta)
                tv = em.v.tensor_copy(cwfm[:, c * 128:(c + 1) * 128], pb[3][0:32, 0:128]).inc("sv")
                self.bank_done(3, tv)
                self.use_bank(3)
                ta = em.t.transpose(pb[3][0:16, 0:128], rw[:, 32:48], self.idf[:]).inc("st")
                em.wait("v", ta)
                tv = em.v.tensor_copy(cwsfm[:, c * 128:(c + 1) * 128], pb[3][0:16, 0:128]).inc("sv")
                self.bank_done(3, tv)
                self.sset("rwk", self.ttok())
            self.tap("cw", cwfm[:], self.vtok())

    def moe_expert(self, e, x1n, cwfm, cwsfm, moe, a_e, a_s, wgu, wdb, wsm, cwb,
                   cwsb, mscr, oh32, oh16):
        nc, em, pb = self.nc, self.em, self.pb
        small = (e % 2 == 0)
        k = e // 2
        em.wait("v", self.ttok())
        em.wait("t", self.vtok())
        for tt_i in range(2):
            self.use_bank(tt_i)
            lhs = oh32[:, e * 128:(e + 1) * 128]
            em.t.ldweights(lhs)
            tm = em.t.matmul(pb[tt_i][:], lhs, cwfm[:, tt_i * 512:tt_i * 512 + 512],
                             start=True, stop=True).inc("st")
            em.wait("v", tm)
            tv = em.v.tensor_copy(cwb[:, tt_i * 512:tt_i * 512 + 512], pb[tt_i][:]).inc("sv")
            self.bank_done(tt_i, tv)
            if small:
                self.use_bank(2 + tt_i)
                lhs = oh16[:, k * 128:(k + 1) * 128]
                em.t.ldweights(lhs)
                tm = em.t.matmul(pb[2 + tt_i][:], lhs, cwsfm[:, tt_i * 512:tt_i * 512 + 512],
                                 start=True, stop=True).inc("st")
                em.wait("v", tm)
                tv = em.v.tensor_copy(cwsb[:, tt_i * 512:tt_i * 512 + 512], pb[2 + tt_i][:]).inc("sv")
                self.bank_done(2 + tt_i, tv)
        em.wait("t", self.vtok())
        for third in range(3):
            for proj in range(2):
                wsrc = self.moe_wg if proj == 0 else self.moe_wu
                wb = wgu[:, (proj % 2) * 4096:(proj % 2) * 4096 + 4096]
                self.sguard("g", (id(wgu), proj % 2))
                em.dma("g", wb.rearrange("p (f i) -> p f i", f=16),
                       apx(wsrc, e * D * MOE_I + third * 256,
                           [[MOE_I, 128], [128 * MOE_I, 16], [1, 256]]), ("dwgu", proj % 2))
            wt = (em.tok(("dwgu", 0)), em.tok(("dwgu", 1)))
            em.wait("t", wt[0])
            em.wait("t", wt[1])
            for ic2 in range(2):
                ii = third * 2 + ic2
                for proj in range(2):
                    wb = wgu[:, (proj % 2) * 4096:(proj % 2) * 4096 + 4096]
                    for tt_i in range(2):
                        bi = proj * 2 + tt_i
                        self.use_bank(bi)
                        last = None
                        for f in range(16):
                            lhs = wb[:, f * 256 + ic2 * 128:f * 256 + ic2 * 128 + 128]
                            em.t.ldweights(lhs)
                            last = em.t.matmul(pb[bi][:], lhs,
                                               x1n[:, f * T + tt_i * 512:f * T + tt_i * 512 + 512],
                                               start=(f == 0), stop=(f == 15)).inc("st")
                        self.sset((id(wgu), proj % 2), last)
                for tt_i in range(2):
                    bg = 0 + tt_i
                    bu = 2 + tt_i
                    tmlast = self.ttok()
                    em.wait("s", tmlast)
                    self.sguard("s", (id(mscr), tt_i % 2))
                    ts = em.s.activation(mscr[:, (tt_i % 2) * 512:(tt_i % 2) * 512 + 512],
                                         pb[bg][:], AF.Silu).inc("ss")
                    em.wait("v", ts)
                    em.v.tensor_mul(mscr[:, (tt_i % 2) * 512:(tt_i % 2) * 512 + 512],
                                    mscr[:, (tt_i % 2) * 512:(tt_i % 2) * 512 + 512], pb[bu][:])
                    self.vf()
                    tv = em.v.tensor_mul(a_e[:, ii * T + tt_i * 512:ii * T + tt_i * 512 + 512],
                                         mscr[:, (tt_i % 2) * 512:(tt_i % 2) * 512 + 512],
                                         cwb[:, tt_i * 512:tt_i * 512 + 512]).inc("sv")
                    self.sset((id(mscr), tt_i % 2), tv)
                    self.bank_done(bg, tv)
                    self.bank_done(bu, tv)
        if small:
            for proj in range(2):
                wsrc = self.cwg if proj == 0 else self.cwu
                wb = wsm[:, 0:2048]
                em.wait("g", self.ttok())
                wt = em.dma("g", wb.rearrange("p (f i) -> p f i", f=16),
                            apx(wsrc, k * D * CHI, [[CHI, 128], [128 * CHI, 16], [1, 128]]), "dw")
                em.wait("t", wt)
                for tt_i in range(2):
                    bi = proj * 2 + tt_i
                    self.use_bank(bi)
                    last = None
                    for f in range(16):
                        lhs = wb[:, f * 128:(f + 1) * 128]
                        em.t.ldweights(lhs)
                        last = em.t.matmul(pb[bi][:], lhs,
                                           x1n[:, f * T + tt_i * 512:f * T + tt_i * 512 + 512],
                                           start=(f == 0), stop=(f == 15)).inc("st")
                    if proj == 1:
                        em.wait("s", last)
                        ts = em.s.activation(mscr[:, tt_i * 512:tt_i * 512 + 512],
                                             pb[0 + tt_i][:], AF.Silu).inc("ss")
                        em.wait("v", ts)
                        em.v.tensor_mul(mscr[:, tt_i * 512:tt_i * 512 + 512],
                                        mscr[:, tt_i * 512:tt_i * 512 + 512], pb[2 + tt_i][:])
                        self.vf()
                        tv = em.v.tensor_mul(a_s[:, tt_i * 512:tt_i * 512 + 512],
                                             mscr[:, tt_i * 512:tt_i * 512 + 512],
                                             cwsb[:, tt_i * 512:tt_i * 512 + 512]).inc("sv")
                        self.bank_done(0 + tt_i, tv)
                        self.bank_done(2 + tt_i, tv)
            em.wait("g", self.ttok())
            wt = em.dma("g", wsm[:, 2048:4096].rearrange("p (o j) -> p o j", o=16),
                        apx(self.cwd, k * CHI * D, [[D, 128], [128, 16], [1, 128]]), "wsm2")
            self.small_wd_tok = wt
        em.wait("t", self.vtok())
        for oq in range(4):
            wb = wdb[:, (oq % 2) * 3072:(oq % 2) * 3072 + 3072]
            self.sguard("g", (id(wdb), oq % 2))
            wt = em.dma("g", wb.rearrange("p (i o) -> p i o", i=6),
                        apx(self.moe_wd, e * MOE_I * D + oq * 512,
                            [[D, 128], [128 * D, 6], [1, 512]]), ("dwd", oq % 2))
            em.wait("t", wt)
            if small:
                em.wait("t", self.small_wd_tok)
            for o4 in range(4):
                o = oq * 4 + o4
                for tt_i in range(2):
                    bi = 4 + (o * 2 + tt_i) % 2
                    self.use_bank(bi)
                    last = None
                    for ii in range(6):
                        lhs = wb[:, ii * 512 + o4 * 128:ii * 512 + o4 * 128 + 128]
                        em.t.ldweights(lhs)
                        last = em.t.matmul(pb[bi][:], lhs,
                                           a_e[:, ii * T + tt_i * 512:ii * T + tt_i * 512 + 512],
                                           start=(ii == 0), stop=(not small and ii == 5)).inc("st")
                    if small:
                        lhs = wsm[:, 2048 + o * 128:2048 + (o + 1) * 128]
                        em.t.ldweights(lhs)
                        last = em.t.matmul(pb[bi][:], lhs,
                                           a_s[:, tt_i * 512:tt_i * 512 + 512],
                                           start=False, stop=True).inc("st")
                    em.wait("v", last)
                    self.sset((id(wdb), oq % 2), last)
                    msl = moe[:, o * T + tt_i * 512:o * T + tt_i * 512 + 512]
                    if e == 0:
                        tv = em.v.tensor_copy(msl, pb[bi][:]).inc("sv")
                    else:
                        tv = em.v.tensor_add(msl, msl, pb[bi][:]).inc("sv")
                    self.bank_done(bi, tv)
                    self.vf()

    def l1_attn(self, x1n, g1):
        nc, em, pb, prm = self.nc, self.em, self.pb, self.prm
        with nc.sbuf_tensor("state", [128, 8 * T], BF16) as state, \
             nc.sbuf_tensor("bnat", [128, 2 * D], BF16) as bnat, \
             nc.sbuf_tensor("btn", [128, 2 * 2048], BF16) as btn, \
             nc.sbuf_tensor("lascr", [128, 3 * 512], F32) as lascr, \
             nc.sbuf_tensor("lbscr", [128, 512], BF16) as lbscr:
            em.wait("g", self.ttok())
            em.wait("g", self.vtok())
            em.wait("g", self.stok())
            for nc8 in range(8):
                bb = bnat[:, (nc8 % 2) * D:(nc8 % 2) * D + D]
                self.sguard("g", (id(bnat), nc8 % 2))
                tl = em.dma("g", bb, self.l1_B[nc8 * 128:(nc8 + 1) * 128, :], ("dbn", nc8 % 2))
                em.wait("t", tl)
                bt = btn[:, (nc8 % 2) * 2048:(nc8 % 2) * 2048 + 2048]
                self.tr16(bb, [bt[:, fq * 512:(fq + 1) * 512] for fq in range(4)],
                          self.idb[:], (6, 7))
                self.sset((id(bnat), nc8 % 2), self.ttok())
                em.wait("t", self.vtok())
                for tt_i in range(2):
                    bi = 2 + tt_i
                    self.use_bank(bi)
                    last = None
                    for f in range(16):
                        lhs = bt[:, f * 128:(f + 1) * 128]
                        em.t.ldweights(lhs)
                        last = em.t.matmul(pb[bi][:], lhs,
                                           x1n[:, f * T + tt_i * 512:f * T + tt_i * 512 + 512],
                                           start=(f == 0), stop=(f == 15)).inc("st")
                    em.wait("s", last)
                    ts = em.s.activation(state[:, nc8 * T + tt_i * 512:nc8 * T + tt_i * 512 + 512],
                                         pb[bi][:], AF.Tanh).inc("ss")
                    self.bank_done(bi, ts)
                    self.sset((id(btn), nc8 % 2), last)
            em.wait("t", self.stok())
            for dch in range(16):
                cb = bnat[:, (dch % 2) * D:(dch % 2) * D + D][:, 0:1024]
                self.sguard("g", (id(bnat), dch % 2))
                tl = em.dma("g", cb, self.l1_C[dch * 128:(dch + 1) * 128, :], ("dbn", dch % 2))
                em.wait("t", tl)
                ct = btn[:, (dch % 2) * 2048:(dch % 2) * 2048 + 1024]
                self.sguard("t", (id(btn), dch % 2))
                for fq in range(2):
                    bi = 6 + fq % 2
                    self.use_bank(bi)
                    last = None
                    for kk in range(4):
                        nb = fq * 4 + kk
                        last = em.t.transpose(pb[bi][:, kk * 128:(kk + 1) * 128],
                                              cb[:, nb * 128:nb * 128 + 128], self.idb[:]).inc("st")
                    em.wait("v", last)
                    tv = em.v.tensor_copy(ct[:, fq * 512:(fq + 1) * 512], pb[bi][:, 0:512]).inc("sv")
                    self.bank_done(bi, tv)
                em.wait("t", self.vtok())
                for tt_i in range(2):
                    bi = 2 + tt_i
                    self.use_bank(bi)
                    last = None
                    for nc8 in range(8):
                        lhs = ct[:, nc8 * 128:(nc8 + 1) * 128]
                        em.t.ldweights(lhs)
                        last = em.t.matmul(pb[bi][:], lhs,
                                           state[:, nc8 * T + tt_i * 512:nc8 * T + tt_i * 512 + 512],
                                           start=(nc8 == 0), stop=(nc8 == 7)).inc("st")
                    em.wait("v", last)
                    em.wait("v", self.l1d_tok)
                    self.sset((id(bnat), dch % 2), last)
                    self.sset((id(btn), dch % 2), last)
                    xsl = x1n[:, dch * T + tt_i * 512:dch * T + tt_i * 512 + 512]
                    gsl = g1[:, dch * T + tt_i * 512:dch * T + tt_i * 512 + 512]
                    dsl = self.gdelta[:, dch * T + tt_i * 512:dch * T + tt_i * 512 + 512]
                    em.v.scalar_tensor_tensor(lascr[:, 0:512], xsl, prm[:, 224 + dch:225 + dch],
                                              pb[bi][:], ALU.mult, ALU.add)
                    tvv = self.vtok()
                    self.bank_done(bi, tvv)
                    self.vf()
                    em.v.tensor_mul(lascr[:, 512:1024], lascr[:, 0:512], gsl)
                    self.vf()
                    em.v.tensor_add(lbscr[:, 0:512], lascr[:, 512:1024], dsl)
                    self.vf()
                    em.v.tensor_copy(dsl, lbscr[:, 0:512])
                    self.vf()
        self.tap("delta1", self.gdelta[:], self.vtok())

    def outstage(self):
        nc, em, pb = self.nc, self.em, self.pb
        with nc.sbuf_tensor("oxt", [128, 2 * D], F32) as oxt, \
             nc.sbuf_tensor("odt", [128, D], BF16) as odt, \
             nc.sbuf_tensor("oot", [128, 2 * D], F32) as oot:
            for c in range(8):
                buf = oxt[:, (c % 2) * D:(c % 2) * D + D]
                self.sguard("q", ("ox", c % 2))
                tl = em.dma("q", buf, self.x_in[c * 128:(c + 1) * 128, :], ("dox", c % 2))
                for fq in range(4):
                    bi = 6 + fq % 2
                    self.use_bank(bi)
                    last = None
                    for k in range(4):
                        f = fq * 4 + k
                        last = em.t.transpose(pb[bi][:, k * 128:(k + 1) * 128],
                                              self.gdelta[:, f * T + c * 128:f * T + c * 128 + 128],
                                              self.idb[:]).inc("st")
                    em.wait("v", last)
                    tv = em.v.tensor_copy(odt[:, fq * 512:(fq + 1) * 512], pb[bi][:, 0:512]).inc("sv")
                    self.bank_done(bi, tv)
                em.wait("v", tl)
                self.vf()
                ob = oot[:, (c % 2) * D:(c % 2) * D + D]
                self.sguard("v", ("oo", c % 2))
                tv = em.v.tensor_add(ob, buf, odt[:]).inc("sv")
                self.vf()
                self.sset(("ox", c % 2), tv)
                em.wait("q", tv)
                td = em.dma("q", self.out[c * 128:(c + 1) * 128, :], ob, ("dout", c % 2))
                self.sset(("oo", c % 2), td)
        em.wait("q", em.tok(("dout", 0)))
        em.wait("q", em.tok(("dout", 1)))
        if "do" in em.sems:
            em.wait("q", em.tok("do"))

    def finish(self):
        em, nc = self.em, self.nc
        with nc.Block() as block:
            @block.tensor
            def _(e):
                em.replay("t", e)

            @block.vector
            def _(e):
                em.replay("v", e)

            @block.scalar
            def _(e):
                em.replay("s", e)

            @block.sync
            def _(e):
                em.replay("q", e)

            @block.gpsimd
            def _(e):
                em.replay("g", e)
        for cm in self.sem_ctxs:
            cm.__exit__(None, None, None)


def make_consts():
    jj, ii = np.meshgrid(np.arange(128), np.arange(128), indexing="ij")
    oh8 = np.zeros((8, 8 * 128), dtype=np.float32)
    for c in range(8):
        oh8[c, c * 128:(c + 1) * 128] = 1.0
    oh32 = np.zeros((32, 32 * 128), dtype=ml_dtypes.bfloat16)
    for c in range(32):
        oh32[c, c * 128:(c + 1) * 128] = 1.0
    oh16s = np.zeros((16, 16 * 128), dtype=ml_dtypes.bfloat16)
    for c in range(16):
        oh16s[c, c * 128:(c + 1) * 128] = 0.05
    return dict(
        ident_f=np.eye(128, dtype=np.float32),
        ident_b=np.eye(128, dtype=ml_dtypes.bfloat16),
        ones_f=np.ones((128, 128), dtype=np.float32),
        triu_neg=np.where(jj <= ii, 0.0, -1e30).astype(np.float32),
        oh8=oh8, oh32=oh32, oh16s=oh16s)


_BUILD_CACHE = {}


def get_build(dbg_names=()):
    key = tuple(sorted(dbg_names))
    if key not in _BUILD_CACHE:
        kb = KB(dbg_names)
        kb.build()
        _BUILD_CACHE[key] = kb
    return _BUILD_CACHE[key]


def make_in_maps(inputs, dbg_names=()):
    consts = make_consts()
    x = np.ascontiguousarray(inputs["L0_attn_u"].reshape(8192, D))
    nw_b = np.ascontiguousarray(
        np.broadcast_to(inputs["L0_attn_norm_weight"][None, :], (128, D_IN)))
    zeros_x = np.zeros((T, D), dtype=np.float32)
    zeros_cb = np.zeros((CONVD,), dtype=np.float32)
    in_maps = []
    for core in range(8):
        b = core // 2
        hh = core % 2
        rows = slice(b * 2048 + hh * T, b * 2048 + (hh + 1) * T)
        fr_rows = slice(b * 2048, b * 2048 + T)
        m = dict(
            x_in=np.ascontiguousarray(x[rows]),
            x_front=(np.ascontiguousarray(x[fr_rows]) if hh == 1 else zeros_x),
            fr_conv_b=(inputs["L0_attn_conv_b"] if hh == 1 else zeros_cb),
            w_in=inputs["L0_attn_W_in"],
            conv_w=inputs["L0_attn_conv_w"],
            conv_b=inputs["L0_attn_conv_b"],
            dt_bias=inputs["L0_attn_dt_bias"],
            a_log=inputs["L0_attn_A_log"],
            d_vec=inputs["L0_attn_D"],
            w_out=inputs["L0_attn_W_out"],
            ffn_gamma=inputs["L0_ffn_gamma"],
            ffn_w=inputs["L0_ffn_W"],
            l1_B=inputs["L1_attn_B"],
            l1_C=inputs["L1_attn_C"],
            l1_D=inputs["L1_attn_D"],
            gate_w=inputs["L1_ffn_gate_w"],
            moe_wg=inputs["L1_ffn_Wg"],
            moe_wu=inputs["L1_ffn_Wu"],
            moe_wd=inputs["L1_ffn_Wd"],
            cwg=inputs["L1_ffn_cWg"],
            cwu=inputs["L1_ffn_cWu"],
            cwd=inputs["L1_ffn_cWd"],
            nw_bcast=nw_b,
            **consts,
        )
        in_maps.append(m)
    return in_maps


def kernel(**inputs):
    kb = get_build()
    in_maps = make_in_maps(inputs)
    res = run_bass_kernel_spmd(kb.nc, in_maps, core_ids=list(range(8)))
    out = np.empty((8192, D), dtype=np.float32)
    for core in range(8):
        b = core // 2
        hh = core % 2
        out[b * 2048 + hh * T:b * 2048 + (hh + 1) * T] = res.results[core]["out"]
    return out.reshape(4, 2048, D)


# version 03
# speedup vs baseline: 1.0960x; 1.0511x over previous
import numpy as np
import ml_dtypes
import concourse.bass as bass
import concourse.mybir as mybir
from concourse.bass_utils import run_bass_kernel_spmd

F32 = mybir.dt.float32
BF16 = mybir.dt.bfloat16
AF = mybir.ActivationFunctionType
ALU = mybir.AluOpType
AX = mybir.AxisListType
MAGIC = 12582912.0

D = 2048
T = 1024
HALF = 512
D_IN = 4096
NH = 64
DS = 64
CONVD = 6144
NPROJ = 10304
MOE_I = 768
NEXP = 32
NCHK = 16
CHI = 128


def apx(t, off, dims):
    a = t if isinstance(t, bass.AP) else t.ap()
    return bass.AP(a.tensor, a.offset + off, [list(d) for d in dims])


class _Pending:
    __slots__ = ("entry", "em")

    def __init__(self, em, entry):
        self.em = em
        self.entry = entry

    def inc(self, semname, amt=1):
        toks = self.entry[4]
        if semname in toks:
            return (semname, toks[semname])
        sem = self.em.sems[semname]
        sem.n += amt
        self.entry[3].append((sem.h, amt))
        toks[semname] = sem.n
        return (semname, sem.n)


class _EngRec:
    def __init__(self, em, name):
        self._em = em
        self._name = name

    def __getattr__(self, meth):
        em = self._em
        name = self._name

        def call(*a, **kw):
            if meth == "transpose":
                assert a[0].tensor.dtype == a[1].tensor.dtype, \
                    ("transpose dtype", a[0].tensor.dtype, a[1].tensor.dtype)
            entry = [meth, a, kw, [], {}]
            em.streams[name].append(entry)
            return _Pending(em, entry)
        return call


class Sem:
    def __init__(self, handle):
        self.h = handle
        self.n = 0


class Em:
    ENGS = ("t", "v", "s", "q", "g")

    def __init__(self, nc, sems, dma_pool):
        self.nc = nc
        self.streams = {e: [] for e in self.ENGS}
        self.sems = sems
        self.dma_pool = dma_pool
        self.memo = {}
        for e in self.ENGS:
            setattr(self, e, _EngRec(self, e))

    def eng(self, name):
        return getattr(self, name)

    def dma_key(self, key):
        if key not in self.sems:
            if isinstance(key, str) and key in ("dw", "dx", "do", "dsh"):
                pass
            if not self.dma_pool:
                raise RuntimeError("dma sem pool exhausted")
            self.sems[key] = self.dma_pool.pop()
        return key

    def dma(self, eng, out, in_, sem="dw", **kw):
        key = self.dma_key(sem)
        return self.eng(eng).dma_start(out=out, in_=in_, **kw).inc(key, 16)

    def wait(self, eng, tok):
        if tok is None:
            return
        name, val = tok
        key = (eng, name)
        if self.memo.get(key, -1) >= val:
            return
        self.memo[key] = val
        self.streams[eng].append(["wait_ge", (self.sems[name].h, val), {}, [], {}])

    def tok(self, semname):
        return (semname, self.sems[semname].n)

    def barrier(self, engs=ENGS):
        tagmap = {"t": "st", "v": "sv", "s": "ss", "q": "sq", "g": "sg"}
        toks = []
        for e in engs:
            self.wait(e, (tagmap[e], self.sems[tagmap[e]].n))
            toks.append(self.eng(e).nop().inc(tagmap[e]))
        for e in engs:
            for tk in toks:
                self.wait(e, tk)

    def replay(self, engname, engine):
        for idx, (meth, a, kw, incs, _toks) in enumerate(self.streams[engname]):
            try:
                ins = getattr(engine, meth)(*a, **kw)
            except Exception as ex:
                raise RuntimeError("replay fail %s[%d] %s kw=%s: %s"
                                   % (engname, idx, meth, kw, ex)) from ex
            for h, amt in incs:
                ins.then_inc(h, amt)


class KB:
    def __init__(self, dbg_names=()):
        self.nc = bass.Bass("TRN2", target_bir_lowering=False)
        self.dbgreq = set(dbg_names)
        self.dbg = {}
        self.sem_ctxs = []
        sems = {}
        for n in ["st", "sv", "ss", "sq", "sg", "do"]:
            cm = self.nc.semaphore(n)
            sems[n] = Sem(cm.__enter__())
            self.sem_ctxs.append(cm)
        dma_pool = []
        for i in range(90):
            cm = self.nc.semaphore("dp%d" % i)
            dma_pool.append(Sem(cm.__enter__()))
            self.sem_ctxs.append(cm)
        self.em = Em(self.nc, sems, dma_pool)
        self.bank_free = [None] * 8
        self.slot_tok = {}
        self._uid = 0
        self._io()

    def _io(self):
        nc = self.nc

        def din(name, shape, dt=F32):
            return nc.dram_tensor(name, list(shape), dt, kind="ExternalInput").ap()
        self.x_in = din("x_in", (T, D))
        self.x_front = din("x_front", (T, D))
        self.w_in = din("w_in", (D, NPROJ))
        self.conv_w = din("conv_w", (CONVD, 4))
        self.conv_b = din("conv_b", (CONVD,))
        self.fr_conv_b = din("fr_conv_b", (CONVD,))
        self.dt_bias = din("dt_bias", (NH,))
        self.a_log = din("a_log", (NH,))
        self.d_vec = din("d_vec", (NH,))
        self.w_out = din("w_out", (D_IN, D))
        self.ffn_gamma = din("ffn_gamma", (D,))
        self.ffn_w = din("ffn_w", (D, D))
        self.l1_B = din("l1_B", (1024, D))
        self.l1_C = din("l1_C", (D, 1024))
        self.l1_D = din("l1_D", (D,))
        self.gate_w = din("gate_w", (NEXP, D))
        self.moe_wg = din("moe_wg", (NEXP, D, MOE_I))
        self.moe_wu = din("moe_wu", (NEXP, D, MOE_I))
        self.moe_wd = din("moe_wd", (NEXP, MOE_I, D))
        self.cwg = din("cwg", (NCHK, D, CHI))
        self.cwu = din("cwu", (NCHK, D, CHI))
        self.cwd = din("cwd", (NCHK, CHI, D))
        self.ident_f = din("ident_f", (128, 128))
        self.ident_b = din("ident_b", (128, 128), BF16)
        self.ones_f = din("ones_f", (128, 128))
        self.triu_neg_d = din("triu_neg", (128, 128))
        self.oh8_d = din("oh8", (8, 8 * 128))
        self.oh32_d = din("oh32", (32, 32 * 128), BF16)
        self.oh16s_d = din("oh16s", (16, 16 * 128), BF16)
        self.nw_bcast_d = din("nw_bcast", (128, D_IN))
        self.out = nc.dram_tensor("out", [T, D], F32, kind="ExternalOutput").ap()
        self.xn_dram = nc.dram_tensor("xn_scr", [16 * 128 * T], BF16, kind="Internal").ap()
        self.cum_dram = nc.dram_tensor("cum_scr", [NH * 128], F32, kind="Internal").ap()
        self.ec_dram = nc.dram_tensor("ec_scr", [NH * 128], F32, kind="Internal").ap()
        self.sz_dram = nc.dram_tensor("sz_scr", [T * D_IN], BF16, kind="Internal").ap()
        self.c64_dram = nc.dram_tensor("c64_scr", [64 * NH * HALF], BF16, kind="Internal").ap()
        self.b64_dram = nc.dram_tensor("b64_scr", [64 * 16 * HALF], BF16, kind="Internal").ap()

    def tap(self, name, ap_, tok=None, eng="q"):
        if name not in self.dbgreq:
            return
        shp = list(ap_.shape)
        o = self.nc.dram_tensor("dbg_" + name, shp, ap_.tensor.dtype, kind="ExternalOutput").ap()
        self.dbg[name] = "dbg_" + name
        if tok is not None:
            self.em.wait(eng, tok)
        ttap = self.em.dma(eng, o, ap_, "do")
        for e in Em.ENGS:
            self.em.wait(e, ttap)

    def etok(self, eng, sem):
        em = self.em
        for entry in reversed(em.streams[eng]):
            if entry[0] not in ("wait_ge", "nop"):
                return _Pending(em, entry).inc(sem)
        return None

    def vtok(self):
        return self.etok("v", "sv")

    def stok(self):
        return self.etok("s", "ss")

    def ttok(self):
        return self.etok("t", "st")

    def vf(self):
        self.em.wait("v", self.vtok())

    def sf(self):
        self.em.wait("s", self.stok())

    def use_bank(self, i):
        self.em.wait("t", self.bank_free[i])

    def sb(self, nm, shape, dt):
        self._uid += 1
        return self.nc.sbuf_tensor("%s_u%d" % (nm, self._uid), shape, dt)

    def sguard(self, eng, key):
        for tok in self.slot_tok.get(key, ()):
            self.em.wait(eng, tok)

    def sset(self, key, *toks):
        self.slot_tok[key] = [t for t in toks if t is not None]

    def bank_done(self, i, tok):
        self.bank_free[i] = tok

    def tr16(self, src, outs, ident, banks):
        em, pb = self.em, self.pb
        for fq in range(4):
            bi = banks[fq % 2]
            self.use_bank(bi)
            last = None
            for k in range(4):
                f = fq * 4 + k
                last = em.t.transpose(pb[bi][:, k * 128:(k + 1) * 128],
                                      src[:, f * 128:f * 128 + 128], ident).inc("st")
            em.wait("v", last)
            evsrc = pb[bi][:, 0:512]
            if len(outs[fq].shape) == 3:
                evsrc = evsrc.rearrange("p (a b) -> p a b", a=4)
            tv = em.v.tensor_copy(outs[fq], evsrc).inc("sv")
            self.bank_done(bi, tv)

    def fm_out_ap(self, dst, fq, tok_off, stride):
        return apx(dst, (fq * 4) * stride + tok_off,
                   [[16 * stride, 128], [stride, 4], [1, 128]])

    def build(self, until="all"):
        nc, em = self.nc, self.em
        self.idf = nc.alloc_sbuf_tensor("idf", [128, 128], F32)
        self.idb = nc.alloc_sbuf_tensor("idb", [128, 128], BF16)
        self.onef = nc.alloc_sbuf_tensor("onef", [128, 128], F32)
        self.oh8 = nc.alloc_sbuf_tensor("oh8_sb", [8, 8 * 128], F32)
        self.prm = nc.alloc_sbuf_tensor("prm", [128, 512], F32)
        self.convp = nc.alloc_sbuf_tensor("convp", [128, 288], F32)
        self.gdelta = nc.alloc_sbuf_tensor("gdelta", [128, 16 * T], BF16)
        self.pb = [nc.alloc_psum_tensor("pb%d" % i, [128, 512], F32) for i in range(6)]
        self.pb.append(nc.alloc_psum_tensor("pbb0", [128, 1024], BF16))
        self.pb.append(nc.alloc_psum_tensor("pbb1", [128, 1024], BF16))
        self.front_limit = "all"
        if until in ("front_ln", "front_winx", "front_dt", "front_tok", "front_S", "front_dtA", "front_dtB", "front_dtC"):
            self.front_limit = until[6:]
            until = "front"
        order = ["consts", "ffn", "front", "l0", "l1"]
        stop = len(order) if until == "all" else order.index(until) + 1
        todo = set(order[:stop])
        self.consts()
        if "ffn" in todo:
            self.l0_ln_ffn()
            em.barrier()
        if "front" in todo or "l0" in todo:
            with nc.sbuf_tensor("H_ar", [128, NH * DS + 512], F32) as H_ar, \
                 nc.sbuf_tensor("xpre", [128, 48], BF16) as xpre:
                self.H = H_ar
                self.xpre = xpre
                em.v.memset(xpre[:], 0.0)
                self.vf()
                if "front" in todo:
                    self.front()
                    em.barrier()
                if "l0" in todo:
                    self.l0_attn()
        em.barrier()
        if "l1" in todo:
            self.l1()
            em.barrier()
        self.outstage()
        self.finish()
        return self

    def consts(self):
        em, prm, pb = self.em, self.prm, self.pb
        em.dma("q", self.idf[:], self.ident_f, "consts")
        em.dma("q", self.idb[:], self.ident_b, "consts")
        em.dma("q", self.onef[:], self.ones_f, "consts")
        em.dma("q", self.oh8[:], self.oh8_d, "consts")
        for f in range(16):
            em.dma("q", prm[:, f:f + 1], self.ffn_gamma[f * 128:(f + 1) * 128, None], "consts")
        em.dma("q", prm[0:NH, 240:241], self.dt_bias[:, None], "consts")
        em.dma("q", prm[0:NH, 241:242], self.a_log[:, None], "consts")
        em.dma("q", prm[0:NH, 243:244], self.d_vec[:, None], "consts")
        em.dma("q", self.convp[:, 0:192].rearrange("p (c k) -> p c k", c=48),
               apx(self.conv_w, 0, [[4, 128], [128 * 4, 48], [1, 4]]), "consts")
        em.dma("q", apx(prm, 352, [[512, 48], [1, 128]]), 
               apx(self.conv_b, 0, [[128, 48], [1, 128]]), "cvb1")
        tok = em.tok("consts")
        tokb = em.tok("cvb1")
        for e in Em.ENGS:
            em.wait(e, tok)
            em.wait(e, tokb)
        t1 = em.t.transpose(pb[0][:, 0:48], prm[0:48, 352:480], self.idf[0:48, 0:48]).inc("st")
        em.wait("v", t1)
        tv = em.v.tensor_copy(self.convp[:, 192:240], pb[0][:, 0:48]).inc("sv")
        em.wait("q", tv)
        em.dma("q", apx(prm, 352, [[512, 48], [1, 128]]),
               apx(self.fr_conv_b, 0, [[128, 48], [1, 128]]), "cvb2")
        tok = em.tok("cvb2")
        em.wait("t", tok)
        t1 = em.t.transpose(pb[1][:, 0:48], prm[0:48, 352:480], self.idf[0:48, 0:48]).inc("st")
        em.wait("v", t1)
        tv = em.v.tensor_copy(self.convp[:, 240:288], pb[1][:, 0:48]).inc("sv")
        em.wait("t", tv)
        self.bank_done(0, tv)
        self.bank_done(1, tv)
        em.v.memset(prm[:, 504:505], 1e-5)
        em.v.memset(prm[:, 505:506], 1e-6)
        em.v.memset(prm[:, 506:507], 1.0)
        self.vf()
        em.s.activation(prm[0:NH, 244:245], prm[0:NH, 241:242], AF.Exp)
        self.sf()
        tk = em.s.mul(prm[0:NH, 242:243], prm[0:NH, 244:245], -1.0).inc("ss")
        em.wait("v", tk)
        em.wait("t", tk)
        t1 = em.t.transpose(pb[0][0:1, 0:NH], prm[0:NH, 243:244], self.idf[0:NH, 0:NH]).inc("st")
        em.wait("s", t1)
        t2 = em.s.copy(prm[0:1, 352:352 + NH], pb[0][0:1, 0:NH]).inc("ss")
        em.wait("t", t2)
        t3 = em.t.matmul(pb[1][:, 0:NH], self.onef[0:1, :], prm[0:1, 352:352 + NH],
                         start=True, stop=True).inc("st")
        em.wait("v", t3)
        t4 = em.v.tensor_copy(prm[:, 96:96 + NH], pb[1][:, 0:NH]).inc("sv")
        em.wait("t", t4)
        em.barrier()

    def l0_ln_ffn(self):
        nc, em, pb, prm = self.nc, self.em, self.pb, self.prm
        ln_mean, ln_istd, ln_var = prm[:, 16:24], prm[:, 24:32], prm[:, 32:40]
        cA, cB, cC = prm[:, 40:48], prm[:, 48:56], prm[:, 56:64]
        inv_as, rmsf = prm[:, 64:72], prm[:, 72:80]
        with nc.sbuf_tensor("xq", [128, 16 * T], BF16) as xq, \
             nc.sbuf_tensor("xtile", [128, 2 * D], F32) as xtile, \
             nc.sbuf_tensor("scr", [128, D], F32) as scr:
            with nc.sbuf_tensor("xn32", [128, 16 * T], F32) as xn32:
                for c in range(8):
                    buf = xtile[:, (c % 2) * D:(c % 2) * D + D]
                    self.sguard("q", ("xt", c % 2))
                    tl = em.dma("q", buf, self.x_in[c * 128:(c + 1) * 128, :], ("dxt", c % 2))
                    em.wait("v", tl)
                    em.wait("s", tl)
                    self.sguard("s", "scrw")
                    self.sguard("v", "scrw")
                    em.v.reduce_sum(cA[:, c:c + 1], buf, axis=AX.X)
                    ts = em.s.activation(scr[:], buf, AF.Square, accum_out=cB[:, c:c + 1]).inc("ss")
                    self.vf()
                    em.wait("v", ts)
                    em.v.tensor_scalar_mul(ln_mean[:, c:c + 1], cA[:, c:c + 1], 1.0 / D)
                    self.vf()
                    em.v.tensor_mul(cC[:, c:c + 1], ln_mean[:, c:c + 1], ln_mean[:, c:c + 1])
                    self.vf()
                    tv = em.v.tensor_scalar(ln_var[:, c:c + 1], cB[:, c:c + 1], 1.0 / D,
                                            cC[:, c:c + 1], ALU.mult, ALU.subtract).inc("sv")
                    em.wait("s", tv)
                    em.s.activation(cC[:, c:c + 1], ln_var[:, c:c + 1], AF.Ln, bias=self.prm[:, 504:505])
                    self.sf()
                    ti = em.s.activation(ln_istd[:, c:c + 1], cC[:, c:c + 1], AF.Exp, scale=-0.5).inc("ss")
                    em.wait("v", ti)
                    tn = em.v.tensor_scalar(scr[:], buf, ln_mean[:, c:c + 1], ln_istd[:, c:c + 1],
                                            ALU.subtract, ALU.mult).inc("sv")
                    self.sset(("xt", c % 2), tn, ts)
                    em.wait("t", tn)
                    self.tr16(scr, [self.fm_out_ap(xn32, fq, c * 128, T) for fq in range(4)],
                              self.idf[:], (0, 1))
                    self.sset("scrw", self.ttok())
                self.tap("xn32", xn32[:], self.vtok())
                afold = xtile[:, 0:T]
                self.vf()
                em.wait("v", self.ttok())
                for f in range(16):
                    tgt = afold if f == 0 else scr[:, 0:T]
                    em.v.tensor_scalar(tgt, xn32[:, f * T:(f + 1) * T], prm[:, f:f + 1],
                                       None, ALU.mult)
                    self.vf()
                    em.v.scalar_tensor_tensor(tgt, tgt, -1.0, tgt, ALU.mult, ALU.max)
                    self.vf()
                    if f > 0:
                        em.v.tensor_max(afold, afold, scr[:, 0:T])
                        self.vf()
                em.wait("t", self.vtok())
                for c in range(8):
                    self.use_bank(2)
                    tt = em.t.transpose(pb[2][:, 0:128], afold[:, c * 128:(c + 1) * 128],
                                        self.idf[:]).inc("st")
                    em.wait("v", tt)
                    tv = em.v.reduce_max(cA[:, c:c + 1], pb[2][:, 0:128], axis=AX.X).inc("sv")
                    self.bank_done(2, tv)
                self.vf()
                em.v.tensor_scalar_add(cC, ln_var, 1e-5)
                self.vf()
                em.v.reciprocal(prm[:, 44:52] if False else cB, cC)
                self.vf()
                em.v.tensor_mul(cB, ln_var, cB)
                tv = self.vtok()
                self.vf()
                em.wait("s", tv)
                em.s.activation(cC, cB, AF.Ln, bias=self.prm[:, 505:506])
                self.sf()
                ts = em.s.activation(rmsf, cC, AF.Exp, scale=-0.5).inc("ss")
                em.wait("v", ts)
                em.v.tensor_tensor(cB, rmsf, cA, ALU.mult)
                self.vf()
                em.v.tensor_scalar_add(cB, cB, 1e-5)
                self.vf()
                em.v.tensor_scalar_mul(inv_as, cB, 1.0 / 127.0)
                em.v.reciprocal(cC, cB)
                self.vf()
                em.v.tensor_scalar_mul(cC, cC, 127.0)
                self.vf()
                em.v.tensor_tensor(cB, rmsf, cC, ALU.mult)
                self.vf()
                ccb = xtile[:, 0:T]
                ivb = xtile[:, T:2 * T]
                self.tap("prmq", prm[:, 0:96], self.vtok())
                self.bcast8(cB, ccb)
                self.bcast8(inv_as, ivb)
                self.tap("ccb", ccb, self.vtok())
                self.tap("ivb", ivb, self.vtok())
                for f in range(16):
                    self.sguard("v", "scrw")
                    em.v.tensor_scalar(scr[:, 0:T], xn32[:, f * T:(f + 1) * T], prm[:, f:f + 1],
                                       None, ALU.mult)
                    self.vf()
                    em.v.tensor_tensor(scr[:, 0:T], scr[:, 0:T], ccb, ALU.mult)
                    self.vf()
                    em.v.tensor_scalar_add(scr[:, 0:T], scr[:, 0:T], MAGIC)
                    self.vf()
                    em.v.tensor_scalar_sub(xq[:, f * T:(f + 1) * T], scr[:, 0:T], MAGIC)
                    self.vf()
                self.tap("xq", xq[:], self.vtok())
                tv = self.vtok()
                em.wait("g", tv)
                for f in range(16):
                    em.dma("g", apx(self.xn_dram, f * 128 * T, [[T, 128], [1, T]]),
                           xn32[:, f * T:(f + 1) * T], "xnout")
                self.xnout_tok = em.tok("xnout")
            em.wait("q", self.xnout_tok)
            em.wait("q", self.vtok())
            with nc.sbuf_tensor("wtile", [128, 2 * D], F32) as wtile, \
                 nc.sbuf_tensor("wq_n", [128, D], BF16) as wq_n, \
                 nc.sbuf_tensor("wqT", [128, 2 * D], BF16) as wqT:
                wsum, wabs = prm[:, 40:41], prm[:, 41:42]
                for i in range(16):
                    buf = wtile[:, (i % 2) * D:(i % 2) * D + D]
                    self.sguard("q", ("wt", i % 2))
                    tl = em.dma("q", buf, self.ffn_w[i * 128:(i + 1) * 128, :], ("dwt", i % 2))
                    em.wait("v", tl)
                    tr = em.v.reduce_sum(wabs, buf, axis=AX.X, apply_absolute_value=True).inc("sv")
                    self.sset(("wt", i % 2), tr)
                    self.vf()
                    if i == 0:
                        em.v.tensor_copy(wsum, wabs)
                    else:
                        em.v.tensor_add(wsum, wsum, wabs)
                    self.vf()
                tv = self.vtok()
                em.wait("t", tv)
                self.use_bank(2)
                tt = em.t.transpose(pb[2][0:1, 0:128], wsum, self.idf[:]).inc("st")
                em.wait("v", tt)
                em.v.reduce_sum(prm[0:1, 490:491], pb[2][0:1, 0:128], axis=AX.X)
                self.bank_done(2, self.vtok())
                self.vf()
                em.v.tensor_scalar(prm[0:1, 491:492], prm[0:1, 490:491], 1.0 / (D * D), 1e-5,
                                   ALU.mult, ALU.add)
                self.vf()
                em.v.reciprocal(prm[0:1, 490:491], prm[0:1, 491:492])
                tv = self.vtok()
                em.wait("t", tv)
                self.vf()
                self.use_bank(2)
                t1 = em.t.matmul(pb[2][:, 0:2], self.onef[0:1, :], prm[0:1, 490:492],
                                 start=True, stop=True).inc("st")
                em.wait("v", t1)
                tv = em.v.tensor_copy(prm[:, 488:490], pb[2][:, 0:2]).inc("sv")
                self.bank_done(2, tv)
                em.wait("v", tv)
                em.wait("t", tv)
                em.wait("s", tv)
                wscale_b = prm[:, 488:489]
                winv_b = prm[:, 489:490]
                for o in range(16):
                    buf = wtile[:, (o % 2) * D:(o % 2) * D + D]
                    self.sguard("q", ("wt", o % 2))
                    tl = em.dma("q", buf, self.ffn_w[o * 128:(o + 1) * 128, :], ("dwt", o % 2))
                    em.wait("v", tl)
                    em.v.tensor_scalar(buf, buf, wscale_b, MAGIC, ALU.mult, ALU.add)
                    self.vf()
                    em.v.tensor_scalar_sub(buf, buf, MAGIC)
                    self.vf()
                    self.sguard("v", "wqn")
                    tv = em.v.tensor_scalar(wq_n[:], buf, 1.0, -1.0, ALU.min, ALU.max).inc("sv")
                    em.wait("t", tv)
                    self.tr16(wq_n, [wqT[:, fq * 512:(fq + 1) * 512] for fq in range(4)],
                              self.idb[:], (6, 7))
                    self.sset("wqn", self.ttok())
                    em.wait("t", self.vtok())
                    for tt_i in range(2):
                        bi = 2 + tt_i
                        self.use_bank(bi)
                        last = None
                        for f in range(16):
                            lhs = wqT[:, f * 128:(f + 1) * 128]
                            em.t.ldweights(lhs)
                            last = em.t.matmul(pb[bi][:], lhs,
                                               xq[:, f * T + tt_i * 512:f * T + tt_i * 512 + 512],
                                               start=(f == 0), stop=(f == 15)).inc("st")
                        em.wait("v", last)
                        self.sguard("v", ("fsil", o % 2))
                        em.v.tensor_tensor(buf[:, 0:512], pb[bi][:],
                                           ivb[:, tt_i * 512:tt_i * 512 + 512], ALU.mult)
                        tvv = self.vtok()
                        self.vf()
                        em.wait("s", tvv)
                        tsl = em.s.activation(self.gdelta[:, o * T + tt_i * 512:o * T + tt_i * 512 + 512],
                                              buf[:, 0:512], AF.Silu, scale=winv_b).inc("ss")
                        self.bank_done(bi, tsl)
                        self.sset(("wt", o % 2), tsl)
                        self.sset(("fsil", o % 2), tsl)
                self.tap("g0", self.gdelta[:], self.stok(), eng="q")

    def bcast8(self, cols, dst):
        em, pb, prm = self.em, self.pb, self.prm
        em.wait("t", self.vtok())
        self.use_bank(3)
        tt = em.t.transpose(pb[3][0:8, 0:128], cols, self.idf[:]).inc("st")
        em.wait("s", tt)
        ts = em.s.copy(prm[0:8, 352:480], pb[3][0:8, 0:128]).inc("ss")
        self.bank_done(3, ts)
        em.wait("t", ts)
        for c in range(8):
            self.use_bank(3)
            t1 = em.t.matmul(pb[3][:, 0:128], self.oh8[:, c * 128:(c + 1) * 128],
                             prm[0:8, 352:480], start=True, stop=True).inc("st")
            em.wait("v", t1)
            tv = em.v.tensor_copy(dst[:, c * 128:(c + 1) * 128], pb[3][:, 0:128]).inc("sv")
            self.bank_done(3, tv)
        em.wait("t", self.vtok())

    def front(self):
        nc, em = self.nc, self.em
        em.v.memset(self.H[0:64, :], 0.0)
        self.vf()
        with nc.sbuf_tensor("fxn", [128, 16 * HALF], BF16) as fxn, \
             nc.sbuf_tensor("fxB", [128, 16 * HALF], BF16) as fxB, \
             nc.sbuf_tensor("ftail", [128, 144], BF16) as ftail, \
             nc.sbuf_tensor("fdt", [128, 3456], F32) as fdt, \
             nc.sbuf_tensor("ftok", [128, 2 * T], BF16) as ftok, \
             nc.sbuf_tensor("fxdtd", [128, D_IN], BF16) as fxdtd, \
             nc.sbuf_tensor("fwbuf", [128, 2 * 2048], BF16) as fwbuf, \
             nc.sbuf_tensor("fcst", [128, 2 * 516], BF16) as fcst, \
             nc.sbuf_tensor("fdiag", [128, 4 * 128], BF16) as fdiag, \
             nc.sbuf_tensor("fscr", [128, 2 * D], F32) as fscr:
            em.v.memset(ftail[:], 0.0)
            self.vf()
            for fh in range(2):
                h0 = fh * HALF
                em.wait("s", self.vtok())
                em.wait("s", self.ttok())
                em.wait("v", self.ttok())
                for c in range(4):
                    buf = fscr[:, (c % 2) * D:(c % 2) * D + D]
                    ob = ftok[:, 0:D]
                    self.sguard("q", ("fx", c % 2))
                    tl = em.dma("q", buf, self.x_front[h0 + c * 128:h0 + (c + 1) * 128, :], ("dfx", c % 2))
                    em.wait("v", tl)
                    em.wait("s", tl)
                    em.v.reduce_sum(self.prm[:, 40:41], buf, axis=AX.X)
                    ts = em.s.activation(ob, buf, AF.Square, accum_out=self.prm[:, 41:42]).inc("ss")
                    self.vf()
                    em.wait("v", ts)
                    em.v.tensor_scalar_mul(self.prm[:, 42:43], self.prm[:, 40:41], 1.0 / D)
                    self.vf()
                    em.v.tensor_mul(self.prm[:, 43:44], self.prm[:, 42:43], self.prm[:, 42:43])
                    self.vf()
                    tv = em.v.tensor_scalar(self.prm[:, 44:45], self.prm[:, 41:42], 1.0 / D,
                                            self.prm[:, 43:44], ALU.mult, ALU.subtract).inc("sv")
                    em.wait("s", tv)
                    em.s.activation(self.prm[:, 45:46], self.prm[:, 44:45], AF.Ln, bias=self.prm[:, 504:505])
                    self.sf()
                    ti = em.s.activation(self.prm[:, 46:47], self.prm[:, 45:46], AF.Exp,
                                         scale=-0.5).inc("ss")
                    em.wait("v", ti)
                    em.v.tensor_scalar(buf, buf, self.prm[:, 42:43], self.prm[:, 46:47],
                                       ALU.subtract, ALU.mult)
                    self.vf()
                    self.sguard("v", "fwb")
                    tc = em.v.tensor_copy(fwbuf[:, 0:D], buf).inc("sv")
                    self.sset(("fx", c % 2), tc)
                    em.wait("t", tc)
                    self.tr16(fwbuf[:, 0:D], [self.fm_out_ap(fxn, fq, c * 128, HALF) for fq in range(4)],
                              self.idb[:], (6, 7))
                    self.sset("fwb", self.ttok())
                if fh == 1:
                    self.vf()
                    em.v.tensor_copy(self.xpre[:].rearrange("p (f k) -> p f k", f=16),
                                     apx(fxn, 509, [[16 * HALF, 128], [HALF, 16], [1, 3]]))
                    self.vf()
                if self.front_limit == "ln":
                    continue
                self.winx_pass(fxn, fxB, None, fdt, fwbuf, fcst, fdiag, ftail,
                               m_list=list(range(32, 48)), dt_too=True, front=True, h0=h0)
                if self.front_limit == "winx":
                    continue
                for c in range(4):
                    self.sum_chunk_front(fxB, fdt, ftok, fxdtd, c)
                    if self.front_limit in ("dt", "tok", "S"):
                        break
        self.tap("H_init", self.H[0:64, 0:NH * DS], self.vtok())

    def winx_pass(self, xnh, xB_fm, C_stage, dtb, wbuf, cstage, diag, tailbuf,
                  m_list, dt_too, front, h0, pre_src=None):
        nc, em, pb, prm = self.nc, self.em, self.pb, self.prm
        bias_off = 240 if front else 192
        em.wait("t", self.vtok())
        cdma_toks = [None, None]
        for mi, m in enumerate(m_list):
            wb = wbuf[:, (mi % 2) * 2048:(mi % 2) * 2048 + 2048]
            self.sguard("g", (id(wbuf), mi % 2))
            self.sguard("g", "fwb")
            wt = em.dma("g", wb.rearrange("p (f i) -> p f i", f=16),
                        apx(self.w_in, m * 128, [[NPROJ, 128], [128 * NPROJ, 16], [1, 128]]),
                        ("dwin", id(wbuf), mi % 2))
            em.wait("t", wt)
            bi = mi % 2
            self.use_bank(bi)
            if pre_src is not None:
                self.use_bank(4)
            last = None
            for f in range(16):
                lhs = wb[:, f * 128:(f + 1) * 128]
                em.t.ldweights(lhs)
                last = em.t.matmul(pb[bi][:], lhs, xnh[:, f * HALF:(f + 1) * HALF],
                                   start=(f == 0), stop=(f == 15)).inc("st")
                if pre_src is not None:
                    last = em.t.matmul(pb[4][:, 0:3], lhs, pre_src[:, f * 3:(f + 1) * 3],
                                       start=(f == 0), stop=(f == 15)).inc("st")
            self.sset((id(wbuf), mi % 2), last)
            ch0 = m * 128 - D_IN
            cmi = ch0 // 128
            cs = cstage[:, (mi % 2) * 516:(mi % 2) * 516 + 516]
            em.wait("v", last)
            self.sguard("v", (id(cstage), mi % 2))
            if pre_src is not None:
                tpre = em.v.tensor_copy(cs[:, 0:3], pb[4][:, 0:3]).inc("sv")
                self.bank_done(4, tpre)
            else:
                em.v.tensor_copy(cs[:, 0:3], tailbuf[:, cmi * 3:cmi * 3 + 3])
            tv = em.v.tensor_copy(cs[:, 3:515], pb[bi][:]).inc("sv")
            self.bank_done(bi, tv)
            self.vf()
            em.v.tensor_copy(tailbuf[:, cmi * 3:cmi * 3 + 3], cs[:, 512:515])
            self.vf()
            for k in range(4):
                em.v.tensor_scalar(diag[:, k * 128:(k + 1) * 128], self.idb[:],
                                   self.convp[:, cmi * 4 + k:cmi * 4 + k + 1], None, ALU.mult)
            tdg = self.vtok()
            self.vf()
            em.wait("t", tdg)
            bi2 = 2 + mi % 2
            self.use_bank(bi2)
            last = None
            for k in range(4):
                em.t.ldweights(diag[:, k * 128:(k + 1) * 128])
                last = em.t.matmul(pb[bi2][:], diag[:, k * 128:(k + 1) * 128],
                                   cs[:, k:k + 512], start=(k == 0), stop=(k == 3)).inc("st")
            self.sset((id(cstage), mi % 2), last)
            em.wait("s", last)
            bias_col = self.convp[:, bias_off + cmi:bias_off + cmi + 1]
            if ch0 < 2048:
                blk = ch0 // 128
                tv = em.s.activation(xB_fm[:, blk * HALF:(blk + 1) * HALF], pb[bi2][:],
                                     AF.Silu, bias=bias_col).inc("ss")
                self.bank_done(bi2, tv)
            else:
                b = (ch0 - 2048) // 128
                cst = C_stage[:, (mi % 2) * HALF:(mi % 2) * HALF + HALF]
                em.wait("s", cdma_toks[mi % 2])
                tv = em.s.activation(cst, pb[bi2][:], AF.Silu, bias=bias_col).inc("ss")
                self.bank_done(bi2, tv)
                em.wait("g", tv)
                em.dma("g", apx(self.c64_dram, (2 * b) * HALF, [[NH * HALF, 64], [1, 512]]),
                       cst[0:64, :], ("c64o", h0, mi % 2))
                cdma_toks[mi % 2] = em.dma("g", apx(self.c64_dram, (2 * b + 1) * HALF,
                                                    [[NH * HALF, 64], [1, 512]]),
                                           cst[64:128, :], ("c64o", h0, mi % 2))
                if b < 8:
                    pass
        if not front:
            tvv = self.stok()
            em.wait("g", tvv)
            em.dma("g", apx(self.b64_dram, 0, [[16 * HALF, 64], [2 * HALF, 8], [1, 512]]),
                   apx(xB_fm, 8 * HALF, [[16 * HALF, 64], [HALF, 8], [1, 512]]), ("b64o", h0))
            em.dma("g", apx(self.b64_dram, HALF, [[16 * HALF, 64], [2 * HALF, 8], [1, 512]]),
                   apx(xB_fm, 64 * (16 * HALF) + 8 * HALF, [[16 * HALF, 64], [HALF, 8], [1, 512]]),
                   ("b64o", h0))
        if dt_too:
            wb = wbuf[:, 0:16 * 64]
            self.sguard("g", (id(wbuf), 0))
            wt = em.dma("g", wb.rearrange("p (f i) -> p f i", f=16),
                        apx(self.w_in, 10240, [[NPROJ, 128], [128 * NPROJ, 16], [1, 64]]),
                        ("dwin", id(wbuf), 0))
            em.wait("t", wt)
            self.use_bank(4)
            last = None
            for f in range(16):
                lhs = wb[:, f * 64:(f + 1) * 64]
                em.t.ldweights(lhs)
                last = em.t.matmul(pb[4][0:64, 0:512], lhs, xnh[:, f * HALF:(f + 1) * HALF],
                                   start=(f == 0), stop=(f == 15)).inc("st")
            em.wait("v", last)
            ts = em.v.tensor_scalar(dtb[0:64, 0:512], pb[4][0:64, 0:512], prm[0:64, 240:241],
                                    None, ALU.add).inc("sv")
            self.bank_done(4, ts)
            em.wait("s", ts)
            em.s.activation(dtb[0:64, 512:1024], dtb[0:64, 0:512], AF.Abs)
            self.sf()
            em.s.activation(dtb[0:64, 1024:1536], dtb[0:64, 512:1024], AF.Exp, scale=-1.0)
            self.sf()
            em.s.activation(dtb[0:64, 1536:2048], dtb[0:64, 1024:1536], AF.Ln, bias=self.prm[0:64, 506:507])
            self.sf()
            ts = em.s.activation(dtb[0:64, 512:1024], dtb[0:64, 0:512], AF.Relu).inc("ss")
            em.wait("v", ts)
            tv = em.v.tensor_add(dtb[0:64, 2048:2560], dtb[0:64, 512:1024],
                                 dtb[0:64, 1536:2048]).inc("sv")
            self.vf()
            em.wait("s", tv)
            em.wait("t", tv)

    def dt_chunk(self, dtb, c, need_ec, need_toks):
        nc, em, pb, prm = self.nc, self.em, self.pb, self.prm
        lim = getattr(self, "front_limit", "all")
        dtp = dtb[0:64, 2048:2560]
        dA = dtb[0:64, 2560:2688]
        cum = dtb[0:64, 2688:2816]
        dte = dtb[0:64, 2816:2944]
        ec = dtb[0:64, 2944:3072]
        dtd = dtb[0:64, 3072:3200]
        em.v.tensor_scalar(dA, dtp[:, c * 128:(c + 1) * 128], prm[0:64, 242:243], None, ALU.mult)
        self.vf()
        em.v.tensor_tensor_scan(cum, self.onef[0:64, 0:128], dA, 0.0, ALU.mult, ALU.add)
        self.vf()
        em.v.tensor_copy(prm[0:64, 245:246], cum[:, 127:128])
        tv = self.vtok()
        self.vf()
        if lim == "dtA":
            return
        em.wait("s", tv)
        em.s.activation(dte, cum, AF.Exp, bias=prm[0:64, 245:246], scale=-1.0)
        em.s.activation(prm[0:64, 246:247], prm[0:64, 245:246], AF.Exp)
        if need_ec:
            em.s.activation(ec, cum, AF.Exp)
        ts = self.stok()
        self.sf()
        em.wait("v", ts)
        em.v.tensor_mul(dtd, dtp[:, c * 128:(c + 1) * 128], dte)
        tv = self.vtok()
        self.vf()
        if lim == "dtB":
            return
        em.wait("t", tv)
        self.use_bank(5)
        t3 = em.t.transpose(pb[5][:, 0:64], dtd, self.idf[0:64, 0:64]).inc("st")
        if need_toks:
            em.t.transpose(pb[5][:, 64:128], dtp[:, c * 128:(c + 1) * 128], self.idf[0:64, 0:64])
            t3 = em.t.transpose(pb[5][:, 128:192], cum, self.idf[0:64, 0:64]).inc("st")
        em.wait("v", t3)
        em.v.tensor_copy(dtb[:, 3200:3264], pb[5][:, 0:64])
        if need_toks:
            em.v.tensor_copy(dtb[:, 3264:3328], pb[5][:, 64:128])
            em.v.tensor_scalar_mul(dtb[:, 3328:3392], pb[5][:, 128:192], -1.0)
        lastent = None
        for entry in reversed(em.streams["v"]):
            if entry[0] not in ("wait_ge", "nop"):
                lastent = entry
                break
        tv = _Pending(em, lastent).inc("sv")
        em.wait("v", tv)
        self.bank_done(5, tv)
        self.use_bank(5)
        t2 = em.t.transpose(pb[5][0:1, 0:64], prm[0:64, 246:247], self.idf[0:64, 0:64]).inc("st")
        em.wait("s", t2)
        t3 = em.s.copy(prm[0:1, 352:416], pb[5][0:1, 0:64]).inc("ss")
        self.bank_done(5, t3)
        em.wait("t", t3)
        self.use_bank(5)
        t4 = em.t.matmul(pb[5][:, 0:64], self.onef[0:1, :], prm[0:1, 352:416],
                         start=True, stop=True).inc("st")
        em.wait("v", t4)
        tv = em.v.tensor_copy(prm[:, 160:224], pb[5][:, 0:64]).inc("sv")
        self.bank_done(5, tv)
        self.vf()

    def xB_transpose_chunk(self, xB_fm, x_dst, B_dst, c, banks):
        em, pb = self.em, self.pb
        for b8 in range(8):
            bi = banks[b8 % 2]
            self.use_bank(bi)
            em.t.transpose(pb[bi][:, 0:128],
                           xB_fm[:, b8 * HALF + c * 128:b8 * HALF + c * 128 + 128], self.idb[:])
            tb = em.t.transpose(pb[bi][:, 128:256],
                                xB_fm[:, (8 + b8) * HALF + c * 128:(8 + b8) * HALF + c * 128 + 128],
                                self.idb[:]).inc("st")
            em.wait("v", tb)
            em.v.tensor_copy(x_dst[:, b8 * 128:(b8 + 1) * 128], pb[bi][:, 0:128])
            tv = em.v.tensor_copy(B_dst[:, b8 * 128:(b8 + 1) * 128], pb[bi][:, 128:256]).inc("sv")
            self.bank_done(bi, tv)
        self.vf()

    def sum_chunk_front(self, fxB, fdt, ftok, fxdtd, c):
        nc, em, pb, prm = self.nc, self.em, self.pb, self.prm
        self.dt_chunk(fdt, c, need_ec=False, need_toks=False)
        if self.front_limit in ("dt", "dtA", "dtB", "dtC"):
            return
        x_tok = ftok[:, 0:T]
        B_tok = ftok[:, T:2 * T]
        em.wait("t", self.vtok())
        self.xB_transpose_chunk(fxB, x_tok, B_tok, c, (6, 7))
        x_ap = apx(x_tok, 0, [[2 * T, 128], [64, 16], [0, 4], [1, 64]])
        d_ap = apx(fdt, 3200, [[3456, 128], [4, 16], [1, 4], [0, 64]])
        tv = em.v.tensor_tensor(fxdtd[:].rearrange("p (g r q) -> p g r q", g=16, r=4),
                                x_ap, d_ap, ALU.mult).inc("sv")
        em.wait("t", tv)
        if self.front_limit == "tok":
            return
        for grp in range(8):
            bi = 2 + grp % 2
            self.use_bank(bi)
            last = None
            for hh in range(8):
                h = grp * 8 + hh
                xbh = h // 4
                lhs = B_tok[:, xbh * 64:(xbh + 1) * 64]
                em.t.ldweights(lhs)
                last = em.t.matmul(pb[bi][0:64, hh * 64:(hh + 1) * 64], lhs,
                                   fxdtd[:, h * 64:(h + 1) * 64], start=True, stop=True).inc("st")
            em.wait("v", last)
            self.h_update(grp, pb[bi])
            self.bank_done(bi, self.vtok())

    def h_update(self, grp, sbank):
        em = self.em
        Hsl = self.H[0:64, grp * 512:(grp + 1) * 512]
        Hscr = self.H[0:64, NH * DS:NH * DS + 512]
        dec_ap = apx(self.prm, 160 + grp * 8, [[512, 64], [1, 8], [0, 64]])
        em.v.tensor_tensor(Hscr.rearrange("p (a b) -> p a b", a=8),
                           Hsl.rearrange("p (a b) -> p a b", a=8), dec_ap, ALU.mult)
        self.vf()
        em.v.tensor_add(Hsl, Hscr, sbank[0:64, 0:512])
        self.vf()

    def l0_attn(self):
        nc, em = self.nc, self.em
        with nc.sbuf_tensor("nwb", [128, D_IN], BF16) as nwb, \
             nc.sbuf_tensor("a_xB", [128, 16 * HALF], BF16) as a_xB, \
             nc.sbuf_tensor("a_tail", [128, 144], BF16) as a_tail, \
             nc.sbuf_tensor("a_dt", [128, 3456], F32) as a_dt, \
             nc.sbuf_tensor("a_tok", [128, 2048], BF16) as a_tok, \
             nc.sbuf_tensor("a_wbuf", [128, 2 * 2048], BF16) as a_wbuf, \
             nc.sbuf_tensor("a_cst", [128, 2 * 516], BF16) as a_cst, \
             nc.sbuf_tensor("a_Cst", [128, 2 * HALF], BF16) as a_Cst, \
             nc.sbuf_tensor("a_diag", [128, 4 * 128], BF16) as a_diag, \
             nc.sbuf_tensor("a_scr", [128, 512], F32) as a_scr:
            self.nwb = nwb
            self.nwb_tok = em.dma("g", nwb[:], self.nw_bcast_d, "nwb")
            em.v.memset(a_tail[:], 0.0)
            self.vf()
            for hf in range(2):
                self.l0_half(hf, nwb, a_xB, a_tail, a_dt, a_tok, a_wbuf, a_cst,
                             a_Cst, a_diag, a_scr)
                em.barrier()
        self.tap("delta0", self.gdelta[:], self.vtok())

    def l0_half(self, hf, nwb, a_xB, a_tail, a_dt, a_tok, a_wbuf, a_cst, a_Cst,
                a_diag, a_scr):
        nc, em, pb, prm = self.nc, self.em, self.pb, self.prm
        h0 = hf * HALF
        with self.sb("xnh", [128, 16 * HALF], BF16) as xnh, \
             self.sb("zst", [128, HALF], BF16) as zst, \
             self.sb("ztr", [128, 2 * 512], BF16) as ztr:
            em.wait("q", self.xnout_tok)
            tl = em.dma("q", xnh[:].rearrange("p (f t) -> p f t", f=16),
                        apx(self.xn_dram, h0, [[T, 128], [128 * T, 16], [1, HALF]]), "xnh")
            em.wait("t", tl)
            zdma = [None, None]
            for m in range(32):
                wb = a_wbuf[:, (m % 2) * 2048:(m % 2) * 2048 + 2048]
                self.sguard("g", (id(a_wbuf), m % 2))
                wt = em.dma("g", wb.rearrange("p (f i) -> p f i", f=16),
                            apx(self.w_in, m * 128, [[NPROJ, 128], [128 * NPROJ, 16], [1, 128]]),
                            ("dwin", id(a_wbuf), m % 2))
                em.wait("t", wt)
                bi = m % 2
                self.use_bank(bi)
                last = None
                for f in range(16):
                    lhs = wb[:, f * 128:(f + 1) * 128]
                    em.t.ldweights(lhs)
                    last = em.t.matmul(pb[bi][:], lhs, xnh[:, f * HALF:(f + 1) * HALF],
                                       start=(f == 0), stop=(f == 15)).inc("st")
                self.sset((id(a_wbuf), m % 2), last)
                em.wait("s", last)
                self.sguard("s", "zst")
                tv = em.s.activation(zst[:], pb[bi][:], AF.Silu).inc("ss")
                self.bank_done(bi, tv)
                em.wait("t", tv)
                bi2 = 6 + m % 2
                self.use_bank(bi2)
                last = None
                for q in range(4):
                    last = em.t.transpose(pb[bi2][:, q * 128:(q + 1) * 128],
                                          zst[:, q * 128:(q + 1) * 128], self.idb[:]).inc("st")
                self.sset("zst", last)
                em.wait("v", last)
                em.wait("v", zdma[m % 2])
                tv = em.v.tensor_copy(ztr[:, (m % 2) * 512:(m % 2) * 512 + 512], pb[bi2][:, 0:512]).inc("sv")
                self.bank_done(bi2, tv)
                em.wait("g", tv)
                zdma[m % 2] = em.dma("g", apx(self.sz_dram, h0 * D_IN + m * 128,
                                              [[D_IN, 128], [128 * D_IN, 4], [1, 128]]),
                                     ztr[:, (m % 2) * 512:(m % 2) * 512 + 512].rearrange(
                                         "p (q i) -> p q i", q=4), ("szo", h0, m % 2))
            self.winx_pass(xnh, a_xB, a_Cst, a_dt, a_wbuf, a_cst, a_diag, a_tail,
                           m_list=list(range(32, 80)), dt_too=True, front=False, h0=h0,
                           pre_src=(self.xpre if hf == 0 else None))
        self.a_xB_ref = a_xB
        em.wait("t", self.stok())
        stageA_sh = (em.tok(("szo", h0, 0)), em.tok(("szo", h0, 1)),
                     em.tok(("c64o", h0, 0)), em.tok(("c64o", h0, 1)),
                     em.tok(("b64o", h0)))
        for _e in ("q", "g", "v", "s", "t"):
            for _tk in stageA_sh:
                em.wait(_e, _tk)
        with self.sb("szc", [128, D_IN], BF16) as szc, \
             self.sb("C64", [64, 2048], BF16) as C64, \
             self.sb("B64", [64, 16 * 128], BF16) as B64, \
             self.sb("Hbf", [64, NH * DS], BF16) as Hbf, \
             self.sb("xdt", [128, 512], BF16) as xdt, \
             self.sb("xdtd", [128, 512], BF16) as xdtd, \
             self.sb("Rq", [128, 2048], F32) as Rq, \
             self.sb("mexp", [128, 2048], BF16) as mexp, \
             self.sb("eCb", [64, 2048], BF16) as eCb, \
             self.sb("Ceq", [64, 2048], BF16) as Ceq, \
             self.sb("GM", [128, 1024], BF16) as GM, \
             self.sb("yz", [128, 1024], F32) as yz, \
             self.sb("sqs", [128, 1024], F32) as sqs, \
             self.sb("ntok", [128, 1024], BF16) as ntok, \
             self.sb("nfm", [128, 32 * 256], BF16) as nfm:
            for c in range(4):
                self.l0_chunk(hf, c, stageA_sh, a_dt, a_tok, szc, C64, B64, Hbf,
                              xdt, xdtd, Rq, mexp, eCb, Ceq, GM, yz, sqs, ntok, nfm)
                if c % 2 == 1:
                    self.l0_wout(hf, c // 2, nfm, a_scr)

    def l0_chunk(self, hf, c, stageA_sh, a_dt, a_tok, szc, C64, B64, Hbf,
                 xdt, xdtd, Rq, mexp, eCb, Ceq, GM, yz, sqs, ntok, nfm):
        nc, em, pb, prm = self.nc, self.em, self.pb, self.prm
        h0 = hf * HALF
        em.wait("g", em.tok("rq") if "rq" in em.sems else None)
        em.wait("g", em.tok("ecb") if "ecb" in em.sems else None)
        self.dt_chunk(a_dt, c, need_ec=True, need_toks=True)
        tv = self.vtok()
        ts = self.stok()
        em.wait("g", ts)
        em.dma("g", apx(self.cum_dram, 0, [[128, 64], [1, 128]]), a_dt[0:64, 2688:2816], "cumo")
        em.dma("g", apx(self.ec_dram, 0, [[128, 64], [1, 128]]), a_dt[0:64, 2944:3072], "eco")
        cumec_tok = (em.tok("cumo"), em.tok("eco"))
        for tk in stageA_sh:
            em.wait("q", tk)
        em.wait("q", self.vtok())
        em.wait("q", self.ttok())
        tsz = em.dma("q", szc[:], apx(self.sz_dram, (h0 + c * 128) * D_IN,
                                      [[D_IN, 128], [1, D_IN]]), "szc")
        tb64 = em.dma("q", B64[:], apx(self.b64_dram, c * 128,
                                       [[16 * HALF, 64], [HALF, 16], [1, 128]]), "b64l")
        em.wait("v", self.ttok())
        em.v.tensor_copy(Hbf[:], self.H[0:64, 0:NH * DS])
        self.vf()
        em.wait("t", self.vtok())
        self.xB_transpose_chunk(self.a_xB_ref, a_tok[:, 0:T], a_tok[:, T:2 * T], c, (6, 7))
        em.wait("t", tsz)
        em.wait("t", tb64)
        for qq in range(4):
            em.wait("q", self.ttok())
            em.wait("q", self.vtok())
            tc64 = em.dma("q", C64[:], apx(self.c64_dram, qq * 16 * HALF + c * 128,
                                           [[NH * HALF, 64], [HALF, 16], [1, 128]]), "c64l")
            em.wait("t", tc64)
            em.wait("v", tc64)
            em.wait("q", self.stok())
            em.wait("q", cumec_tok[0])
            em.wait("q", cumec_tok[1])
            em.wait("g", cumec_tok[1])
            t1 = em.dma("q", Rq[:].rearrange("p (h i) -> p h i", h=16),
                        apx(self.cum_dram, qq * 16 * 128, [[0, 128], [128, 16], [1, 128]]), "rq")
            em.wait("g", t1)
            t2 = em.dma("g", Rq[:].rearrange("p (h i) -> p h i", h=16),
                        apx(self.triu_neg_d, 0, [[128, 128], [0, 16], [1, 128]]), "rqa",
                        accum_op=ALU.add)
            em.wait("v", t2)
            em.wait("v", self.stok())
            ncol_ap = apx(a_dt, 3328 + qq * 16, [[3456, 128], [1, 16], [0, 128]])
            tv = em.v.tensor_tensor(Rq[:].rearrange("p (h i) -> p h i", h=16),
                                    Rq[:].rearrange("p (h i) -> p h i", h=16),
                                    ncol_ap, ALU.add).inc("sv")
            em.wait("s", tv)
            em.wait("s", self.vtok())
            em.s.activation(mexp[:, 0:1024], Rq[:, 0:1024], AF.Exp)
            ts = em.s.activation(mexp[:, 1024:2048], Rq[:, 1024:2048], AF.Exp).inc("ss")
            t3 = em.dma("g", eCb[:].rearrange("p (h i) -> p h i", h=16),
                        apx(self.ec_dram, qq * 16 * 128, [[0, 64], [128, 16], [1, 128]]), "ecb")
            em.wait("v", t3)
            em.wait("v", self.ttok())
            tce = em.v.tensor_mul(Ceq[:], C64[:], eCb[:]).inc("sv")
            em.wait("t", tce)
            em.wait("v", ts)
            for g2 in range(2):
                grp = qq * 2 + g2
                ybank = 0 + grp % 2
                sbank = 4 + grp % 2
                em.wait("v", self.ttok())
                x_ap = apx(a_tok, 2 * grp * 64, [[2048, 128], [64, 2], [0, 4], [1, 64]])
                dtp_ap = apx(a_dt, 3264 + grp * 8, [[3456, 128], [4, 2], [1, 4], [0, 64]])
                dtd_ap = apx(a_dt, 3200 + grp * 8, [[3456, 128], [4, 2], [1, 4], [0, 64]])
                em.v.tensor_tensor(xdt[:].rearrange("p (g r q) -> p g r q", g=2, r=4),
                                   x_ap, dtp_ap, ALU.mult)
                tv = em.v.tensor_tensor(xdtd[:].rearrange("p (g r q) -> p g r q", g=2, r=4),
                                        x_ap, dtd_ap, ALU.mult).inc("sv")
                em.wait("t", tv)
                self.use_bank(ybank)
                self.use_bank(sbank)
                ylast = None
                for half4 in range(2):
                    gtbank = 2 + half4
                    self.use_bank(gtbank)
                    gtlast = None
                    for k in range(4):
                        h = grp * 8 + half4 * 4 + k
                        xbh = h // 4
                        lhs = B64[:, xbh * 128:(xbh + 1) * 128]
                        em.t.ldweights(lhs)
                        lh = h - qq * 16
                        gtlast = em.t.matmul(pb[gtbank][:, k * 128:(k + 1) * 128], lhs,
                                             C64[:, lh * 128:(lh + 1) * 128],
                                             start=True, stop=True).inc("st")
                    em.wait("v", gtlast)
                    self.sguard("v", ("GM", half4))
                    msl = mexp[:, (g2 * 8 + half4 * 4) * 64 * 2:(g2 * 8 + half4 * 4 + 4) * 64 * 2]
                    tv = em.v.tensor_mul(GM[:, half4 * 512:(half4 + 1) * 512],
                                         pb[gtbank][:], msl).inc("sv")
                    self.bank_done(gtbank, tv)
                    em.wait("t", tv)
                    for k in range(4):
                        h = grp * 8 + half4 * 4 + k
                        hq = g2 * 8 + half4 * 4 + k
                        lhs1 = GM[:, half4 * 512 + k * 128:half4 * 512 + (k + 1) * 128]
                        em.t.ldweights(lhs1)
                        em.t.matmul(pb[ybank][:, (half4 * 4 + k) * 64:(half4 * 4 + k + 1) * 64],
                                    lhs1, xdt[:, (half4 * 4 + k) * 64:(half4 * 4 + k + 1) * 64],
                                    start=True, stop=False)
                        lhs2 = Ceq[:, hq * 128:(hq + 1) * 128]
                        em.t.ldweights(lhs2)
                        ylast = em.t.matmul(pb[ybank][:, (half4 * 4 + k) * 64:(half4 * 4 + k + 1) * 64],
                                            lhs2, Hbf[:, h * 64:(h + 1) * 64],
                                            start=False, stop=True).inc("st")
                    self.sset(("GM", half4), ylast)
                slast = None
                for hh in range(8):
                    h = grp * 8 + hh
                    xbh = h // 4
                    lhs = a_tok[:, T + xbh * 64:T + (xbh + 1) * 64]
                    em.t.ldweights(lhs)
                    slast = em.t.matmul(pb[sbank][0:64, hh * 64:(hh + 1) * 64], lhs,
                                        xdtd[:, hh * 64:(hh + 1) * 64], start=True, stop=True).inc("st")
                em.wait("v", ylast)
                xg_ap = apx(a_tok, 2 * grp * 64, [[2048, 128], [64, 2], [0, 4], [1, 64]])
                D_ap = apx(prm, 96 + grp * 8, [[512, 128], [4, 2], [1, 4], [0, 64]])
                xsc = sqs[:, (grp % 2) * 512:(grp % 2) * 512 + 512]
                em.v.tensor_tensor(xsc.rearrange("p (g r q) -> p g r q", g=2, r=4),
                                   xg_ap, D_ap, ALU.mult)
                self.vf()
                em.v.tensor_add(xsc, xsc, pb[ybank][:])
                tyv = self.vtok()
                self.bank_done(ybank, tyv)
                self.vf()
                yhalf = yz[:, (grp % 2) * 512:(grp % 2) * 512 + 512]
                em.v.tensor_mul(yhalf, xsc, szc[:, grp * 512:(grp + 1) * 512])
                self.vf()
                em.wait("v", slast)
                self.h_update(grp, pb[sbank])
                self.bank_done(sbank, self.vtok())
                if grp % 2 == 1:
                    self.l0_rms(hf, c, grp // 2, yz, sqs, ntok, nfm)

    def l0_rms(self, hf, c, qq, yz, sqs, ntok, nfm):
        nc, em, pb, prm = self.nc, self.em, self.pb, self.prm
        em.wait("s", self.vtok())
        em.wait("v", self.nwb_tok)
        ts = em.s.activation(sqs[:], yz[:], AF.Square).inc("ss")
        em.wait("v", ts)
        rms = prm[:, 248:264]
        tv = em.v.tensor_reduce(rms.rearrange("p (h o) -> p h o", h=16),
                                sqs[:].rearrange("p (h q) -> p h q", h=16),
                                AX.X, ALU.add).inc("sv")
        em.wait("s", tv)
        em.s.activation(prm[:, 280:296], rms, AF.Ln, bias=self.prm[:, 504:505], scale=1.0 / DS)
        self.sf()
        ts = em.s.activation(rms, prm[:, 280:296], AF.Exp, scale=-0.5).inc("ss")
        em.wait("v", ts)
        rms_ap = apx(prm, 248, [[512, 128], [1, 16], [0, 64]])
        em.v.tensor_tensor(sqs[:].rearrange("p (h q) -> p h q", h=16),
                           yz[:].rearrange("p (h q) -> p h q", h=16), rms_ap, ALU.mult)
        self.vf()
        self.sguard("v", "ntok")
        tv = em.v.tensor_mul(ntok[:], sqs[:], self.nwb[:, qq * 1024:(qq + 1) * 1024]).inc("sv")
        em.wait("t", tv)
        for fq in range(2):
            bi = 6 + fq % 2
            self.use_bank(bi)
            last = None
            for k in range(4):
                fb = fq * 4 + k
                last = em.t.transpose(pb[bi][:, k * 128:(k + 1) * 128],
                                      ntok[:, fb * 128:fb * 128 + 128], self.idb[:]).inc("st")
            em.wait("v", last)
            out_ap = apx(nfm, (qq * 8 + fq * 4) * 256 + (c % 2) * 128,
                         [[32 * 256, 128], [256, 4], [1, 128]])
            tv = em.v.tensor_copy(out_ap, pb[bi][:, 0:512].rearrange("p (a b) -> p a b", a=4)).inc("sv")
            self.bank_done(bi, tv)
            self.sset("ntok", self.ttok())

    def l0_wout(self, hf, pair, nfm, a_scr):
        nc, em, pb = self.nc, self.em, self.pb
        em.wait("t", self.vtok())
        tok0 = hf * HALF + pair * 256
        with self.sb("wob", [128, 2 * 2048], BF16) as wob:
            for o in range(16):
                bi = 4 + o % 2
                self.use_bank(bi)
                last = None
                for istrip in range(2):
                    wb = wob[:, istrip * 2048:(istrip + 1) * 2048]
                    self.sguard("g", (id(wob), istrip))
                    wt = em.dma("g", wb.rearrange("p (f i) -> p f i", f=16),
                                apx(self.w_out, (istrip * 16 * 128) * D + o * 128,
                                    [[D, 128], [128 * D, 16], [1, 128]]), ("dwob", istrip))
                    em.wait("t", wt)
                    for f in range(16):
                        ii = istrip * 16 + f
                        lhs = wb[:, f * 128:(f + 1) * 128]
                        em.t.ldweights(lhs)
                        last = em.t.matmul(pb[bi][:, 0:256], lhs, nfm[:, ii * 256:(ii + 1) * 256],
                                           start=(ii == 0), stop=(ii == 31)).inc("st")
                    self.sset((id(wob), istrip), last)
                em.wait("v", last)
                gofs = o * T + tok0
                gsl = self.gdelta[:, gofs:gofs + 256]
                em.v.tensor_mul(a_scr[:, 0:256], pb[bi][:, 0:256], gsl)
                tvv = self.vtok()
                self.bank_done(bi, tvv)
                self.vf()
                em.v.tensor_copy(gsl, a_scr[:, 0:256])
                self.vf()

    def l1(self):
        nc, em, pb, prm = self.nc, self.em, self.pb, self.prm
        for f in range(16):
            em.dma("q", prm[:, 224 + f:225 + f], self.l1_D[f * 128:(f + 1) * 128, None], "l1d")
        self.l1d_tok = em.tok("l1d")
        with nc.sbuf_tensor("x1n", [128, 16 * T], BF16) as x1n:
            with nc.sbuf_tensor("x1f", [128, 16 * T], F32) as x1f, \
                 nc.sbuf_tensor("l1scr", [128, 2 * D], F32) as l1scr:
                for c in range(8):
                    buf = l1scr[:, (c % 2) * D:(c % 2) * D + D]
                    self.sguard("q", ("l1x", c % 2))
                    tl = em.dma("q", buf, self.x_in[c * 128:(c + 1) * 128, :], ("dl1x", c % 2))
                    em.wait("t", tl)
                    for fq in range(4):
                        bi = fq % 2
                        self.use_bank(bi)
                        last = None
                        for k in range(4):
                            f = fq * 4 + k
                            last = em.t.transpose(pb[bi][:, k * 128:(k + 1) * 128],
                                                  buf[:, f * 128:f * 128 + 128], self.idf[:]).inc("st")
                        em.wait("v", last)
                        out_ap = self.fm_out_ap(x1f, fq, c * 128, T)
                        g_ap = self.fm_out_ap(self.gdelta, fq, c * 128, T)
                        tv = em.v.tensor_add(out_ap, pb[bi][:].rearrange("p (a b) -> p a b", a=4),
                                             g_ap).inc("sv")
                        self.bank_done(bi, tv)
                        if fq == 3:
                            self.sset(("l1x", c % 2), tv)
                em.wait("t", self.vtok())
                em.wait("s", self.vtok())
                for tt_i in range(2):
                    self.use_bank(2)
                    last = None
                    for f in range(16):
                        last = em.t.matmul(pb[2][0:1, 0:512], self.onef[:, 0:1],
                                           x1f[:, f * T + tt_i * 512:f * T + tt_i * 512 + 512],
                                           start=(f == 0), stop=(f == 15)).inc("st")
                    em.wait("s", last)
                    ts = em.s.mul(l1scr[0:1, tt_i * 512:tt_i * 512 + 512],
                                  pb[2][0:1, 0:512], 1.0 / D).inc("ss")
                    self.bank_done(2, ts)
                for tt_i in range(2):
                    self.use_bank(3)
                    last = None
                    for f in range(16):
                        tq = em.s.activation(l1scr[:, D:D + 512],
                                             x1f[:, f * T + tt_i * 512:f * T + tt_i * 512 + 512],
                                             AF.Square).inc("ss")
                        em.wait("t", tq)
                        last = em.t.matmul(pb[3][0:1, 0:512], self.onef[:, 0:1],
                                           l1scr[:, D:D + 512], start=(f == 0),
                                           stop=(f == 15)).inc("st")
                        em.wait("s", last)
                    ts = em.s.mul(l1scr[0:1, 1024 + tt_i * 512:1024 + tt_i * 512 + 512],
                                  pb[3][0:1, 0:512], 1.0 / D).inc("ss")
                    self.bank_done(3, ts)
                em.wait("v", self.stok())
                em.v.tensor_mul(l1scr[0:1, D:D + T], l1scr[0:1, 0:T], l1scr[0:1, 0:T])
                self.vf()
                em.v.tensor_sub(l1scr[0:1, 1024:1024 + T], l1scr[0:1, 1024:1024 + T],
                                l1scr[0:1, D:D + T])
                tv = self.vtok()
                self.vf()
                em.wait("s", tv)
                em.s.activation(l1scr[0:1, D:D + T], l1scr[0:1, 1024:1024 + T], AF.Ln, bias=self.prm[0:1, 504:505])
                self.sf()
                ts = em.s.activation(l1scr[0:1, 1024:1024 + T], l1scr[0:1, D:D + T],
                                     AF.Exp, scale=-0.5).inc("ss")
                em.wait("t", ts)
                tmm = None
                for half in range(2):
                    self.use_bank(2 + half)
                    em.t.matmul(pb[2 + half][:, 0:512], self.onef[0:1, :],
                                l1scr[0:1, half * 512:half * 512 + 512],
                                start=True, stop=True).inc("st")
                    self.use_bank(4 + half)
                    tmm = em.t.matmul(pb[4 + half][:, 0:512], self.onef[0:1, :],
                                      l1scr[0:1, 1024 + half * 512:1024 + half * 512 + 512],
                                      start=True, stop=True).inc("st")
                em.wait("v", tmm)
                for half in range(2):
                    em.v.tensor_copy(l1scr[:, D + half * 512:D + half * 512 + 512],
                                     pb[2 + half][:, 0:512])
                    em.v.tensor_copy(l1scr[:, D + T + half * 512:D + T + half * 512 + 512],
                                     pb[4 + half][:, 0:512])
                lastent = None
                for entry in reversed(em.streams["v"]):
                    if entry[0] not in ("wait_ge", "nop"):
                        lastent = entry
                        break
                tv = _Pending(em, lastent).inc("sv")
                em.wait("v", tv)
                for bi in (2, 3, 4, 5):
                    self.bank_done(bi, tv)
                meanb = l1scr[:, D:D + T]
                istdb = l1scr[:, D + T:D + 2 * T]
                for f in range(16):
                    em.v.tensor_sub(x1f[:, f * T:(f + 1) * T], x1f[:, f * T:(f + 1) * T], meanb)
                    self.vf()
                    em.v.tensor_mul(x1n[:, f * T:(f + 1) * T], x1f[:, f * T:(f + 1) * T], istdb)
                    self.vf()
                self.tap("x1n", x1n[:], self.vtok())
            with nc.sbuf_tensor("cwfm", [32, T], BF16) as cwfm, \
                 nc.sbuf_tensor("cwsfm", [16, T], BF16) as cwsfm:
                self.routing(x1n, cwfm, cwsfm)
                em.barrier()
                with nc.sbuf_tensor("moe", [128, 16 * T], BF16) as moe:
                    with nc.sbuf_tensor("oh32sb", [32, 32 * 128], BF16) as oh32, \
                         nc.sbuf_tensor("oh16sb", [16, 16 * 128], BF16) as oh16, \
                         nc.sbuf_tensor("a_e", [128, 6 * T], BF16) as a_e, \
                         nc.sbuf_tensor("a_s", [128, T], BF16) as a_s, \
                         nc.sbuf_tensor("wgu", [128, 4 * 4096], BF16) as wgu, \
                         nc.sbuf_tensor("wdb", [128, 2 * 3072], BF16) as wdb, \
                         nc.sbuf_tensor("wsm", [128, 2048 + 2048], BF16) as wsm, \
                         nc.sbuf_tensor("cwb", [128, T], BF16) as cwb, \
                         nc.sbuf_tensor("cwsb", [128, T], BF16) as cwsb, \
                         nc.sbuf_tensor("mscr", [128, 2 * 512], BF16) as mscr:
                        em.dma("q", oh32[:], self.oh32_d, "oh32")
                        tl = em.dma("q", oh16[:], self.oh16s_d, "oh32")
                        em.wait("t", tl)
                        for e in range(NEXP):
                            self.moe_expert(e, x1n, cwfm, cwsfm, moe, a_e, a_s, wgu,
                                            wdb, wsm, cwb, cwsb, mscr, oh32, oh16)
                        self.tap("moe", moe[:], self.vtok())
                        em.wait("s", self.vtok())
                        for o in range(16):
                            for tt_i in range(2):
                                sl = moe[:, o * T + tt_i * 512:o * T + tt_i * 512 + 512]
                                em.s.activation(mscr[:, 0:512], sl, AF.Silu)
                                self.sf()
                                em.s.copy(sl, mscr[:, 0:512])
                                self.sf()
                        ts = self.stok()
                        em.wait("t", ts)
                        em.wait("v", ts)
                    self.l1_attn(x1n, moe)

    def routing(self, x1n, cwfm, cwsfm):
        nc, em, pb, prm = self.nc, self.em, self.pb, self.prm
        with nc.sbuf_tensor("gw", [32, D], BF16) as gw, \
             nc.sbuf_tensor("gwT", [128, 16 * 32], BF16) as gwT, \
             nc.sbuf_tensor("lgt", [32, T], F32) as lgt, \
             nc.sbuf_tensor("rs", [128, 8 * 32], F32) as rs, \
             nc.sbuf_tensor("rw", [128, 14 * 32], F32) as rw:
            em.wait("g", self.vtok())
            em.wait("g", self.stok())
            tl = em.dma("g", gw[:], self.gate_w, "gw")
            em.wait("t", tl)
            for f in range(16):
                bi = 6 + f % 2
                self.use_bank(bi)
                tt = em.t.transpose(pb[bi][:, 0:32], gw[:, f * 128:(f + 1) * 128],
                                    self.idb[0:32, 0:32]).inc("st")
                em.wait("v", tt)
                tv = em.v.tensor_copy(gwT[:, f * 32:(f + 1) * 32], pb[bi][:, 0:32]).inc("sv")
                self.bank_done(bi, tv)
            em.wait("t", self.vtok())
            for tt_i in range(2):
                self.use_bank(2)
                last = None
                for f in range(16):
                    lhs = gwT[:, f * 32:(f + 1) * 32]
                    em.t.ldweights(lhs)
                    last = em.t.matmul(pb[2][0:32, 0:512], lhs,
                                       x1n[:, f * T + tt_i * 512:f * T + tt_i * 512 + 512],
                                       start=(f == 0), stop=(f == 15)).inc("st")
                em.wait("v", last)
                tv = em.v.tensor_copy(lgt[:, tt_i * 512:tt_i * 512 + 512], pb[2][0:32, 0:512]).inc("sv")
                self.bank_done(2, tv)
            em.wait("t", self.vtok())
            for c in range(8):
                self.use_bank(3)
                tt = em.t.transpose(pb[3][:, 0:32], lgt[:, c * 128:(c + 1) * 128],
                                    self.idf[0:32, 0:32]).inc("st")
                em.wait("v", tt)
                self.sguard("v", "rwk")
                lt = rs[:, 0:32]
                cur = rs[:, 32:64]
                msk = rs[:, 64:96]
                ex = rs[:, 96:128]
                cwv = rs[:, 128:160]
                mx = rs[:, 160:165]
                tv = em.v.tensor_copy(lt, pb[3][:, 0:32]).inc("sv")
                self.bank_done(3, tv)
                self.vf()
                em.v.tensor_copy(cur, lt)
                self.vf()
                for j in range(4):
                    em.v.reduce_max(mx[:, j:j + 1], cur, axis=AX.X)
                    self.vf()
                    if j < 3:
                        em.v.tensor_scalar(msk, cur, mx[:, j:j + 1], None, ALU.is_ge)
                        self.vf()
                        em.v.scalar_tensor_tensor(cur, msk, -1e30, cur, ALU.mult, ALU.add)
                        self.vf()
                em.v.tensor_scalar_mul(mx[:, 4:5], mx[:, 0:1], -1.0)
                tv = self.vtok()
                self.vf()
                em.wait("s", tv)
                ts = em.s.activation(ex, lt, AF.Exp, bias=mx[:, 4:5]).inc("ss")
                em.wait("v", ts)
                em.v.tensor_scalar(msk, lt, mx[:, 3:4], None, ALU.is_ge)
                self.vf()
                em.v.tensor_mul(cwv, ex, msk)
                self.vf()
                em.v.reduce_sum(mx[:, 0:1], cwv, axis=AX.X)
                self.vf()
                em.v.reciprocal(mx[:, 1:2], mx[:, 0:1])
                self.vf()
                em.v.tensor_scalar(rw[:, 0:32], cwv, mx[:, 1:2], None, ALU.mult)
                self.vf()
                em.v.tensor_tensor(rw[:, 32:48], apx(rw, 0, [[14 * 32, 128], [2, 16]]),
                                   apx(rw, 1, [[14 * 32, 128], [2, 16]]), ALU.add)
                tv = self.vtok()
                self.vf()
                em.wait("t", tv)
                self.use_bank(3)
                ta = em.t.transpose(pb[3][0:32, 0:128], rw[:, 0:32], self.idf[:]).inc("st")
                em.wait("v", ta)
                tv = em.v.tensor_copy(cwfm[:, c * 128:(c + 1) * 128], pb[3][0:32, 0:128]).inc("sv")
                self.bank_done(3, tv)
                self.use_bank(3)
                ta = em.t.transpose(pb[3][0:16, 0:128], rw[:, 32:48], self.idf[:]).inc("st")
                em.wait("v", ta)
                tv = em.v.tensor_copy(cwsfm[:, c * 128:(c + 1) * 128], pb[3][0:16, 0:128]).inc("sv")
                self.bank_done(3, tv)
                self.sset("rwk", self.ttok())
            self.tap("cw", cwfm[:], self.vtok())

    def moe_expert(self, e, x1n, cwfm, cwsfm, moe, a_e, a_s, wgu, wdb, wsm, cwb,
                   cwsb, mscr, oh32, oh16):
        nc, em, pb = self.nc, self.em, self.pb
        small = (e % 2 == 0)
        k = e // 2
        em.wait("v", self.ttok())
        em.wait("t", self.vtok())
        for tt_i in range(2):
            self.use_bank(tt_i)
            lhs = oh32[:, e * 128:(e + 1) * 128]
            em.t.ldweights(lhs)
            tm = em.t.matmul(pb[tt_i][:], lhs, cwfm[:, tt_i * 512:tt_i * 512 + 512],
                             start=True, stop=True).inc("st")
            em.wait("v", tm)
            tv = em.v.tensor_copy(cwb[:, tt_i * 512:tt_i * 512 + 512], pb[tt_i][:]).inc("sv")
            self.bank_done(tt_i, tv)
            if small:
                self.use_bank(2 + tt_i)
                lhs = oh16[:, k * 128:(k + 1) * 128]
                em.t.ldweights(lhs)
                tm = em.t.matmul(pb[2 + tt_i][:], lhs, cwsfm[:, tt_i * 512:tt_i * 512 + 512],
                                 start=True, stop=True).inc("st")
                em.wait("v", tm)
                tv = em.v.tensor_copy(cwsb[:, tt_i * 512:tt_i * 512 + 512], pb[2 + tt_i][:]).inc("sv")
                self.bank_done(2 + tt_i, tv)
        em.wait("t", self.vtok())
        for third in range(3):
            for proj in range(2):
                sl4 = (third % 2) * 2 + proj
                wsrc = self.moe_wg if proj == 0 else self.moe_wu
                wb = wgu[:, sl4 * 4096:sl4 * 4096 + 4096]
                self.sguard("g", (id(wgu), sl4))
                em.dma("g", wb.rearrange("p (f i) -> p f i", f=16),
                       apx(wsrc, e * D * MOE_I + third * 256,
                           [[MOE_I, 128], [128 * MOE_I, 16], [1, 256]]), ("dwgu", sl4))
            wt = (em.tok(("dwgu", (third % 2) * 2)), em.tok(("dwgu", (third % 2) * 2 + 1)))
            em.wait("t", wt[0])
            em.wait("t", wt[1])
            for ic2 in range(2):
                ii = third * 2 + ic2
                for proj in range(2):
                    sl4 = (third % 2) * 2 + proj
                    wb = wgu[:, sl4 * 4096:sl4 * 4096 + 4096]
                    for tt_i in range(2):
                        bi = proj * 2 + tt_i
                        self.use_bank(bi)
                        last = None
                        for f in range(16):
                            lhs = wb[:, f * 256 + ic2 * 128:f * 256 + ic2 * 128 + 128]
                            em.t.ldweights(lhs)
                            last = em.t.matmul(pb[bi][:], lhs,
                                               x1n[:, f * T + tt_i * 512:f * T + tt_i * 512 + 512],
                                               start=(f == 0), stop=(f == 15)).inc("st")
                        self.sset((id(wgu), sl4), last)
                for tt_i in range(2):
                    bg = 0 + tt_i
                    bu = 2 + tt_i
                    tmlast = self.ttok()
                    em.wait("s", tmlast)
                    self.sguard("s", (id(mscr), tt_i % 2))
                    ts = em.s.activation(mscr[:, (tt_i % 2) * 512:(tt_i % 2) * 512 + 512],
                                         pb[bg][:], AF.Silu).inc("ss")
                    em.wait("v", ts)
                    em.v.tensor_mul(mscr[:, (tt_i % 2) * 512:(tt_i % 2) * 512 + 512],
                                    mscr[:, (tt_i % 2) * 512:(tt_i % 2) * 512 + 512], pb[bu][:])
                    self.vf()
                    tv = em.v.tensor_mul(a_e[:, ii * T + tt_i * 512:ii * T + tt_i * 512 + 512],
                                         mscr[:, (tt_i % 2) * 512:(tt_i % 2) * 512 + 512],
                                         cwb[:, tt_i * 512:tt_i * 512 + 512]).inc("sv")
                    self.sset((id(mscr), tt_i % 2), tv)
                    self.bank_done(bg, tv)
                    self.bank_done(bu, tv)
        if small:
            for proj in range(2):
                wsrc = self.cwg if proj == 0 else self.cwu
                wb = wsm[:, 0:2048]
                em.wait("g", self.ttok())
                wt = em.dma("g", wb.rearrange("p (f i) -> p f i", f=16),
                            apx(wsrc, k * D * CHI, [[CHI, 128], [128 * CHI, 16], [1, 128]]), "dw")
                em.wait("t", wt)
                for tt_i in range(2):
                    bi = proj * 2 + tt_i
                    self.use_bank(bi)
                    last = None
                    for f in range(16):
                        lhs = wb[:, f * 128:(f + 1) * 128]
                        em.t.ldweights(lhs)
                        last = em.t.matmul(pb[bi][:], lhs,
                                           x1n[:, f * T + tt_i * 512:f * T + tt_i * 512 + 512],
                                           start=(f == 0), stop=(f == 15)).inc("st")
                    if proj == 1:
                        em.wait("s", last)
                        ts = em.s.activation(mscr[:, tt_i * 512:tt_i * 512 + 512],
                                             pb[0 + tt_i][:], AF.Silu).inc("ss")
                        em.wait("v", ts)
                        em.v.tensor_mul(mscr[:, tt_i * 512:tt_i * 512 + 512],
                                        mscr[:, tt_i * 512:tt_i * 512 + 512], pb[2 + tt_i][:])
                        self.vf()
                        tv = em.v.tensor_mul(a_s[:, tt_i * 512:tt_i * 512 + 512],
                                             mscr[:, tt_i * 512:tt_i * 512 + 512],
                                             cwsb[:, tt_i * 512:tt_i * 512 + 512]).inc("sv")
                        self.bank_done(0 + tt_i, tv)
                        self.bank_done(2 + tt_i, tv)
            em.wait("g", self.ttok())
            wt = em.dma("g", wsm[:, 2048:4096].rearrange("p (o j) -> p o j", o=16),
                        apx(self.cwd, k * CHI * D, [[D, 128], [128, 16], [1, 128]]), "wsm2")
            self.small_wd_tok = wt
        em.wait("t", self.vtok())
        for oq in range(4):
            wb = wdb[:, (oq % 2) * 3072:(oq % 2) * 3072 + 3072]
            self.sguard("g", (id(wdb), oq % 2))
            wt = em.dma("g", wb.rearrange("p (i o) -> p i o", i=6),
                        apx(self.moe_wd, e * MOE_I * D + oq * 512,
                            [[D, 128], [128 * D, 6], [1, 512]]), ("dwd", oq % 2))
            em.wait("t", wt)
            if small:
                em.wait("t", self.small_wd_tok)
            for o4 in range(4):
                o = oq * 4 + o4
                for tt_i in range(2):
                    bi = 4 + (o * 2 + tt_i) % 2
                    self.use_bank(bi)
                    last = None
                    for ii in range(6):
                        lhs = wb[:, ii * 512 + o4 * 128:ii * 512 + o4 * 128 + 128]
                        em.t.ldweights(lhs)
                        last = em.t.matmul(pb[bi][:], lhs,
                                           a_e[:, ii * T + tt_i * 512:ii * T + tt_i * 512 + 512],
                                           start=(ii == 0), stop=(not small and ii == 5)).inc("st")
                    if small:
                        lhs = wsm[:, 2048 + o * 128:2048 + (o + 1) * 128]
                        em.t.ldweights(lhs)
                        last = em.t.matmul(pb[bi][:], lhs,
                                           a_s[:, tt_i * 512:tt_i * 512 + 512],
                                           start=False, stop=True).inc("st")
                    em.wait("v", last)
                    self.sset((id(wdb), oq % 2), last)
                    msl = moe[:, o * T + tt_i * 512:o * T + tt_i * 512 + 512]
                    if e == 0:
                        tv = em.v.tensor_copy(msl, pb[bi][:]).inc("sv")
                    else:
                        tv = em.v.tensor_add(msl, msl, pb[bi][:]).inc("sv")
                    self.bank_done(bi, tv)
                    self.vf()

    def l1_attn(self, x1n, g1):
        nc, em, pb, prm = self.nc, self.em, self.pb, self.prm
        with nc.sbuf_tensor("state", [128, 8 * T], BF16) as state, \
             nc.sbuf_tensor("bnat", [128, 2 * D], BF16) as bnat, \
             nc.sbuf_tensor("btn", [128, 2 * 2048], BF16) as btn, \
             nc.sbuf_tensor("lascr", [128, 3 * 512], F32) as lascr, \
             nc.sbuf_tensor("lbscr", [128, 512], BF16) as lbscr:
            em.wait("g", self.ttok())
            em.wait("g", self.vtok())
            em.wait("g", self.stok())
            for nc8 in range(8):
                bb = bnat[:, (nc8 % 2) * D:(nc8 % 2) * D + D]
                self.sguard("g", (id(bnat), nc8 % 2))
                tl = em.dma("g", bb, self.l1_B[nc8 * 128:(nc8 + 1) * 128, :], ("dbn", nc8 % 2))
                em.wait("t", tl)
                bt = btn[:, (nc8 % 2) * 2048:(nc8 % 2) * 2048 + 2048]
                self.tr16(bb, [bt[:, fq * 512:(fq + 1) * 512] for fq in range(4)],
                          self.idb[:], (6, 7))
                self.sset((id(bnat), nc8 % 2), self.ttok())
                em.wait("t", self.vtok())
                for tt_i in range(2):
                    bi = 2 + tt_i
                    self.use_bank(bi)
                    last = None
                    for f in range(16):
                        lhs = bt[:, f * 128:(f + 1) * 128]
                        em.t.ldweights(lhs)
                        last = em.t.matmul(pb[bi][:], lhs,
                                           x1n[:, f * T + tt_i * 512:f * T + tt_i * 512 + 512],
                                           start=(f == 0), stop=(f == 15)).inc("st")
                    em.wait("s", last)
                    ts = em.s.activation(state[:, nc8 * T + tt_i * 512:nc8 * T + tt_i * 512 + 512],
                                         pb[bi][:], AF.Tanh).inc("ss")
                    self.bank_done(bi, ts)
                    self.sset((id(btn), nc8 % 2), last)
            em.wait("t", self.stok())
            for dch in range(16):
                cb = bnat[:, (dch % 2) * D:(dch % 2) * D + D][:, 0:1024]
                self.sguard("g", (id(bnat), dch % 2))
                tl = em.dma("g", cb, self.l1_C[dch * 128:(dch + 1) * 128, :], ("dbn", dch % 2))
                em.wait("t", tl)
                ct = btn[:, (dch % 2) * 2048:(dch % 2) * 2048 + 1024]
                self.sguard("t", (id(btn), dch % 2))
                for fq in range(2):
                    bi = 6 + fq % 2
                    self.use_bank(bi)
                    last = None
                    for kk in range(4):
                        nb = fq * 4 + kk
                        last = em.t.transpose(pb[bi][:, kk * 128:(kk + 1) * 128],
                                              cb[:, nb * 128:nb * 128 + 128], self.idb[:]).inc("st")
                    em.wait("v", last)
                    tv = em.v.tensor_copy(ct[:, fq * 512:(fq + 1) * 512], pb[bi][:, 0:512]).inc("sv")
                    self.bank_done(bi, tv)
                em.wait("t", self.vtok())
                for tt_i in range(2):
                    bi = 2 + tt_i
                    self.use_bank(bi)
                    last = None
                    for nc8 in range(8):
                        lhs = ct[:, nc8 * 128:(nc8 + 1) * 128]
                        em.t.ldweights(lhs)
                        last = em.t.matmul(pb[bi][:], lhs,
                                           state[:, nc8 * T + tt_i * 512:nc8 * T + tt_i * 512 + 512],
                                           start=(nc8 == 0), stop=(nc8 == 7)).inc("st")
                    em.wait("v", last)
                    em.wait("v", self.l1d_tok)
                    self.sset((id(bnat), dch % 2), last)
                    self.sset((id(btn), dch % 2), last)
                    xsl = x1n[:, dch * T + tt_i * 512:dch * T + tt_i * 512 + 512]
                    gsl = g1[:, dch * T + tt_i * 512:dch * T + tt_i * 512 + 512]
                    dsl = self.gdelta[:, dch * T + tt_i * 512:dch * T + tt_i * 512 + 512]
                    em.v.scalar_tensor_tensor(lascr[:, 0:512], xsl, prm[:, 224 + dch:225 + dch],
                                              pb[bi][:], ALU.mult, ALU.add)
                    tvv = self.vtok()
                    self.bank_done(bi, tvv)
                    self.vf()
                    em.v.tensor_mul(lascr[:, 512:1024], lascr[:, 0:512], gsl)
                    self.vf()
                    em.v.tensor_add(lbscr[:, 0:512], lascr[:, 512:1024], dsl)
                    self.vf()
                    em.v.tensor_copy(dsl, lbscr[:, 0:512])
                    self.vf()
        self.tap("delta1", self.gdelta[:], self.vtok())

    def outstage(self):
        nc, em, pb = self.nc, self.em, self.pb
        with nc.sbuf_tensor("oxt", [128, 2 * D], F32) as oxt, \
             nc.sbuf_tensor("odt", [128, D], BF16) as odt, \
             nc.sbuf_tensor("oot", [128, 2 * D], F32) as oot:
            for c in range(8):
                buf = oxt[:, (c % 2) * D:(c % 2) * D + D]
                self.sguard("q", ("ox", c % 2))
                tl = em.dma("q", buf, self.x_in[c * 128:(c + 1) * 128, :], ("dox", c % 2))
                for fq in range(4):
                    bi = 6 + fq % 2
                    self.use_bank(bi)
                    last = None
                    for k in range(4):
                        f = fq * 4 + k
                        last = em.t.transpose(pb[bi][:, k * 128:(k + 1) * 128],
                                              self.gdelta[:, f * T + c * 128:f * T + c * 128 + 128],
                                              self.idb[:]).inc("st")
                    em.wait("v", last)
                    tv = em.v.tensor_copy(odt[:, fq * 512:(fq + 1) * 512], pb[bi][:, 0:512]).inc("sv")
                    self.bank_done(bi, tv)
                em.wait("v", tl)
                self.vf()
                ob = oot[:, (c % 2) * D:(c % 2) * D + D]
                self.sguard("v", ("oo", c % 2))
                tv = em.v.tensor_add(ob, buf, odt[:]).inc("sv")
                self.vf()
                self.sset(("ox", c % 2), tv)
                em.wait("q", tv)
                td = em.dma("q", self.out[c * 128:(c + 1) * 128, :], ob, ("dout", c % 2))
                self.sset(("oo", c % 2), td)
        em.wait("q", em.tok(("dout", 0)))
        em.wait("q", em.tok(("dout", 1)))
        if "do" in em.sems:
            em.wait("q", em.tok("do"))

    def finish(self):
        em, nc = self.em, self.nc
        with nc.Block() as block:
            @block.tensor
            def _(e):
                em.replay("t", e)

            @block.vector
            def _(e):
                em.replay("v", e)

            @block.scalar
            def _(e):
                em.replay("s", e)

            @block.sync
            def _(e):
                em.replay("q", e)

            @block.gpsimd
            def _(e):
                em.replay("g", e)
        for cm in self.sem_ctxs:
            cm.__exit__(None, None, None)


def make_consts():
    jj, ii = np.meshgrid(np.arange(128), np.arange(128), indexing="ij")
    oh8 = np.zeros((8, 8 * 128), dtype=np.float32)
    for c in range(8):
        oh8[c, c * 128:(c + 1) * 128] = 1.0
    oh32 = np.zeros((32, 32 * 128), dtype=ml_dtypes.bfloat16)
    for c in range(32):
        oh32[c, c * 128:(c + 1) * 128] = 1.0
    oh16s = np.zeros((16, 16 * 128), dtype=ml_dtypes.bfloat16)
    for c in range(16):
        oh16s[c, c * 128:(c + 1) * 128] = 0.05
    return dict(
        ident_f=np.eye(128, dtype=np.float32),
        ident_b=np.eye(128, dtype=ml_dtypes.bfloat16),
        ones_f=np.ones((128, 128), dtype=np.float32),
        triu_neg=np.where(jj <= ii, 0.0, -1e30).astype(np.float32),
        oh8=oh8, oh32=oh32, oh16s=oh16s)


_BUILD_CACHE = {}


def get_build(dbg_names=()):
    key = tuple(sorted(dbg_names))
    if key not in _BUILD_CACHE:
        kb = KB(dbg_names)
        kb.build()
        _BUILD_CACHE[key] = kb
    return _BUILD_CACHE[key]


def make_in_maps(inputs, dbg_names=()):
    consts = make_consts()
    x = np.ascontiguousarray(inputs["L0_attn_u"].reshape(8192, D))
    nw_b = np.ascontiguousarray(
        np.broadcast_to(inputs["L0_attn_norm_weight"][None, :], (128, D_IN)))
    zeros_x = np.zeros((T, D), dtype=np.float32)
    zeros_cb = np.zeros((CONVD,), dtype=np.float32)
    in_maps = []
    for core in range(8):
        b = core // 2
        hh = core % 2
        rows = slice(b * 2048 + hh * T, b * 2048 + (hh + 1) * T)
        fr_rows = slice(b * 2048, b * 2048 + T)
        m = dict(
            x_in=np.ascontiguousarray(x[rows]),
            x_front=(np.ascontiguousarray(x[fr_rows]) if hh == 1 else zeros_x),
            fr_conv_b=(inputs["L0_attn_conv_b"] if hh == 1 else zeros_cb),
            w_in=inputs["L0_attn_W_in"],
            conv_w=inputs["L0_attn_conv_w"],
            conv_b=inputs["L0_attn_conv_b"],
            dt_bias=inputs["L0_attn_dt_bias"],
            a_log=inputs["L0_attn_A_log"],
            d_vec=inputs["L0_attn_D"],
            w_out=inputs["L0_attn_W_out"],
            ffn_gamma=inputs["L0_ffn_gamma"],
            ffn_w=inputs["L0_ffn_W"],
            l1_B=inputs["L1_attn_B"],
            l1_C=inputs["L1_attn_C"],
            l1_D=inputs["L1_attn_D"],
            gate_w=inputs["L1_ffn_gate_w"],
            moe_wg=inputs["L1_ffn_Wg"],
            moe_wu=inputs["L1_ffn_Wu"],
            moe_wd=inputs["L1_ffn_Wd"],
            cwg=inputs["L1_ffn_cWg"],
            cwu=inputs["L1_ffn_cWu"],
            cwd=inputs["L1_ffn_cWd"],
            nw_bcast=nw_b,
            **consts,
        )
        in_maps.append(m)
    return in_maps


def kernel(**inputs):
    kb = get_build()
    in_maps = make_in_maps(inputs)
    res = run_bass_kernel_spmd(kb.nc, in_maps, core_ids=list(range(8)))
    out = np.empty((8192, D), dtype=np.float32)
    for core in range(8):
        b = core // 2
        hh = core % 2
        out[b * 2048 + hh * T:b * 2048 + (hh + 1) * T] = res.results[core]["out"]
    return out.reshape(4, 2048, D)


# version 04 (final)
# speedup vs baseline: 1.1186x; 1.0206x over previous
import numpy as np
import ml_dtypes
import concourse.bass as bass
import concourse.mybir as mybir
from concourse.bass_utils import run_bass_kernel_spmd

F32 = mybir.dt.float32
BF16 = mybir.dt.bfloat16
AF = mybir.ActivationFunctionType
ALU = mybir.AluOpType
AX = mybir.AxisListType
MAGIC = 12582912.0

D = 2048
T = 1024
HALF = 512
D_IN = 4096
NH = 64
DS = 64
CONVD = 6144
NPROJ = 10304
MOE_I = 768
NEXP = 32
NCHK = 16
CHI = 128


def apx(t, off, dims):
    a = t if isinstance(t, bass.AP) else t.ap()
    return bass.AP(a.tensor, a.offset + off, [list(d) for d in dims])


class _Pending:
    __slots__ = ("entry", "em")

    def __init__(self, em, entry):
        self.em = em
        self.entry = entry

    def inc(self, semname, amt=1):
        toks = self.entry[4]
        if semname in toks:
            return (semname, toks[semname])
        sem = self.em.sems[semname]
        sem.n += amt
        self.entry[3].append((sem.h, amt))
        toks[semname] = sem.n
        return (semname, sem.n)


class _EngRec:
    def __init__(self, em, name):
        self._em = em
        self._name = name

    def __getattr__(self, meth):
        em = self._em
        name = self._name

        def call(*a, **kw):
            if meth == "transpose":
                assert a[0].tensor.dtype == a[1].tensor.dtype, \
                    ("transpose dtype", a[0].tensor.dtype, a[1].tensor.dtype)
            entry = [meth, a, kw, [], {}]
            em.streams[name].append(entry)
            return _Pending(em, entry)
        return call


class Sem:
    def __init__(self, handle):
        self.h = handle
        self.n = 0


class Em:
    ENGS = ("t", "v", "s", "q", "g")

    def __init__(self, nc, sems, dma_pool):
        self.nc = nc
        self.streams = {e: [] for e in self.ENGS}
        self.sems = sems
        self.dma_pool = dma_pool
        self.memo = {}
        for e in self.ENGS:
            setattr(self, e, _EngRec(self, e))

    def eng(self, name):
        return getattr(self, name)

    def dma_key(self, key):
        if key not in self.sems:
            if isinstance(key, str) and key in ("dw", "dx", "do", "dsh"):
                pass
            if not self.dma_pool:
                raise RuntimeError("dma sem pool exhausted")
            self.sems[key] = self.dma_pool.pop()
        return key

    def dma(self, eng, out, in_, sem="dw", **kw):
        key = self.dma_key(sem)
        return self.eng(eng).dma_start(out=out, in_=in_, **kw).inc(key, 16)

    def wait(self, eng, tok):
        if tok is None:
            return
        name, val = tok
        key = (eng, name)
        if self.memo.get(key, -1) >= val:
            return
        self.memo[key] = val
        self.streams[eng].append(["wait_ge", (self.sems[name].h, val), {}, [], {}])

    def tok(self, semname):
        return (semname, self.sems[semname].n)

    def barrier(self, engs=ENGS):
        tagmap = {"t": "st", "v": "sv", "s": "ss", "q": "sq", "g": "sg"}
        toks = []
        for e in engs:
            self.wait(e, (tagmap[e], self.sems[tagmap[e]].n))
            toks.append(self.eng(e).nop().inc(tagmap[e]))
        for e in engs:
            for tk in toks:
                self.wait(e, tk)

    def replay(self, engname, engine):
        for idx, (meth, a, kw, incs, _toks) in enumerate(self.streams[engname]):
            try:
                ins = getattr(engine, meth)(*a, **kw)
            except Exception as ex:
                raise RuntimeError("replay fail %s[%d] %s kw=%s: %s"
                                   % (engname, idx, meth, kw, ex)) from ex
            for h, amt in incs:
                ins.then_inc(h, amt)


class KB:
    def __init__(self, dbg_names=()):
        self.nc = bass.Bass("TRN2", target_bir_lowering=False)
        self.dbgreq = set(dbg_names)
        self.dbg = {}
        self.sem_ctxs = []
        sems = {}
        for n in ["st", "sv", "ss", "sq", "sg", "do"]:
            cm = self.nc.semaphore(n)
            sems[n] = Sem(cm.__enter__())
            self.sem_ctxs.append(cm)
        dma_pool = []
        for i in range(90):
            cm = self.nc.semaphore("dp%d" % i)
            dma_pool.append(Sem(cm.__enter__()))
            self.sem_ctxs.append(cm)
        self.em = Em(self.nc, sems, dma_pool)
        self.bank_free = [None] * 8
        self.slot_tok = {}
        self._uid = 0
        self._io()

    def _io(self):
        nc = self.nc

        def din(name, shape, dt=F32):
            return nc.dram_tensor(name, list(shape), dt, kind="ExternalInput").ap()
        self.x_in = din("x_in", (T, D))
        self.x_front = din("x_front", (T, D))
        self.w_in = din("w_in", (D, NPROJ))
        self.conv_w = din("conv_w", (CONVD, 4))
        self.conv_b = din("conv_b", (CONVD,))
        self.fr_conv_b = din("fr_conv_b", (CONVD,))
        self.dt_bias = din("dt_bias", (NH,))
        self.a_log = din("a_log", (NH,))
        self.d_vec = din("d_vec", (NH,))
        self.w_out = din("w_out", (D_IN, D))
        self.ffn_gamma = din("ffn_gamma", (D,))
        self.ffn_w = din("ffn_w", (D, D))
        self.l1_B = din("l1_B", (1024, D))
        self.l1_C = din("l1_C", (D, 1024))
        self.l1_D = din("l1_D", (D,))
        self.gate_w = din("gate_w", (NEXP, D))
        self.moe_wg = din("moe_wg", (NEXP, D, MOE_I))
        self.moe_wu = din("moe_wu", (NEXP, D, MOE_I))
        self.moe_wd = din("moe_wd", (NEXP, MOE_I, D))
        self.cwg = din("cwg", (NCHK, D, CHI))
        self.cwu = din("cwu", (NCHK, D, CHI))
        self.cwd = din("cwd", (NCHK, CHI, D))
        self.ident_f = din("ident_f", (128, 128))
        self.ident_b = din("ident_b", (128, 128), BF16)
        self.ones_f = din("ones_f", (128, 128))
        self.triu_neg_d = din("triu_neg", (128, 128))
        self.oh8_d = din("oh8", (8, 8 * 128))
        self.oh32_d = din("oh32", (32, 32 * 128), BF16)
        self.oh16s_d = din("oh16s", (16, 16 * 128), BF16)
        self.nw_bcast_d = din("nw_bcast", (128, D_IN))
        self.out = nc.dram_tensor("out", [T, D], F32, kind="ExternalOutput").ap()
        self.xn_dram = nc.dram_tensor("xn_scr", [16 * 128 * T], BF16, kind="Internal").ap()
        self.cum_dram = nc.dram_tensor("cum_scr", [NH * 128], F32, kind="Internal").ap()
        self.ec_dram = nc.dram_tensor("ec_scr", [NH * 128], F32, kind="Internal").ap()
        self.sz_dram = nc.dram_tensor("sz_scr", [T * D_IN], BF16, kind="Internal").ap()
        self.c64_dram = nc.dram_tensor("c64_scr", [64 * NH * HALF], BF16, kind="Internal").ap()
        self.b64_dram = nc.dram_tensor("b64_scr", [64 * 16 * HALF], BF16, kind="Internal").ap()

    def tap(self, name, ap_, tok=None, eng="q"):
        if name not in self.dbgreq:
            return
        shp = list(ap_.shape)
        o = self.nc.dram_tensor("dbg_" + name, shp, ap_.tensor.dtype, kind="ExternalOutput").ap()
        self.dbg[name] = "dbg_" + name
        if tok is not None:
            self.em.wait(eng, tok)
        ttap = self.em.dma(eng, o, ap_, "do")
        for e in Em.ENGS:
            self.em.wait(e, ttap)

    def etok(self, eng, sem):
        em = self.em
        for entry in reversed(em.streams[eng]):
            if entry[0] not in ("wait_ge", "nop"):
                return _Pending(em, entry).inc(sem)
        return None

    def vtok(self):
        return self.etok("v", "sv")

    def stok(self):
        return self.etok("s", "ss")

    def ttok(self):
        return self.etok("t", "st")

    def vf(self):
        self.em.wait("v", self.vtok())

    def sf(self):
        self.em.wait("s", self.stok())

    def use_bank(self, i):
        self.em.wait("t", self.bank_free[i])

    def sb(self, nm, shape, dt):
        self._uid += 1
        return self.nc.sbuf_tensor("%s_u%d" % (nm, self._uid), shape, dt)

    def sguard(self, eng, key):
        for tok in self.slot_tok.get(key, ()):
            self.em.wait(eng, tok)

    def sset(self, key, *toks):
        self.slot_tok[key] = [t for t in toks if t is not None]

    def bank_done(self, i, tok):
        self.bank_free[i] = tok

    def tr16(self, src, outs, ident, banks):
        em, pb = self.em, self.pb
        for fq in range(4):
            bi = banks[fq % 2]
            self.use_bank(bi)
            last = None
            for k in range(4):
                f = fq * 4 + k
                last = em.t.transpose(pb[bi][:, k * 128:(k + 1) * 128],
                                      src[:, f * 128:f * 128 + 128], ident).inc("st")
            em.wait("v", last)
            evsrc = pb[bi][:, 0:512]
            if len(outs[fq].shape) == 3:
                evsrc = evsrc.rearrange("p (a b) -> p a b", a=4)
            tv = em.v.tensor_copy(outs[fq], evsrc).inc("sv")
            self.bank_done(bi, tv)

    def fm_out_ap(self, dst, fq, tok_off, stride):
        return apx(dst, (fq * 4) * stride + tok_off,
                   [[16 * stride, 128], [stride, 4], [1, 128]])

    def build(self, until="all"):
        nc, em = self.nc, self.em
        self.idf = nc.alloc_sbuf_tensor("idf", [128, 128], F32)
        self.idb = nc.alloc_sbuf_tensor("idb", [128, 128], BF16)
        self.onef = nc.alloc_sbuf_tensor("onef", [128, 128], F32)
        self.oh8 = nc.alloc_sbuf_tensor("oh8_sb", [8, 8 * 128], F32)
        self.prm = nc.alloc_sbuf_tensor("prm", [128, 512], F32)
        self.convp = nc.alloc_sbuf_tensor("convp", [128, 288], F32)
        self.gdelta = nc.alloc_sbuf_tensor("gdelta", [128, 16 * T], BF16)
        self.pb = [nc.alloc_psum_tensor("pb%d" % i, [128, 512], F32) for i in range(6)]
        self.pb.append(nc.alloc_psum_tensor("pbb0", [128, 1024], BF16))
        self.pb.append(nc.alloc_psum_tensor("pbb1", [128, 1024], BF16))
        self.front_limit = "all"
        if until in ("front_ln", "front_winx", "front_dt", "front_tok", "front_S", "front_dtA", "front_dtB", "front_dtC"):
            self.front_limit = until[6:]
            until = "front"
        order = ["consts", "ffn", "front", "l0", "l1"]
        stop = len(order) if until == "all" else order.index(until) + 1
        todo = set(order[:stop])
        self.consts()
        if "ffn" in todo:
            self.l0_ln_ffn()
            em.barrier()
        if "front" in todo or "l0" in todo:
            with nc.sbuf_tensor("H_ar", [128, NH * DS + 512], F32) as H_ar, \
                 nc.sbuf_tensor("xpre", [128, 48], BF16) as xpre:
                self.H = H_ar
                self.xpre = xpre
                em.v.memset(xpre[:], 0.0)
                self.vf()
                if "front" in todo:
                    self.front()
                    em.barrier()
                if "l0" in todo:
                    self.l0_attn()
        em.barrier()
        if "l1" in todo:
            self.l1()
            em.barrier()
        self.outstage()
        self.finish()
        return self

    def consts(self):
        em, prm, pb = self.em, self.prm, self.pb
        em.dma("q", self.idf[:], self.ident_f, "consts")
        em.dma("q", self.idb[:], self.ident_b, "consts")
        em.dma("q", self.onef[:], self.ones_f, "consts")
        em.dma("q", self.oh8[:], self.oh8_d, "consts")
        for f in range(16):
            em.dma("q", prm[:, f:f + 1], self.ffn_gamma[f * 128:(f + 1) * 128, None], "consts")
        em.dma("q", prm[0:NH, 240:241], self.dt_bias[:, None], "consts")
        em.dma("q", prm[0:NH, 241:242], self.a_log[:, None], "consts")
        em.dma("q", prm[0:NH, 243:244], self.d_vec[:, None], "consts")
        em.dma("q", self.convp[:, 0:192].rearrange("p (c k) -> p c k", c=48),
               apx(self.conv_w, 0, [[4, 128], [128 * 4, 48], [1, 4]]), "consts")
        em.dma("q", apx(prm, 352, [[512, 48], [1, 128]]), 
               apx(self.conv_b, 0, [[128, 48], [1, 128]]), "cvb1")
        tok = em.tok("consts")
        tokb = em.tok("cvb1")
        for e in Em.ENGS:
            em.wait(e, tok)
            em.wait(e, tokb)
        t1 = em.t.transpose(pb[0][:, 0:48], prm[0:48, 352:480], self.idf[0:48, 0:48]).inc("st")
        em.wait("v", t1)
        tv = em.v.tensor_copy(self.convp[:, 192:240], pb[0][:, 0:48]).inc("sv")
        em.wait("q", tv)
        em.dma("q", apx(prm, 352, [[512, 48], [1, 128]]),
               apx(self.fr_conv_b, 0, [[128, 48], [1, 128]]), "cvb2")
        tok = em.tok("cvb2")
        em.wait("t", tok)
        t1 = em.t.transpose(pb[1][:, 0:48], prm[0:48, 352:480], self.idf[0:48, 0:48]).inc("st")
        em.wait("v", t1)
        tv = em.v.tensor_copy(self.convp[:, 240:288], pb[1][:, 0:48]).inc("sv")
        em.wait("t", tv)
        self.bank_done(0, tv)
        self.bank_done(1, tv)
        em.v.memset(prm[:, 504:505], 1e-5)
        em.v.memset(prm[:, 505:506], 1e-6)
        em.v.memset(prm[:, 506:507], 1.0)
        self.vf()
        em.s.activation(prm[0:NH, 244:245], prm[0:NH, 241:242], AF.Exp)
        self.sf()
        tk = em.s.mul(prm[0:NH, 242:243], prm[0:NH, 244:245], -1.0).inc("ss")
        em.wait("v", tk)
        em.wait("t", tk)
        t1 = em.t.transpose(pb[0][0:1, 0:NH], prm[0:NH, 243:244], self.idf[0:NH, 0:NH]).inc("st")
        em.wait("s", t1)
        t2 = em.s.copy(prm[0:1, 352:352 + NH], pb[0][0:1, 0:NH]).inc("ss")
        em.wait("t", t2)
        t3 = em.t.matmul(pb[1][:, 0:NH], self.onef[0:1, :], prm[0:1, 352:352 + NH],
                         start=True, stop=True).inc("st")
        em.wait("v", t3)
        t4 = em.v.tensor_copy(prm[:, 96:96 + NH], pb[1][:, 0:NH]).inc("sv")
        em.wait("t", t4)
        em.barrier()

    def l0_ln_ffn(self):
        nc, em, pb, prm = self.nc, self.em, self.pb, self.prm
        ln_mean, ln_istd, ln_var = prm[:, 16:24], prm[:, 24:32], prm[:, 32:40]
        cA, cB, cC = prm[:, 40:48], prm[:, 48:56], prm[:, 56:64]
        inv_as, rmsf = prm[:, 64:72], prm[:, 72:80]
        with nc.sbuf_tensor("xq", [128, 16 * T], BF16) as xq, \
             nc.sbuf_tensor("xtile", [128, 2 * D], F32) as xtile, \
             nc.sbuf_tensor("scr", [128, D], F32) as scr:
            with nc.sbuf_tensor("xn32", [128, 16 * T], F32) as xn32:
                for c in range(8):
                    buf = xtile[:, (c % 2) * D:(c % 2) * D + D]
                    self.sguard("q", ("xt", c % 2))
                    tl = em.dma("q", buf, self.x_in[c * 128:(c + 1) * 128, :], ("dxt", c % 2))
                    em.wait("v", tl)
                    em.wait("s", tl)
                    self.sguard("s", "scrw")
                    self.sguard("v", "scrw")
                    em.v.reduce_sum(cA[:, c:c + 1], buf, axis=AX.X)
                    ts = em.s.activation(scr[:], buf, AF.Square, accum_out=cB[:, c:c + 1]).inc("ss")
                    self.vf()
                    em.wait("v", ts)
                    em.v.tensor_scalar_mul(ln_mean[:, c:c + 1], cA[:, c:c + 1], 1.0 / D)
                    self.vf()
                    em.v.tensor_mul(cC[:, c:c + 1], ln_mean[:, c:c + 1], ln_mean[:, c:c + 1])
                    self.vf()
                    tv = em.v.tensor_scalar(ln_var[:, c:c + 1], cB[:, c:c + 1], 1.0 / D,
                                            cC[:, c:c + 1], ALU.mult, ALU.subtract).inc("sv")
                    em.wait("s", tv)
                    em.s.activation(cC[:, c:c + 1], ln_var[:, c:c + 1], AF.Ln, bias=self.prm[:, 504:505])
                    self.sf()
                    ti = em.s.activation(ln_istd[:, c:c + 1], cC[:, c:c + 1], AF.Exp, scale=-0.5).inc("ss")
                    em.wait("v", ti)
                    tn = em.v.tensor_scalar(scr[:], buf, ln_mean[:, c:c + 1], ln_istd[:, c:c + 1],
                                            ALU.subtract, ALU.mult).inc("sv")
                    self.sset(("xt", c % 2), tn, ts)
                    em.wait("t", tn)
                    self.tr16(scr, [self.fm_out_ap(xn32, fq, c * 128, T) for fq in range(4)],
                              self.idf[:], (0, 1))
                    self.sset("scrw", self.ttok())
                self.tap("xn32", xn32[:], self.vtok())
                afold = xtile[:, 0:T]
                self.vf()
                em.wait("v", self.ttok())
                for f in range(16):
                    tgt = afold if f == 0 else scr[:, 0:T]
                    em.v.tensor_scalar(tgt, xn32[:, f * T:(f + 1) * T], prm[:, f:f + 1],
                                       None, ALU.mult)
                    self.vf()
                    em.v.scalar_tensor_tensor(tgt, tgt, -1.0, tgt, ALU.mult, ALU.max)
                    self.vf()
                    if f > 0:
                        em.v.tensor_max(afold, afold, scr[:, 0:T])
                        self.vf()
                em.wait("t", self.vtok())
                for c in range(8):
                    self.use_bank(2)
                    tt = em.t.transpose(pb[2][:, 0:128], afold[:, c * 128:(c + 1) * 128],
                                        self.idf[:]).inc("st")
                    em.wait("v", tt)
                    tv = em.v.reduce_max(cA[:, c:c + 1], pb[2][:, 0:128], axis=AX.X).inc("sv")
                    self.bank_done(2, tv)
                self.vf()
                em.v.tensor_scalar_add(cC, ln_var, 1e-5)
                self.vf()
                em.v.reciprocal(prm[:, 44:52] if False else cB, cC)
                self.vf()
                em.v.tensor_mul(cB, ln_var, cB)
                tv = self.vtok()
                self.vf()
                em.wait("s", tv)
                em.s.activation(cC, cB, AF.Ln, bias=self.prm[:, 505:506])
                self.sf()
                ts = em.s.activation(rmsf, cC, AF.Exp, scale=-0.5).inc("ss")
                em.wait("v", ts)
                em.v.tensor_tensor(cB, rmsf, cA, ALU.mult)
                self.vf()
                em.v.tensor_scalar_add(cB, cB, 1e-5)
                self.vf()
                em.v.tensor_scalar_mul(inv_as, cB, 1.0 / 127.0)
                em.v.reciprocal(cC, cB)
                self.vf()
                em.v.tensor_scalar_mul(cC, cC, 127.0)
                self.vf()
                em.v.tensor_tensor(cB, rmsf, cC, ALU.mult)
                self.vf()
                ccb = xtile[:, 0:T]
                ivb = xtile[:, T:2 * T]
                self.tap("prmq", prm[:, 0:96], self.vtok())
                self.bcast8(cB, ccb)
                self.bcast8(inv_as, ivb)
                self.tap("ccb", ccb, self.vtok())
                self.tap("ivb", ivb, self.vtok())
                for f in range(16):
                    self.sguard("v", "scrw")
                    em.v.tensor_scalar(scr[:, 0:T], xn32[:, f * T:(f + 1) * T], prm[:, f:f + 1],
                                       None, ALU.mult)
                    self.vf()
                    em.v.tensor_tensor(scr[:, 0:T], scr[:, 0:T], ccb, ALU.mult)
                    self.vf()
                    em.v.tensor_scalar_add(scr[:, 0:T], scr[:, 0:T], MAGIC)
                    self.vf()
                    em.v.tensor_scalar_sub(xq[:, f * T:(f + 1) * T], scr[:, 0:T], MAGIC)
                    self.vf()
                self.tap("xq", xq[:], self.vtok())
                tv = self.vtok()
                em.wait("g", tv)
                for f in range(16):
                    em.dma("g", apx(self.xn_dram, f * 128 * T, [[T, 128], [1, T]]),
                           xn32[:, f * T:(f + 1) * T], "xnout")
                self.xnout_tok = em.tok("xnout")
            em.wait("q", self.xnout_tok)
            em.wait("q", self.vtok())
            with nc.sbuf_tensor("wtile", [128, 2 * D], F32) as wtile, \
                 nc.sbuf_tensor("wq_n", [128, D], BF16) as wq_n, \
                 nc.sbuf_tensor("wqT", [128, 2 * D], BF16) as wqT:
                wsum, wabs = prm[:, 40:41], prm[:, 41:42]
                for i in range(16):
                    buf = wtile[:, (i % 2) * D:(i % 2) * D + D]
                    self.sguard("q", ("wt", i % 2))
                    tl = em.dma("q", buf, self.ffn_w[i * 128:(i + 1) * 128, :], ("dwt", i % 2))
                    em.wait("v", tl)
                    tr = em.v.reduce_sum(wabs, buf, axis=AX.X, apply_absolute_value=True).inc("sv")
                    self.sset(("wt", i % 2), tr)
                    self.vf()
                    if i == 0:
                        em.v.tensor_copy(wsum, wabs)
                    else:
                        em.v.tensor_add(wsum, wsum, wabs)
                    self.vf()
                tv = self.vtok()
                em.wait("t", tv)
                self.use_bank(2)
                tt = em.t.transpose(pb[2][0:1, 0:128], wsum, self.idf[:]).inc("st")
                em.wait("v", tt)
                em.v.reduce_sum(prm[0:1, 490:491], pb[2][0:1, 0:128], axis=AX.X)
                self.bank_done(2, self.vtok())
                self.vf()
                em.v.tensor_scalar(prm[0:1, 491:492], prm[0:1, 490:491], 1.0 / (D * D), 1e-5,
                                   ALU.mult, ALU.add)
                self.vf()
                em.v.reciprocal(prm[0:1, 490:491], prm[0:1, 491:492])
                tv = self.vtok()
                em.wait("t", tv)
                self.vf()
                self.use_bank(2)
                t1 = em.t.matmul(pb[2][:, 0:2], self.onef[0:1, :], prm[0:1, 490:492],
                                 start=True, stop=True).inc("st")
                em.wait("v", t1)
                tv = em.v.tensor_copy(prm[:, 488:490], pb[2][:, 0:2]).inc("sv")
                self.bank_done(2, tv)
                em.wait("v", tv)
                em.wait("t", tv)
                em.wait("s", tv)
                wscale_b = prm[:, 488:489]
                winv_b = prm[:, 489:490]
                for o in range(16):
                    buf = wtile[:, (o % 2) * D:(o % 2) * D + D]
                    self.sguard("q", ("wt", o % 2))
                    tl = em.dma("q", buf, self.ffn_w[o * 128:(o + 1) * 128, :], ("dwt", o % 2))
                    em.wait("v", tl)
                    em.v.tensor_scalar(buf, buf, wscale_b, MAGIC, ALU.mult, ALU.add)
                    self.vf()
                    em.v.tensor_scalar_sub(buf, buf, MAGIC)
                    self.vf()
                    self.sguard("v", "wqn")
                    tv = em.v.tensor_scalar(wq_n[:], buf, 1.0, -1.0, ALU.min, ALU.max).inc("sv")
                    em.wait("t", tv)
                    self.tr16(wq_n, [wqT[:, fq * 512:(fq + 1) * 512] for fq in range(4)],
                              self.idb[:], (6, 7))
                    self.sset("wqn", self.ttok())
                    em.wait("t", self.vtok())
                    for tt_i in range(2):
                        bi = 2 + tt_i
                        self.use_bank(bi)
                        last = None
                        for f in range(16):
                            lhs = wqT[:, f * 128:(f + 1) * 128]
                            em.t.ldweights(lhs)
                            last = em.t.matmul(pb[bi][:], lhs,
                                               xq[:, f * T + tt_i * 512:f * T + tt_i * 512 + 512],
                                               start=(f == 0), stop=(f == 15)).inc("st")
                        em.wait("v", last)
                        self.sguard("v", ("fsil", o % 2))
                        em.v.tensor_tensor(buf[:, 0:512], pb[bi][:],
                                           ivb[:, tt_i * 512:tt_i * 512 + 512], ALU.mult)
                        tvv = self.vtok()
                        self.vf()
                        em.wait("s", tvv)
                        tsl = em.s.activation(self.gdelta[:, o * T + tt_i * 512:o * T + tt_i * 512 + 512],
                                              buf[:, 0:512], AF.Silu, scale=winv_b).inc("ss")
                        self.bank_done(bi, tsl)
                        self.sset(("wt", o % 2), tsl)
                        self.sset(("fsil", o % 2), tsl)
                self.tap("g0", self.gdelta[:], self.stok(), eng="q")

    def bcast8(self, cols, dst):
        em, pb, prm = self.em, self.pb, self.prm
        em.wait("t", self.vtok())
        self.use_bank(3)
        tt = em.t.transpose(pb[3][0:8, 0:128], cols, self.idf[:]).inc("st")
        em.wait("s", tt)
        ts = em.s.copy(prm[0:8, 352:480], pb[3][0:8, 0:128]).inc("ss")
        self.bank_done(3, ts)
        em.wait("t", ts)
        for c in range(8):
            self.use_bank(3)
            t1 = em.t.matmul(pb[3][:, 0:128], self.oh8[:, c * 128:(c + 1) * 128],
                             prm[0:8, 352:480], start=True, stop=True).inc("st")
            em.wait("v", t1)
            tv = em.v.tensor_copy(dst[:, c * 128:(c + 1) * 128], pb[3][:, 0:128]).inc("sv")
            self.bank_done(3, tv)
        em.wait("t", self.vtok())

    def front(self):
        nc, em = self.nc, self.em
        em.v.memset(self.H[0:64, :], 0.0)
        self.vf()
        with nc.sbuf_tensor("fxn", [128, 16 * HALF], BF16) as fxn, \
             nc.sbuf_tensor("fxB", [128, 16 * HALF], BF16) as fxB, \
             nc.sbuf_tensor("ftail", [128, 144], BF16) as ftail, \
             nc.sbuf_tensor("fdt", [128, 3456], F32) as fdt, \
             nc.sbuf_tensor("ftok", [128, 2 * T], BF16) as ftok, \
             nc.sbuf_tensor("fxdtd", [128, D_IN], BF16) as fxdtd, \
             nc.sbuf_tensor("fwbuf", [128, 2 * 2048], BF16) as fwbuf, \
             nc.sbuf_tensor("fcst", [128, 2 * 516], BF16) as fcst, \
             nc.sbuf_tensor("fdiag", [128, 4 * 128], BF16) as fdiag, \
             nc.sbuf_tensor("fscr", [128, 2 * D], F32) as fscr:
            em.v.memset(ftail[:], 0.0)
            self.vf()
            for fh in range(2):
                h0 = fh * HALF
                em.wait("s", self.vtok())
                em.wait("s", self.ttok())
                em.wait("v", self.ttok())
                for c in range(4):
                    buf = fscr[:, (c % 2) * D:(c % 2) * D + D]
                    ob = ftok[:, 0:D]
                    self.sguard("q", ("fx", c % 2))
                    tl = em.dma("q", buf, self.x_front[h0 + c * 128:h0 + (c + 1) * 128, :], ("dfx", c % 2))
                    em.wait("v", tl)
                    em.wait("s", tl)
                    em.v.reduce_sum(self.prm[:, 40:41], buf, axis=AX.X)
                    ts = em.s.activation(ob, buf, AF.Square, accum_out=self.prm[:, 41:42]).inc("ss")
                    self.vf()
                    em.wait("v", ts)
                    em.v.tensor_scalar_mul(self.prm[:, 42:43], self.prm[:, 40:41], 1.0 / D)
                    self.vf()
                    em.v.tensor_mul(self.prm[:, 43:44], self.prm[:, 42:43], self.prm[:, 42:43])
                    self.vf()
                    tv = em.v.tensor_scalar(self.prm[:, 44:45], self.prm[:, 41:42], 1.0 / D,
                                            self.prm[:, 43:44], ALU.mult, ALU.subtract).inc("sv")
                    em.wait("s", tv)
                    em.s.activation(self.prm[:, 45:46], self.prm[:, 44:45], AF.Ln, bias=self.prm[:, 504:505])
                    self.sf()
                    ti = em.s.activation(self.prm[:, 46:47], self.prm[:, 45:46], AF.Exp,
                                         scale=-0.5).inc("ss")
                    em.wait("v", ti)
                    em.v.tensor_scalar(buf, buf, self.prm[:, 42:43], self.prm[:, 46:47],
                                       ALU.subtract, ALU.mult)
                    self.vf()
                    self.sguard("v", "fwb")
                    tc = em.v.tensor_copy(fwbuf[:, 0:D], buf).inc("sv")
                    self.sset(("fx", c % 2), tc)
                    em.wait("t", tc)
                    self.tr16(fwbuf[:, 0:D], [self.fm_out_ap(fxn, fq, c * 128, HALF) for fq in range(4)],
                              self.idb[:], (6, 7))
                    self.sset("fwb", self.ttok())
                if fh == 1:
                    self.vf()
                    em.v.tensor_copy(self.xpre[:].rearrange("p (f k) -> p f k", f=16),
                                     apx(fxn, 509, [[16 * HALF, 128], [HALF, 16], [1, 3]]))
                    self.vf()
                if self.front_limit == "ln":
                    continue
                self.winx_pass(fxn, fxB, None, fdt, fwbuf, fcst, fdiag, ftail,
                               m_list=list(range(32, 48)), dt_too=True, front=True, h0=h0)
                if self.front_limit == "winx":
                    continue
                for c in range(4):
                    self.sum_chunk_front(fxB, fdt, ftok, fxdtd, c)
                    if self.front_limit in ("dt", "tok", "S"):
                        break
        self.tap("H_init", self.H[0:64, 0:NH * DS], self.vtok())

    def winx_pass(self, xnh, xB_fm, C_stage, dtb, wbuf, cstage, diag, tailbuf,
                  m_list, dt_too, front, h0, pre_src=None):
        nc, em, pb, prm = self.nc, self.em, self.pb, self.prm
        bias_off = 240 if front else 192
        em.wait("t", self.vtok())
        cdma_toks = [None, None]
        for mi, m in enumerate(m_list):
            wb = wbuf[:, (mi % 2) * 2048:(mi % 2) * 2048 + 2048]
            self.sguard("g", (id(wbuf), mi % 2))
            self.sguard("g", "fwb")
            wt = em.dma("g", wb.rearrange("p (f i) -> p f i", f=16),
                        apx(self.w_in, m * 128, [[NPROJ, 128], [128 * NPROJ, 16], [1, 128]]),
                        ("dwin", id(wbuf), mi % 2))
            em.wait("t", wt)
            bi = mi % 2
            self.use_bank(bi)
            if pre_src is not None:
                self.use_bank(4)
            last = None
            for f in range(16):
                lhs = wb[:, f * 128:(f + 1) * 128]
                em.t.ldweights(lhs)
                last = em.t.matmul(pb[bi][:], lhs, xnh[:, f * HALF:(f + 1) * HALF],
                                   start=(f == 0), stop=(f == 15)).inc("st")
                if pre_src is not None:
                    last = em.t.matmul(pb[4][:, 0:3], lhs, pre_src[:, f * 3:(f + 1) * 3],
                                       start=(f == 0), stop=(f == 15)).inc("st")
            self.sset((id(wbuf), mi % 2), last)
            ch0 = m * 128 - D_IN
            cmi = ch0 // 128
            cs = cstage[:, (mi % 2) * 516:(mi % 2) * 516 + 516]
            em.wait("v", last)
            self.sguard("v", (id(cstage), mi % 2))
            if pre_src is not None:
                tpre = em.v.tensor_copy(cs[:, 0:3], pb[4][:, 0:3]).inc("sv")
                self.bank_done(4, tpre)
            else:
                em.v.tensor_copy(cs[:, 0:3], tailbuf[:, cmi * 3:cmi * 3 + 3])
            tv = em.v.tensor_copy(cs[:, 3:515], pb[bi][:]).inc("sv")
            self.bank_done(bi, tv)
            self.vf()
            em.v.tensor_copy(tailbuf[:, cmi * 3:cmi * 3 + 3], cs[:, 512:515])
            self.vf()
            for k in range(4):
                em.v.tensor_scalar(diag[:, k * 128:(k + 1) * 128], self.idb[:],
                                   self.convp[:, cmi * 4 + k:cmi * 4 + k + 1], None, ALU.mult)
            tdg = self.vtok()
            self.vf()
            em.wait("t", tdg)
            bi2 = 2 + mi % 2
            self.use_bank(bi2)
            last = None
            for k in range(4):
                em.t.ldweights(diag[:, k * 128:(k + 1) * 128])
                last = em.t.matmul(pb[bi2][:], diag[:, k * 128:(k + 1) * 128],
                                   cs[:, k:k + 512], start=(k == 0), stop=(k == 3)).inc("st")
            self.sset((id(cstage), mi % 2), last)
            em.wait("s", last)
            bias_col = self.convp[:, bias_off + cmi:bias_off + cmi + 1]
            if ch0 < 2048:
                blk = ch0 // 128
                tv = em.s.activation(xB_fm[:, blk * HALF:(blk + 1) * HALF], pb[bi2][:],
                                     AF.Silu, bias=bias_col).inc("ss")
                self.bank_done(bi2, tv)
            else:
                b = (ch0 - 2048) // 128
                cst = C_stage[:, (mi % 2) * HALF:(mi % 2) * HALF + HALF]
                em.wait("s", cdma_toks[mi % 2])
                tv = em.s.activation(cst, pb[bi2][:], AF.Silu, bias=bias_col).inc("ss")
                self.bank_done(bi2, tv)
                em.wait("g", tv)
                em.dma("g", apx(self.c64_dram, (2 * b) * HALF, [[NH * HALF, 64], [1, 512]]),
                       cst[0:64, :], ("c64o", h0, mi % 2))
                cdma_toks[mi % 2] = em.dma("g", apx(self.c64_dram, (2 * b + 1) * HALF,
                                                    [[NH * HALF, 64], [1, 512]]),
                                           cst[64:128, :], ("c64o", h0, mi % 2))
                if b < 8:
                    pass
        if not front:
            tvv = self.stok()
            em.wait("g", tvv)
            em.dma("g", apx(self.b64_dram, 0, [[16 * HALF, 64], [2 * HALF, 8], [1, 512]]),
                   apx(xB_fm, 8 * HALF, [[16 * HALF, 64], [HALF, 8], [1, 512]]), ("b64o", h0))
            em.dma("g", apx(self.b64_dram, HALF, [[16 * HALF, 64], [2 * HALF, 8], [1, 512]]),
                   apx(xB_fm, 64 * (16 * HALF) + 8 * HALF, [[16 * HALF, 64], [HALF, 8], [1, 512]]),
                   ("b64o", h0))
        if dt_too:
            wb = wbuf[:, 0:16 * 64]
            self.sguard("g", (id(wbuf), 0))
            wt = em.dma("g", wb.rearrange("p (f i) -> p f i", f=16),
                        apx(self.w_in, 10240, [[NPROJ, 128], [128 * NPROJ, 16], [1, 64]]),
                        ("dwin", id(wbuf), 0))
            em.wait("t", wt)
            self.use_bank(4)
            last = None
            for f in range(16):
                lhs = wb[:, f * 64:(f + 1) * 64]
                em.t.ldweights(lhs)
                last = em.t.matmul(pb[4][0:64, 0:512], lhs, xnh[:, f * HALF:(f + 1) * HALF],
                                   start=(f == 0), stop=(f == 15)).inc("st")
            em.wait("v", last)
            ts = em.v.tensor_scalar(dtb[0:64, 0:512], pb[4][0:64, 0:512], prm[0:64, 240:241],
                                    None, ALU.add).inc("sv")
            self.bank_done(4, ts)
            em.wait("s", ts)
            em.s.activation(dtb[0:64, 512:1024], dtb[0:64, 0:512], AF.Abs)
            self.sf()
            em.s.activation(dtb[0:64, 1024:1536], dtb[0:64, 512:1024], AF.Exp, scale=-1.0)
            self.sf()
            em.s.activation(dtb[0:64, 1536:2048], dtb[0:64, 1024:1536], AF.Ln, bias=self.prm[0:64, 506:507])
            self.sf()
            ts = em.s.activation(dtb[0:64, 512:1024], dtb[0:64, 0:512], AF.Relu).inc("ss")
            em.wait("v", ts)
            tv = em.v.tensor_add(dtb[0:64, 2048:2560], dtb[0:64, 512:1024],
                                 dtb[0:64, 1536:2048]).inc("sv")
            self.vf()
            em.wait("s", tv)
            em.wait("t", tv)

    def dt_chunk(self, dtb, c, need_ec, need_toks):
        nc, em, pb, prm = self.nc, self.em, self.pb, self.prm
        lim = getattr(self, "front_limit", "all")
        dtp = dtb[0:64, 2048:2560]
        dA = dtb[0:64, 2560:2688]
        cum = dtb[0:64, 2688:2816]
        dte = dtb[0:64, 2816:2944]
        ec = dtb[0:64, 2944:3072]
        dtd = dtb[0:64, 3072:3200]
        em.v.tensor_scalar(dA, dtp[:, c * 128:(c + 1) * 128], prm[0:64, 242:243], None, ALU.mult)
        self.vf()
        em.v.tensor_tensor_scan(cum, self.onef[0:64, 0:128], dA, 0.0, ALU.mult, ALU.add)
        self.vf()
        em.v.tensor_copy(prm[0:64, 245:246], cum[:, 127:128])
        tv = self.vtok()
        self.vf()
        if lim == "dtA":
            return
        em.wait("s", tv)
        em.s.activation(dte, cum, AF.Exp, bias=prm[0:64, 245:246], scale=-1.0)
        em.s.activation(prm[0:64, 246:247], prm[0:64, 245:246], AF.Exp)
        if need_ec:
            em.s.activation(ec, cum, AF.Exp)
        ts = self.stok()
        self.sf()
        em.wait("v", ts)
        em.v.tensor_mul(dtd, dtp[:, c * 128:(c + 1) * 128], dte)
        tv = self.vtok()
        self.vf()
        if lim == "dtB":
            return
        em.wait("t", tv)
        self.use_bank(5)
        t3 = em.t.transpose(pb[5][:, 0:64], dtd, self.idf[0:64, 0:64]).inc("st")
        if need_toks:
            em.t.transpose(pb[5][:, 64:128], dtp[:, c * 128:(c + 1) * 128], self.idf[0:64, 0:64])
            t3 = em.t.transpose(pb[5][:, 128:192], cum, self.idf[0:64, 0:64]).inc("st")
        em.wait("v", t3)
        em.v.tensor_copy(dtb[:, 3200:3264], pb[5][:, 0:64])
        if need_toks:
            em.v.tensor_copy(dtb[:, 3264:3328], pb[5][:, 64:128])
            em.v.tensor_scalar_mul(dtb[:, 3328:3392], pb[5][:, 128:192], -1.0)
        lastent = None
        for entry in reversed(em.streams["v"]):
            if entry[0] not in ("wait_ge", "nop"):
                lastent = entry
                break
        tv = _Pending(em, lastent).inc("sv")
        em.wait("v", tv)
        self.bank_done(5, tv)
        self.use_bank(5)
        t2 = em.t.transpose(pb[5][0:1, 0:64], prm[0:64, 246:247], self.idf[0:64, 0:64]).inc("st")
        em.wait("s", t2)
        t3 = em.s.copy(prm[0:1, 352:416], pb[5][0:1, 0:64]).inc("ss")
        self.bank_done(5, t3)
        em.wait("t", t3)
        self.use_bank(5)
        t4 = em.t.matmul(pb[5][:, 0:64], self.onef[0:1, :], prm[0:1, 352:416],
                         start=True, stop=True).inc("st")
        em.wait("v", t4)
        tv = em.v.tensor_copy(prm[:, 160:224], pb[5][:, 0:64]).inc("sv")
        self.bank_done(5, tv)
        self.vf()

    def xB_transpose_chunk(self, xB_fm, x_dst, B_dst, c, banks):
        em, pb = self.em, self.pb
        for b8 in range(8):
            bi = banks[b8 % 2]
            self.use_bank(bi)
            em.t.transpose(pb[bi][:, 0:128],
                           xB_fm[:, b8 * HALF + c * 128:b8 * HALF + c * 128 + 128], self.idb[:])
            tb = em.t.transpose(pb[bi][:, 128:256],
                                xB_fm[:, (8 + b8) * HALF + c * 128:(8 + b8) * HALF + c * 128 + 128],
                                self.idb[:]).inc("st")
            em.wait("v", tb)
            em.v.tensor_copy(x_dst[:, b8 * 128:(b8 + 1) * 128], pb[bi][:, 0:128])
            tv = em.v.tensor_copy(B_dst[:, b8 * 128:(b8 + 1) * 128], pb[bi][:, 128:256]).inc("sv")
            self.bank_done(bi, tv)
        self.vf()

    def sum_chunk_front(self, fxB, fdt, ftok, fxdtd, c):
        nc, em, pb, prm = self.nc, self.em, self.pb, self.prm
        self.dt_chunk(fdt, c, need_ec=False, need_toks=False)
        if self.front_limit in ("dt", "dtA", "dtB", "dtC"):
            return
        x_tok = ftok[:, 0:T]
        B_tok = ftok[:, T:2 * T]
        em.wait("t", self.vtok())
        self.xB_transpose_chunk(fxB, x_tok, B_tok, c, (6, 7))
        x_ap = apx(x_tok, 0, [[2 * T, 128], [64, 16], [0, 4], [1, 64]])
        d_ap = apx(fdt, 3200, [[3456, 128], [4, 16], [1, 4], [0, 64]])
        tv = em.v.tensor_tensor(fxdtd[:].rearrange("p (g r q) -> p g r q", g=16, r=4),
                                x_ap, d_ap, ALU.mult).inc("sv")
        em.wait("t", tv)
        if self.front_limit == "tok":
            return
        for grp in range(8):
            bi = 2 + grp % 2
            self.use_bank(bi)
            last = None
            for hh in range(8):
                h = grp * 8 + hh
                xbh = h // 4
                lhs = B_tok[:, xbh * 64:(xbh + 1) * 64]
                em.t.ldweights(lhs)
                last = em.t.matmul(pb[bi][0:64, hh * 64:(hh + 1) * 64], lhs,
                                   fxdtd[:, h * 64:(h + 1) * 64], start=True, stop=True).inc("st")
            em.wait("v", last)
            self.h_update(grp, pb[bi])
            self.bank_done(bi, self.vtok())

    def h_update(self, grp, sbank):
        em = self.em
        Hsl = self.H[0:64, grp * 512:(grp + 1) * 512]
        Hscr = self.H[0:64, NH * DS:NH * DS + 512]
        dec_ap = apx(self.prm, 160 + grp * 8, [[512, 64], [1, 8], [0, 64]])
        em.v.tensor_tensor(Hscr.rearrange("p (a b) -> p a b", a=8),
                           Hsl.rearrange("p (a b) -> p a b", a=8), dec_ap, ALU.mult)
        self.vf()
        em.v.tensor_add(Hsl, Hscr, sbank[0:64, 0:512])
        self.vf()

    def l0_attn(self):
        nc, em = self.nc, self.em
        with nc.sbuf_tensor("nwb", [128, D_IN], BF16) as nwb, \
             nc.sbuf_tensor("a_xB", [128, 16 * HALF], BF16) as a_xB, \
             nc.sbuf_tensor("a_tail", [128, 144], BF16) as a_tail, \
             nc.sbuf_tensor("a_dt", [128, 3456], F32) as a_dt, \
             nc.sbuf_tensor("a_tok", [128, 2048], BF16) as a_tok, \
             nc.sbuf_tensor("a_wbuf", [128, 2 * 2048], BF16) as a_wbuf, \
             nc.sbuf_tensor("a_cst", [128, 2 * 516], BF16) as a_cst, \
             nc.sbuf_tensor("a_Cst", [128, 2 * HALF], BF16) as a_Cst, \
             nc.sbuf_tensor("a_diag", [128, 4 * 128], BF16) as a_diag, \
             nc.sbuf_tensor("a_scr", [128, 512], F32) as a_scr:
            self.nwb = nwb
            self.nwb_tok = em.dma("g", nwb[:], self.nw_bcast_d, "nwb")
            em.v.memset(a_tail[:], 0.0)
            self.vf()
            for hf in range(2):
                self.l0_half(hf, nwb, a_xB, a_tail, a_dt, a_tok, a_wbuf, a_cst,
                             a_Cst, a_diag, a_scr)
                em.barrier()
        self.tap("delta0", self.gdelta[:], self.vtok())

    def l0_half(self, hf, nwb, a_xB, a_tail, a_dt, a_tok, a_wbuf, a_cst, a_Cst,
                a_diag, a_scr):
        nc, em, pb, prm = self.nc, self.em, self.pb, self.prm
        h0 = hf * HALF
        with self.sb("xnh", [128, 16 * HALF], BF16) as xnh, \
             self.sb("zst", [128, HALF], BF16) as zst, \
             self.sb("ztr", [128, 2 * 512], BF16) as ztr:
            em.wait("q", self.xnout_tok)
            tl = em.dma("q", xnh[:].rearrange("p (f t) -> p f t", f=16),
                        apx(self.xn_dram, h0, [[T, 128], [128 * T, 16], [1, HALF]]), "xnh")
            em.wait("t", tl)
            zdma = [None, None]
            for m in range(32):
                wb = a_wbuf[:, (m % 2) * 2048:(m % 2) * 2048 + 2048]
                self.sguard("g", (id(a_wbuf), m % 2))
                wt = em.dma("g", wb.rearrange("p (f i) -> p f i", f=16),
                            apx(self.w_in, m * 128, [[NPROJ, 128], [128 * NPROJ, 16], [1, 128]]),
                            ("dwin", id(a_wbuf), m % 2))
                em.wait("t", wt)
                bi = m % 2
                self.use_bank(bi)
                last = None
                for f in range(16):
                    lhs = wb[:, f * 128:(f + 1) * 128]
                    em.t.ldweights(lhs)
                    last = em.t.matmul(pb[bi][:], lhs, xnh[:, f * HALF:(f + 1) * HALF],
                                       start=(f == 0), stop=(f == 15)).inc("st")
                self.sset((id(a_wbuf), m % 2), last)
                em.wait("s", last)
                self.sguard("s", "zst")
                tv = em.s.activation(zst[:], pb[bi][:], AF.Silu).inc("ss")
                self.bank_done(bi, tv)
                em.wait("t", tv)
                bi2 = 6 + m % 2
                self.use_bank(bi2)
                last = None
                for q in range(4):
                    last = em.t.transpose(pb[bi2][:, q * 128:(q + 1) * 128],
                                          zst[:, q * 128:(q + 1) * 128], self.idb[:]).inc("st")
                self.sset("zst", last)
                em.wait("v", last)
                em.wait("v", zdma[m % 2])
                tv = em.v.tensor_copy(ztr[:, (m % 2) * 512:(m % 2) * 512 + 512], pb[bi2][:, 0:512]).inc("sv")
                self.bank_done(bi2, tv)
                em.wait("g", tv)
                zdma[m % 2] = em.dma("g", apx(self.sz_dram, h0 * D_IN + m * 128,
                                              [[D_IN, 128], [128 * D_IN, 4], [1, 128]]),
                                     ztr[:, (m % 2) * 512:(m % 2) * 512 + 512].rearrange(
                                         "p (q i) -> p q i", q=4), ("szo", h0, m % 2))
            self.winx_pass(xnh, a_xB, a_Cst, a_dt, a_wbuf, a_cst, a_diag, a_tail,
                           m_list=list(range(32, 80)), dt_too=True, front=False, h0=h0,
                           pre_src=(self.xpre if hf == 0 else None))
        self.a_xB_ref = a_xB
        em.wait("t", self.stok())
        stageA_sh = (em.tok(("szo", h0, 0)), em.tok(("szo", h0, 1)),
                     em.tok(("c64o", h0, 0)), em.tok(("c64o", h0, 1)),
                     em.tok(("b64o", h0)))
        for _e in ("q", "g", "v", "s", "t"):
            for _tk in stageA_sh:
                em.wait(_e, _tk)
        with self.sb("szc", [128, D_IN], BF16) as szc, \
             self.sb("C64", [64, 2048], BF16) as C64, \
             self.sb("B64", [64, 16 * 128], BF16) as B64, \
             self.sb("Hbf", [64, NH * DS], BF16) as Hbf, \
             self.sb("xdt", [128, 512], BF16) as xdt, \
             self.sb("xdtd", [128, 512], BF16) as xdtd, \
             self.sb("Rq", [128, 2048], F32) as Rq, \
             self.sb("mexp", [128, 2048], BF16) as mexp, \
             self.sb("eCb", [64, 2048], BF16) as eCb, \
             self.sb("Ceq", [64, 2048], BF16) as Ceq, \
             self.sb("GM", [128, 1024], BF16) as GM, \
             self.sb("yz", [128, 1024], F32) as yz, \
             self.sb("sqs", [128, 1024], F32) as sqs, \
             self.sb("ntok", [128, 1024], BF16) as ntok, \
             self.sb("nfm", [128, 32 * 256], BF16) as nfm:
            for c in range(4):
                self.l0_chunk(hf, c, stageA_sh, a_dt, a_tok, szc, C64, B64, Hbf,
                              xdt, xdtd, Rq, mexp, eCb, Ceq, GM, yz, sqs, ntok, nfm)
                if c % 2 == 1:
                    self.l0_wout(hf, c // 2, nfm, a_scr)

    def l0_chunk(self, hf, c, stageA_sh, a_dt, a_tok, szc, C64, B64, Hbf,
                 xdt, xdtd, Rq, mexp, eCb, Ceq, GM, yz, sqs, ntok, nfm):
        nc, em, pb, prm = self.nc, self.em, self.pb, self.prm
        h0 = hf * HALF
        em.wait("g", em.tok("rq") if "rq" in em.sems else None)
        em.wait("g", em.tok("ecb") if "ecb" in em.sems else None)
        self.dt_chunk(a_dt, c, need_ec=True, need_toks=True)
        tv = self.vtok()
        ts = self.stok()
        em.wait("g", ts)
        em.dma("g", apx(self.cum_dram, 0, [[128, 64], [1, 128]]), a_dt[0:64, 2688:2816], "cumo")
        em.dma("g", apx(self.ec_dram, 0, [[128, 64], [1, 128]]), a_dt[0:64, 2944:3072], "eco")
        cumec_tok = (em.tok("cumo"), em.tok("eco"))
        for tk in stageA_sh:
            em.wait("q", tk)
        em.wait("q", self.vtok())
        em.wait("q", self.ttok())
        tsz = em.dma("q", szc[:], apx(self.sz_dram, (h0 + c * 128) * D_IN,
                                      [[D_IN, 128], [1, D_IN]]), "szc")
        tb64 = em.dma("q", B64[:], apx(self.b64_dram, c * 128,
                                       [[16 * HALF, 64], [HALF, 16], [1, 128]]), "b64l")
        em.wait("v", self.ttok())
        em.v.tensor_copy(Hbf[:], self.H[0:64, 0:NH * DS])
        self.vf()
        em.wait("t", self.vtok())
        self.xB_transpose_chunk(self.a_xB_ref, a_tok[:, 0:T], a_tok[:, T:2 * T], c, (6, 7))
        em.wait("t", tsz)
        em.wait("t", tb64)
        for qq in range(4):
            em.wait("q", self.ttok())
            em.wait("q", self.vtok())
            tc64 = em.dma("q", C64[:], apx(self.c64_dram, qq * 16 * HALF + c * 128,
                                           [[NH * HALF, 64], [HALF, 16], [1, 128]]), "c64l")
            em.wait("t", tc64)
            em.wait("v", tc64)
            em.wait("q", self.stok())
            em.wait("q", cumec_tok[0])
            em.wait("q", cumec_tok[1])
            em.wait("g", cumec_tok[1])
            t1 = em.dma("q", Rq[:].rearrange("p (h i) -> p h i", h=16),
                        apx(self.cum_dram, qq * 16 * 128, [[0, 128], [128, 16], [1, 128]]), "rq")
            em.wait("g", t1)
            t2 = em.dma("g", Rq[:].rearrange("p (h i) -> p h i", h=16),
                        apx(self.triu_neg_d, 0, [[128, 128], [0, 16], [1, 128]]), "rqa",
                        accum_op=ALU.add)
            em.wait("v", t2)
            em.wait("v", self.stok())
            ncol_ap = apx(a_dt, 3328 + qq * 16, [[3456, 128], [1, 16], [0, 128]])
            tv = em.v.tensor_tensor(Rq[:].rearrange("p (h i) -> p h i", h=16),
                                    Rq[:].rearrange("p (h i) -> p h i", h=16),
                                    ncol_ap, ALU.add).inc("sv")
            em.wait("s", tv)
            em.wait("s", self.vtok())
            em.s.activation(mexp[:, 0:1024], Rq[:, 0:1024], AF.Exp)
            ts = em.s.activation(mexp[:, 1024:2048], Rq[:, 1024:2048], AF.Exp).inc("ss")
            t3 = em.dma("g", eCb[:].rearrange("p (h i) -> p h i", h=16),
                        apx(self.ec_dram, qq * 16 * 128, [[0, 64], [128, 16], [1, 128]]), "ecb")
            em.wait("v", t3)
            em.wait("v", self.ttok())
            tce = em.v.tensor_mul(Ceq[:], C64[:], eCb[:]).inc("sv")
            em.wait("t", tce)
            em.wait("v", ts)
            for g2 in range(2):
                grp = qq * 2 + g2
                ybank = 0 + grp % 2
                sbank = 4 + grp % 2
                em.wait("v", self.ttok())
                x_ap = apx(a_tok, 2 * grp * 64, [[2048, 128], [64, 2], [0, 4], [1, 64]])
                dtp_ap = apx(a_dt, 3264 + grp * 8, [[3456, 128], [4, 2], [1, 4], [0, 64]])
                dtd_ap = apx(a_dt, 3200 + grp * 8, [[3456, 128], [4, 2], [1, 4], [0, 64]])
                em.v.tensor_tensor(xdt[:].rearrange("p (g r q) -> p g r q", g=2, r=4),
                                   x_ap, dtp_ap, ALU.mult)
                tv = em.v.tensor_tensor(xdtd[:].rearrange("p (g r q) -> p g r q", g=2, r=4),
                                        x_ap, dtd_ap, ALU.mult).inc("sv")
                em.wait("t", tv)
                self.use_bank(ybank)
                self.use_bank(sbank)
                ylast = None
                for half4 in range(2):
                    gtbank = 2 + half4
                    self.use_bank(gtbank)
                    gtlast = None
                    for k in range(4):
                        h = grp * 8 + half4 * 4 + k
                        xbh = h // 4
                        lhs = B64[:, xbh * 128:(xbh + 1) * 128]
                        em.t.ldweights(lhs)
                        lh = h - qq * 16
                        gtlast = em.t.matmul(pb[gtbank][:, k * 128:(k + 1) * 128], lhs,
                                             C64[:, lh * 128:(lh + 1) * 128],
                                             start=True, stop=True).inc("st")
                    em.wait("v", gtlast)
                    self.sguard("v", ("GM", half4))
                    msl = mexp[:, (g2 * 8 + half4 * 4) * 64 * 2:(g2 * 8 + half4 * 4 + 4) * 64 * 2]
                    tv = em.v.tensor_mul(GM[:, half4 * 512:(half4 + 1) * 512],
                                         pb[gtbank][:], msl).inc("sv")
                    self.bank_done(gtbank, tv)
                    em.wait("t", tv)
                    for k in range(4):
                        h = grp * 8 + half4 * 4 + k
                        hq = g2 * 8 + half4 * 4 + k
                        lhs1 = GM[:, half4 * 512 + k * 128:half4 * 512 + (k + 1) * 128]
                        em.t.ldweights(lhs1)
                        em.t.matmul(pb[ybank][:, (half4 * 4 + k) * 64:(half4 * 4 + k + 1) * 64],
                                    lhs1, xdt[:, (half4 * 4 + k) * 64:(half4 * 4 + k + 1) * 64],
                                    start=True, stop=False)
                        lhs2 = Ceq[:, hq * 128:(hq + 1) * 128]
                        em.t.ldweights(lhs2)
                        ylast = em.t.matmul(pb[ybank][:, (half4 * 4 + k) * 64:(half4 * 4 + k + 1) * 64],
                                            lhs2, Hbf[:, h * 64:(h + 1) * 64],
                                            start=False, stop=True).inc("st")
                    self.sset(("GM", half4), ylast)
                slast = None
                for hh in range(8):
                    h = grp * 8 + hh
                    xbh = h // 4
                    lhs = a_tok[:, T + xbh * 64:T + (xbh + 1) * 64]
                    em.t.ldweights(lhs)
                    slast = em.t.matmul(pb[sbank][0:64, hh * 64:(hh + 1) * 64], lhs,
                                        xdtd[:, hh * 64:(hh + 1) * 64], start=True, stop=True).inc("st")
                em.wait("v", ylast)
                xg_ap = apx(a_tok, 2 * grp * 64, [[2048, 128], [64, 2], [0, 4], [1, 64]])
                D_ap = apx(prm, 96 + grp * 8, [[512, 128], [4, 2], [1, 4], [0, 64]])
                xsc = sqs[:, (grp % 2) * 512:(grp % 2) * 512 + 512]
                em.v.tensor_tensor(xsc.rearrange("p (g r q) -> p g r q", g=2, r=4),
                                   xg_ap, D_ap, ALU.mult)
                self.vf()
                em.v.tensor_add(xsc, xsc, pb[ybank][:])
                tyv = self.vtok()
                self.bank_done(ybank, tyv)
                self.vf()
                yhalf = yz[:, (grp % 2) * 512:(grp % 2) * 512 + 512]
                em.v.tensor_mul(yhalf, xsc, szc[:, grp * 512:(grp + 1) * 512])
                self.vf()
                em.wait("v", slast)
                self.h_update(grp, pb[sbank])
                self.bank_done(sbank, self.vtok())
                if grp % 2 == 1:
                    self.l0_rms(hf, c, grp // 2, yz, sqs, ntok, nfm)

    def l0_rms(self, hf, c, qq, yz, sqs, ntok, nfm):
        nc, em, pb, prm = self.nc, self.em, self.pb, self.prm
        em.wait("s", self.vtok())
        em.wait("v", self.nwb_tok)
        ts = em.s.activation(sqs[:], yz[:], AF.Square).inc("ss")
        em.wait("v", ts)
        rms = prm[:, 248:264]
        tv = em.v.tensor_reduce(rms.rearrange("p (h o) -> p h o", h=16),
                                sqs[:].rearrange("p (h q) -> p h q", h=16),
                                AX.X, ALU.add).inc("sv")
        em.wait("s", tv)
        em.s.activation(prm[:, 280:296], rms, AF.Ln, bias=self.prm[:, 504:505], scale=1.0 / DS)
        self.sf()
        ts = em.s.activation(rms, prm[:, 280:296], AF.Exp, scale=-0.5).inc("ss")
        em.wait("v", ts)
        rms_ap = apx(prm, 248, [[512, 128], [1, 16], [0, 64]])
        em.v.tensor_tensor(sqs[:].rearrange("p (h q) -> p h q", h=16),
                           yz[:].rearrange("p (h q) -> p h q", h=16), rms_ap, ALU.mult)
        self.vf()
        self.sguard("v", "ntok")
        tv = em.v.tensor_mul(ntok[:], sqs[:], self.nwb[:, qq * 1024:(qq + 1) * 1024]).inc("sv")
        em.wait("t", tv)
        for fq in range(2):
            bi = 6 + fq % 2
            self.use_bank(bi)
            last = None
            for k in range(4):
                fb = fq * 4 + k
                last = em.t.transpose(pb[bi][:, k * 128:(k + 1) * 128],
                                      ntok[:, fb * 128:fb * 128 + 128], self.idb[:]).inc("st")
            em.wait("v", last)
            out_ap = apx(nfm, (qq * 8 + fq * 4) * 256 + (c % 2) * 128,
                         [[32 * 256, 128], [256, 4], [1, 128]])
            tv = em.v.tensor_copy(out_ap, pb[bi][:, 0:512].rearrange("p (a b) -> p a b", a=4)).inc("sv")
            self.bank_done(bi, tv)
            self.sset("ntok", self.ttok())

    def l0_wout(self, hf, pair, nfm, a_scr):
        nc, em, pb = self.nc, self.em, self.pb
        em.wait("t", self.vtok())
        tok0 = hf * HALF + pair * 256
        with self.sb("wob", [128, 2 * 2048], BF16) as wob:
            for o in range(16):
                bi = 4 + o % 2
                self.use_bank(bi)
                last = None
                for istrip in range(2):
                    wb = wob[:, istrip * 2048:(istrip + 1) * 2048]
                    self.sguard("g", (id(wob), istrip))
                    wt = em.dma("g", wb.rearrange("p (f i) -> p f i", f=16),
                                apx(self.w_out, (istrip * 16 * 128) * D + o * 128,
                                    [[D, 128], [128 * D, 16], [1, 128]]), ("dwob", istrip))
                    em.wait("t", wt)
                    for f in range(16):
                        ii = istrip * 16 + f
                        lhs = wb[:, f * 128:(f + 1) * 128]
                        em.t.ldweights(lhs)
                        last = em.t.matmul(pb[bi][:, 0:256], lhs, nfm[:, ii * 256:(ii + 1) * 256],
                                           start=(ii == 0), stop=(ii == 31)).inc("st")
                    self.sset((id(wob), istrip), last)
                em.wait("v", last)
                gofs = o * T + tok0
                gsl = self.gdelta[:, gofs:gofs + 256]
                em.v.tensor_mul(a_scr[:, 0:256], pb[bi][:, 0:256], gsl)
                tvv = self.vtok()
                self.bank_done(bi, tvv)
                self.vf()
                em.v.tensor_copy(gsl, a_scr[:, 0:256])
                self.vf()

    def l1(self):
        nc, em, pb, prm = self.nc, self.em, self.pb, self.prm
        for f in range(16):
            em.dma("q", prm[:, 224 + f:225 + f], self.l1_D[f * 128:(f + 1) * 128, None], "l1d")
        self.l1d_tok = em.tok("l1d")
        with nc.sbuf_tensor("x1n", [128, 16 * T], BF16) as x1n:
            with nc.sbuf_tensor("x1f", [128, 16 * T], F32) as x1f, \
                 nc.sbuf_tensor("l1scr", [128, 2 * D], F32) as l1scr:
                for c in range(8):
                    buf = l1scr[:, (c % 2) * D:(c % 2) * D + D]
                    self.sguard("q", ("l1x", c % 2))
                    tl = em.dma("q", buf, self.x_in[c * 128:(c + 1) * 128, :], ("dl1x", c % 2))
                    em.wait("t", tl)
                    for fq in range(4):
                        bi = fq % 2
                        self.use_bank(bi)
                        last = None
                        for k in range(4):
                            f = fq * 4 + k
                            last = em.t.transpose(pb[bi][:, k * 128:(k + 1) * 128],
                                                  buf[:, f * 128:f * 128 + 128], self.idf[:]).inc("st")
                        em.wait("v", last)
                        out_ap = self.fm_out_ap(x1f, fq, c * 128, T)
                        g_ap = self.fm_out_ap(self.gdelta, fq, c * 128, T)
                        tv = em.v.tensor_add(out_ap, pb[bi][:].rearrange("p (a b) -> p a b", a=4),
                                             g_ap).inc("sv")
                        self.bank_done(bi, tv)
                        if fq == 3:
                            self.sset(("l1x", c % 2), tv)
                em.wait("t", self.vtok())
                em.wait("s", self.vtok())
                for tt_i in range(2):
                    self.use_bank(2)
                    last = None
                    for f in range(16):
                        last = em.t.matmul(pb[2][0:1, 0:512], self.onef[:, 0:1],
                                           x1f[:, f * T + tt_i * 512:f * T + tt_i * 512 + 512],
                                           start=(f == 0), stop=(f == 15)).inc("st")
                    em.wait("s", last)
                    ts = em.s.mul(l1scr[0:1, tt_i * 512:tt_i * 512 + 512],
                                  pb[2][0:1, 0:512], 1.0 / D).inc("ss")
                    self.bank_done(2, ts)
                for tt_i in range(2):
                    self.use_bank(3)
                    last = None
                    for f in range(16):
                        tq = em.s.activation(l1scr[:, D:D + 512],
                                             x1f[:, f * T + tt_i * 512:f * T + tt_i * 512 + 512],
                                             AF.Square).inc("ss")
                        em.wait("t", tq)
                        last = em.t.matmul(pb[3][0:1, 0:512], self.onef[:, 0:1],
                                           l1scr[:, D:D + 512], start=(f == 0),
                                           stop=(f == 15)).inc("st")
                        em.wait("s", last)
                    ts = em.s.mul(l1scr[0:1, 1024 + tt_i * 512:1024 + tt_i * 512 + 512],
                                  pb[3][0:1, 0:512], 1.0 / D).inc("ss")
                    self.bank_done(3, ts)
                em.wait("v", self.stok())
                em.v.tensor_mul(l1scr[0:1, D:D + T], l1scr[0:1, 0:T], l1scr[0:1, 0:T])
                self.vf()
                em.v.tensor_sub(l1scr[0:1, 1024:1024 + T], l1scr[0:1, 1024:1024 + T],
                                l1scr[0:1, D:D + T])
                tv = self.vtok()
                self.vf()
                em.wait("s", tv)
                em.s.activation(l1scr[0:1, D:D + T], l1scr[0:1, 1024:1024 + T], AF.Ln, bias=self.prm[0:1, 504:505])
                self.sf()
                ts = em.s.activation(l1scr[0:1, 1024:1024 + T], l1scr[0:1, D:D + T],
                                     AF.Exp, scale=-0.5).inc("ss")
                em.wait("t", ts)
                tmm = None
                for half in range(2):
                    self.use_bank(2 + half)
                    em.t.matmul(pb[2 + half][:, 0:512], self.onef[0:1, :],
                                l1scr[0:1, half * 512:half * 512 + 512],
                                start=True, stop=True).inc("st")
                    self.use_bank(4 + half)
                    tmm = em.t.matmul(pb[4 + half][:, 0:512], self.onef[0:1, :],
                                      l1scr[0:1, 1024 + half * 512:1024 + half * 512 + 512],
                                      start=True, stop=True).inc("st")
                em.wait("v", tmm)
                for half in range(2):
                    em.v.tensor_copy(l1scr[:, D + half * 512:D + half * 512 + 512],
                                     pb[2 + half][:, 0:512])
                    em.v.tensor_copy(l1scr[:, D + T + half * 512:D + T + half * 512 + 512],
                                     pb[4 + half][:, 0:512])
                lastent = None
                for entry in reversed(em.streams["v"]):
                    if entry[0] not in ("wait_ge", "nop"):
                        lastent = entry
                        break
                tv = _Pending(em, lastent).inc("sv")
                em.wait("v", tv)
                for bi in (2, 3, 4, 5):
                    self.bank_done(bi, tv)
                meanb = l1scr[:, D:D + T]
                istdb = l1scr[:, D + T:D + 2 * T]
                for f in range(16):
                    em.v.tensor_sub(x1f[:, f * T:(f + 1) * T], x1f[:, f * T:(f + 1) * T], meanb)
                    self.vf()
                    em.v.tensor_mul(x1n[:, f * T:(f + 1) * T], x1f[:, f * T:(f + 1) * T], istdb)
                    self.vf()
                self.tap("x1n", x1n[:], self.vtok())
            with nc.sbuf_tensor("cwfm", [32, T], BF16) as cwfm, \
                 nc.sbuf_tensor("cwsfm", [16, T], BF16) as cwsfm:
                self.routing(x1n, cwfm, cwsfm)
                em.barrier()
                with nc.sbuf_tensor("moe", [128, 16 * T], BF16) as moe:
                    with nc.sbuf_tensor("oh32sb", [32, 32 * 128], BF16) as oh32, \
                         nc.sbuf_tensor("oh16sb", [16, 16 * 128], BF16) as oh16, \
                         nc.sbuf_tensor("a_e", [128, 6 * T], BF16) as a_e, \
                         nc.sbuf_tensor("a_s", [128, T], BF16) as a_s, \
                         nc.sbuf_tensor("wgu", [128, 4 * 4096], BF16) as wgu, \
                         nc.sbuf_tensor("wdb", [128, 2 * 3072], BF16) as wdb, \
                         nc.sbuf_tensor("wsm", [128, 2048 + 2048], BF16) as wsm, \
                         nc.sbuf_tensor("cwb", [128, T], BF16) as cwb, \
                         nc.sbuf_tensor("cwsb", [128, T], BF16) as cwsb, \
                         nc.sbuf_tensor("mscr", [128, 2 * 512], BF16) as mscr:
                        em.dma("q", oh32[:], self.oh32_d, "oh32")
                        tl = em.dma("q", oh16[:], self.oh16s_d, "oh32")
                        em.wait("t", tl)
                        for e in range(NEXP):
                            self.moe_expert(e, x1n, cwfm, cwsfm, moe, a_e, a_s, wgu,
                                            wdb, wsm, cwb, cwsb, mscr, oh32, oh16)
                        self.tap("moe", moe[:], self.vtok())
                        em.wait("s", self.vtok())
                        for o in range(16):
                            for tt_i in range(2):
                                sl = moe[:, o * T + tt_i * 512:o * T + tt_i * 512 + 512]
                                em.s.activation(mscr[:, 0:512], sl, AF.Silu)
                                self.sf()
                                em.s.copy(sl, mscr[:, 0:512])
                                self.sf()
                        ts = self.stok()
                        em.wait("t", ts)
                        em.wait("v", ts)
                    self.l1_attn(x1n, moe)

    def routing(self, x1n, cwfm, cwsfm):
        nc, em, pb, prm = self.nc, self.em, self.pb, self.prm
        with nc.sbuf_tensor("gw", [32, D], BF16) as gw, \
             nc.sbuf_tensor("gwT", [128, 16 * 32], BF16) as gwT, \
             nc.sbuf_tensor("lgt", [32, T], F32) as lgt, \
             nc.sbuf_tensor("rs", [128, 8 * 32], F32) as rs, \
             nc.sbuf_tensor("rw", [128, 14 * 32], F32) as rw:
            em.wait("g", self.vtok())
            em.wait("g", self.stok())
            tl = em.dma("g", gw[:], self.gate_w, "gw")
            em.wait("t", tl)
            for f in range(16):
                bi = 6 + f % 2
                self.use_bank(bi)
                tt = em.t.transpose(pb[bi][:, 0:32], gw[:, f * 128:(f + 1) * 128],
                                    self.idb[0:32, 0:32]).inc("st")
                em.wait("v", tt)
                tv = em.v.tensor_copy(gwT[:, f * 32:(f + 1) * 32], pb[bi][:, 0:32]).inc("sv")
                self.bank_done(bi, tv)
            em.wait("t", self.vtok())
            for tt_i in range(2):
                self.use_bank(2)
                last = None
                for f in range(16):
                    lhs = gwT[:, f * 32:(f + 1) * 32]
                    em.t.ldweights(lhs)
                    last = em.t.matmul(pb[2][0:32, 0:512], lhs,
                                       x1n[:, f * T + tt_i * 512:f * T + tt_i * 512 + 512],
                                       start=(f == 0), stop=(f == 15)).inc("st")
                em.wait("v", last)
                tv = em.v.tensor_copy(lgt[:, tt_i * 512:tt_i * 512 + 512], pb[2][0:32, 0:512]).inc("sv")
                self.bank_done(2, tv)
            em.wait("t", self.vtok())
            for c in range(8):
                self.use_bank(3)
                tt = em.t.transpose(pb[3][:, 0:32], lgt[:, c * 128:(c + 1) * 128],
                                    self.idf[0:32, 0:32]).inc("st")
                em.wait("v", tt)
                self.sguard("v", "rwk")
                lt = rs[:, 0:32]
                cur = rs[:, 32:64]
                msk = rs[:, 64:96]
                ex = rs[:, 96:128]
                cwv = rs[:, 128:160]
                mx = rs[:, 160:165]
                tv = em.v.tensor_copy(lt, pb[3][:, 0:32]).inc("sv")
                self.bank_done(3, tv)
                self.vf()
                em.v.tensor_copy(cur, lt)
                self.vf()
                for j in range(4):
                    em.v.reduce_max(mx[:, j:j + 1], cur, axis=AX.X)
                    self.vf()
                    if j < 3:
                        em.v.tensor_scalar(msk, cur, mx[:, j:j + 1], None, ALU.is_ge)
                        self.vf()
                        em.v.scalar_tensor_tensor(cur, msk, -1e30, cur, ALU.mult, ALU.add)
                        self.vf()
                em.v.tensor_scalar_mul(mx[:, 4:5], mx[:, 0:1], -1.0)
                tv = self.vtok()
                self.vf()
                em.wait("s", tv)
                ts = em.s.activation(ex, lt, AF.Exp, bias=mx[:, 4:5]).inc("ss")
                em.wait("v", ts)
                em.v.tensor_scalar(msk, lt, mx[:, 3:4], None, ALU.is_ge)
                self.vf()
                em.v.tensor_mul(cwv, ex, msk)
                self.vf()
                em.v.reduce_sum(mx[:, 0:1], cwv, axis=AX.X)
                self.vf()
                em.v.reciprocal(mx[:, 1:2], mx[:, 0:1])
                self.vf()
                em.v.tensor_scalar(rw[:, 0:32], cwv, mx[:, 1:2], None, ALU.mult)
                self.vf()
                em.v.tensor_tensor(rw[:, 32:48], apx(rw, 0, [[14 * 32, 128], [2, 16]]),
                                   apx(rw, 1, [[14 * 32, 128], [2, 16]]), ALU.add)
                tv = self.vtok()
                self.vf()
                em.wait("t", tv)
                self.use_bank(3)
                ta = em.t.transpose(pb[3][0:32, 0:128], rw[:, 0:32], self.idf[:]).inc("st")
                em.wait("v", ta)
                tv = em.v.tensor_copy(cwfm[:, c * 128:(c + 1) * 128], pb[3][0:32, 0:128]).inc("sv")
                self.bank_done(3, tv)
                self.use_bank(3)
                ta = em.t.transpose(pb[3][0:16, 0:128], rw[:, 32:48], self.idf[:]).inc("st")
                em.wait("v", ta)
                tv = em.v.tensor_copy(cwsfm[:, c * 128:(c + 1) * 128], pb[3][0:16, 0:128]).inc("sv")
                self.bank_done(3, tv)
                self.sset("rwk", self.ttok())
            self.tap("cw", cwfm[:], self.vtok())

    def moe_expert(self, e, x1n, cwfm, cwsfm, moe, a_e, a_s, wgu, wdb, wsm, cwb,
                   cwsb, mscr, oh32, oh16):
        nc, em, pb = self.nc, self.em, self.pb
        small = (e % 2 == 0)
        k = e // 2
        em.wait("v", self.ttok())
        em.wait("t", self.vtok())
        for tt_i in range(2):
            self.use_bank(tt_i)
            lhs = oh32[:, e * 128:(e + 1) * 128]
            em.t.ldweights(lhs)
            tm = em.t.matmul(pb[tt_i][:], lhs, cwfm[:, tt_i * 512:tt_i * 512 + 512],
                             start=True, stop=True).inc("st")
            em.wait("v", tm)
            tv = em.v.tensor_copy(cwb[:, tt_i * 512:tt_i * 512 + 512], pb[tt_i][:]).inc("sv")
            self.bank_done(tt_i, tv)
            if small:
                self.use_bank(2 + tt_i)
                lhs = oh16[:, k * 128:(k + 1) * 128]
                em.t.ldweights(lhs)
                tm = em.t.matmul(pb[2 + tt_i][:], lhs, cwsfm[:, tt_i * 512:tt_i * 512 + 512],
                                 start=True, stop=True).inc("st")
                em.wait("v", tm)
                tv = em.v.tensor_copy(cwsb[:, tt_i * 512:tt_i * 512 + 512], pb[2 + tt_i][:]).inc("sv")
                self.bank_done(2 + tt_i, tv)
        em.wait("t", self.vtok())
        for third in range(3):
            for proj in range(2):
                sl4 = ((e * 3 + third) % 2) * 2 + proj
                wsrc = self.moe_wg if proj == 0 else self.moe_wu
                wb = wgu[:, sl4 * 4096:sl4 * 4096 + 4096]
                self.sguard("g", (id(wgu), sl4))
                em.dma("g", wb.rearrange("p (f i) -> p f i", f=16),
                       apx(wsrc, e * D * MOE_I + third * 256,
                           [[MOE_I, 128], [128 * MOE_I, 16], [1, 256]]), ("dwgu", sl4))
            wt = (em.tok(("dwgu", ((e * 3 + third) % 2) * 2)), em.tok(("dwgu", ((e * 3 + third) % 2) * 2 + 1)))
            em.wait("t", wt[0])
            em.wait("t", wt[1])
            for ic2 in range(2):
                ii = third * 2 + ic2
                for proj in range(2):
                    sl4 = ((e * 3 + third) % 2) * 2 + proj
                    wb = wgu[:, sl4 * 4096:sl4 * 4096 + 4096]
                    for tt_i in range(2):
                        bi = proj * 2 + tt_i
                        self.use_bank(bi)
                        last = None
                        for f in range(16):
                            lhs = wb[:, f * 256 + ic2 * 128:f * 256 + ic2 * 128 + 128]
                            em.t.ldweights(lhs)
                            last = em.t.matmul(pb[bi][:], lhs,
                                               x1n[:, f * T + tt_i * 512:f * T + tt_i * 512 + 512],
                                               start=(f == 0), stop=(f == 15)).inc("st")
                        self.sset((id(wgu), sl4), last)
                for tt_i in range(2):
                    bg = 0 + tt_i
                    bu = 2 + tt_i
                    tmlast = self.ttok()
                    em.wait("s", tmlast)
                    self.sguard("s", (id(mscr), tt_i % 2))
                    ts = em.s.activation(mscr[:, (tt_i % 2) * 512:(tt_i % 2) * 512 + 512],
                                         pb[bg][:], AF.Silu).inc("ss")
                    em.wait("v", ts)
                    em.v.tensor_mul(mscr[:, (tt_i % 2) * 512:(tt_i % 2) * 512 + 512],
                                    mscr[:, (tt_i % 2) * 512:(tt_i % 2) * 512 + 512], pb[bu][:])
                    self.vf()
                    tv = em.v.tensor_mul(a_e[:, ii * T + tt_i * 512:ii * T + tt_i * 512 + 512],
                                         mscr[:, (tt_i % 2) * 512:(tt_i % 2) * 512 + 512],
                                         cwb[:, tt_i * 512:tt_i * 512 + 512]).inc("sv")
                    self.sset((id(mscr), tt_i % 2), tv)
                    self.bank_done(bg, tv)
                    self.bank_done(bu, tv)
        if small:
            for proj in range(2):
                wsrc = self.cwg if proj == 0 else self.cwu
                wb = wsm[:, 0:2048]
                self.sguard("g", (id(wsm), 0))
                wt = em.dma("g", wb.rearrange("p (f i) -> p f i", f=16),
                            apx(wsrc, k * D * CHI, [[CHI, 128], [128 * CHI, 16], [1, 128]]), "dw")
                em.wait("t", wt)
                for tt_i in range(2):
                    bi = proj * 2 + tt_i
                    self.use_bank(bi)
                    last = None
                    for f in range(16):
                        lhs = wb[:, f * 128:(f + 1) * 128]
                        em.t.ldweights(lhs)
                        last = em.t.matmul(pb[bi][:], lhs,
                                           x1n[:, f * T + tt_i * 512:f * T + tt_i * 512 + 512],
                                           start=(f == 0), stop=(f == 15)).inc("st")
                    self.sset((id(wsm), 0), last)
                    if proj == 1:
                        em.wait("s", last)
                        ts = em.s.activation(mscr[:, tt_i * 512:tt_i * 512 + 512],
                                             pb[0 + tt_i][:], AF.Silu).inc("ss")
                        em.wait("v", ts)
                        em.v.tensor_mul(mscr[:, tt_i * 512:tt_i * 512 + 512],
                                        mscr[:, tt_i * 512:tt_i * 512 + 512], pb[2 + tt_i][:])
                        self.vf()
                        tv = em.v.tensor_mul(a_s[:, tt_i * 512:tt_i * 512 + 512],
                                             mscr[:, tt_i * 512:tt_i * 512 + 512],
                                             cwsb[:, tt_i * 512:tt_i * 512 + 512]).inc("sv")
                        self.bank_done(0 + tt_i, tv)
                        self.bank_done(2 + tt_i, tv)
            self.sguard("g", (id(wsm), 1))
            wt = em.dma("g", wsm[:, 2048:4096].rearrange("p (o j) -> p o j", o=16),
                        apx(self.cwd, k * CHI * D, [[D, 128], [128, 16], [1, 128]]), "wsm2")
            self.small_wd_tok = wt
        em.wait("t", self.vtok())
        for oq in range(4):
            wb = wdb[:, (oq % 2) * 3072:(oq % 2) * 3072 + 3072]
            self.sguard("g", (id(wdb), oq % 2))
            wt = em.dma("g", wb.rearrange("p (i o) -> p i o", i=6),
                        apx(self.moe_wd, e * MOE_I * D + oq * 512,
                            [[D, 128], [128 * D, 6], [1, 512]]), ("dwd", oq % 2))
            em.wait("t", wt)
            if small:
                em.wait("t", self.small_wd_tok)
            for o4 in range(4):
                o = oq * 4 + o4
                for tt_i in range(2):
                    bi = 4 + (o * 2 + tt_i) % 2
                    self.use_bank(bi)
                    last = None
                    for ii in range(6):
                        lhs = wb[:, ii * 512 + o4 * 128:ii * 512 + o4 * 128 + 128]
                        em.t.ldweights(lhs)
                        last = em.t.matmul(pb[bi][:], lhs,
                                           a_e[:, ii * T + tt_i * 512:ii * T + tt_i * 512 + 512],
                                           start=(ii == 0), stop=(not small and ii == 5)).inc("st")
                    if small:
                        lhs = wsm[:, 2048 + o * 128:2048 + (o + 1) * 128]
                        em.t.ldweights(lhs)
                        last = em.t.matmul(pb[bi][:], lhs,
                                           a_s[:, tt_i * 512:tt_i * 512 + 512],
                                           start=False, stop=True).inc("st")
                        self.sset((id(wsm), 1), last)
                    em.wait("v", last)
                    self.sset((id(wdb), oq % 2), last)
                    msl = moe[:, o * T + tt_i * 512:o * T + tt_i * 512 + 512]
                    if e == 0:
                        tv = em.v.tensor_copy(msl, pb[bi][:]).inc("sv")
                    else:
                        tv = em.v.tensor_add(msl, msl, pb[bi][:]).inc("sv")
                    self.bank_done(bi, tv)
                    self.vf()

    def l1_attn(self, x1n, g1):
        nc, em, pb, prm = self.nc, self.em, self.pb, self.prm
        with nc.sbuf_tensor("state", [128, 8 * T], BF16) as state, \
             nc.sbuf_tensor("bnat", [128, 2 * D], BF16) as bnat, \
             nc.sbuf_tensor("btn", [128, 2 * 2048], BF16) as btn, \
             nc.sbuf_tensor("lascr", [128, 3 * 512], F32) as lascr, \
             nc.sbuf_tensor("lbscr", [128, 512], BF16) as lbscr:
            em.wait("g", self.ttok())
            em.wait("g", self.vtok())
            em.wait("g", self.stok())
            for nc8 in range(8):
                bb = bnat[:, (nc8 % 2) * D:(nc8 % 2) * D + D]
                self.sguard("g", (id(bnat), nc8 % 2))
                tl = em.dma("g", bb, self.l1_B[nc8 * 128:(nc8 + 1) * 128, :], ("dbn", nc8 % 2))
                em.wait("t", tl)
                bt = btn[:, (nc8 % 2) * 2048:(nc8 % 2) * 2048 + 2048]
                self.tr16(bb, [bt[:, fq * 512:(fq + 1) * 512] for fq in range(4)],
                          self.idb[:], (6, 7))
                self.sset((id(bnat), nc8 % 2), self.ttok())
                em.wait("t", self.vtok())
                for tt_i in range(2):
                    bi = 2 + tt_i
                    self.use_bank(bi)
                    last = None
                    for f in range(16):
                        lhs = bt[:, f * 128:(f + 1) * 128]
                        em.t.ldweights(lhs)
                        last = em.t.matmul(pb[bi][:], lhs,
                                           x1n[:, f * T + tt_i * 512:f * T + tt_i * 512 + 512],
                                           start=(f == 0), stop=(f == 15)).inc("st")
                    em.wait("s", last)
                    ts = em.s.activation(state[:, nc8 * T + tt_i * 512:nc8 * T + tt_i * 512 + 512],
                                         pb[bi][:], AF.Tanh).inc("ss")
                    self.bank_done(bi, ts)
                    self.sset((id(btn), nc8 % 2), last)
            em.wait("t", self.stok())
            for dch in range(16):
                cb = bnat[:, (dch % 2) * D:(dch % 2) * D + D][:, 0:1024]
                self.sguard("g", (id(bnat), dch % 2))
                tl = em.dma("g", cb, self.l1_C[dch * 128:(dch + 1) * 128, :], ("dbn", dch % 2))
                em.wait("t", tl)
                ct = btn[:, (dch % 2) * 2048:(dch % 2) * 2048 + 1024]
                self.sguard("t", (id(btn), dch % 2))
                for fq in range(2):
                    bi = 6 + fq % 2
                    self.use_bank(bi)
                    last = None
                    for kk in range(4):
                        nb = fq * 4 + kk
                        last = em.t.transpose(pb[bi][:, kk * 128:(kk + 1) * 128],
                                              cb[:, nb * 128:nb * 128 + 128], self.idb[:]).inc("st")
                    em.wait("v", last)
                    tv = em.v.tensor_copy(ct[:, fq * 512:(fq + 1) * 512], pb[bi][:, 0:512]).inc("sv")
                    self.bank_done(bi, tv)
                em.wait("t", self.vtok())
                for tt_i in range(2):
                    bi = 2 + tt_i
                    self.use_bank(bi)
                    last = None
                    for nc8 in range(8):
                        lhs = ct[:, nc8 * 128:(nc8 + 1) * 128]
                        em.t.ldweights(lhs)
                        last = em.t.matmul(pb[bi][:], lhs,
                                           state[:, nc8 * T + tt_i * 512:nc8 * T + tt_i * 512 + 512],
                                           start=(nc8 == 0), stop=(nc8 == 7)).inc("st")
                    em.wait("v", last)
                    em.wait("v", self.l1d_tok)
                    self.sset((id(bnat), dch % 2), last)
                    self.sset((id(btn), dch % 2), last)
                    xsl = x1n[:, dch * T + tt_i * 512:dch * T + tt_i * 512 + 512]
                    gsl = g1[:, dch * T + tt_i * 512:dch * T + tt_i * 512 + 512]
                    dsl = self.gdelta[:, dch * T + tt_i * 512:dch * T + tt_i * 512 + 512]
                    em.v.scalar_tensor_tensor(lascr[:, 0:512], xsl, prm[:, 224 + dch:225 + dch],
                                              pb[bi][:], ALU.mult, ALU.add)
                    tvv = self.vtok()
                    self.bank_done(bi, tvv)
                    self.vf()
                    em.v.tensor_mul(lascr[:, 512:1024], lascr[:, 0:512], gsl)
                    self.vf()
                    em.v.tensor_add(lbscr[:, 0:512], lascr[:, 512:1024], dsl)
                    self.vf()
                    em.v.tensor_copy(dsl, lbscr[:, 0:512])
                    self.vf()
        self.tap("delta1", self.gdelta[:], self.vtok())

    def outstage(self):
        nc, em, pb = self.nc, self.em, self.pb
        with nc.sbuf_tensor("oxt", [128, 2 * D], F32) as oxt, \
             nc.sbuf_tensor("odt", [128, D], BF16) as odt, \
             nc.sbuf_tensor("oot", [128, 2 * D], F32) as oot:
            for c in range(8):
                buf = oxt[:, (c % 2) * D:(c % 2) * D + D]
                self.sguard("q", ("ox", c % 2))
                tl = em.dma("q", buf, self.x_in[c * 128:(c + 1) * 128, :], ("dox", c % 2))
                for fq in range(4):
                    bi = 6 + fq % 2
                    self.use_bank(bi)
                    last = None
                    for k in range(4):
                        f = fq * 4 + k
                        last = em.t.transpose(pb[bi][:, k * 128:(k + 1) * 128],
                                              self.gdelta[:, f * T + c * 128:f * T + c * 128 + 128],
                                              self.idb[:]).inc("st")
                    em.wait("v", last)
                    tv = em.v.tensor_copy(odt[:, fq * 512:(fq + 1) * 512], pb[bi][:, 0:512]).inc("sv")
                    self.bank_done(bi, tv)
                em.wait("v", tl)
                self.vf()
                ob = oot[:, (c % 2) * D:(c % 2) * D + D]
                self.sguard("v", ("oo", c % 2))
                tv = em.v.tensor_add(ob, buf, odt[:]).inc("sv")
                self.vf()
                self.sset(("ox", c % 2), tv)
                em.wait("q", tv)
                td = em.dma("q", self.out[c * 128:(c + 1) * 128, :], ob, ("dout", c % 2))
                self.sset(("oo", c % 2), td)
        em.wait("q", em.tok(("dout", 0)))
        em.wait("q", em.tok(("dout", 1)))
        if "do" in em.sems:
            em.wait("q", em.tok("do"))

    def finish(self):
        em, nc = self.em, self.nc
        with nc.Block() as block:
            @block.tensor
            def _(e):
                em.replay("t", e)

            @block.vector
            def _(e):
                em.replay("v", e)

            @block.scalar
            def _(e):
                em.replay("s", e)

            @block.sync
            def _(e):
                em.replay("q", e)

            @block.gpsimd
            def _(e):
                em.replay("g", e)
        for cm in self.sem_ctxs:
            cm.__exit__(None, None, None)


def make_consts():
    jj, ii = np.meshgrid(np.arange(128), np.arange(128), indexing="ij")
    oh8 = np.zeros((8, 8 * 128), dtype=np.float32)
    for c in range(8):
        oh8[c, c * 128:(c + 1) * 128] = 1.0
    oh32 = np.zeros((32, 32 * 128), dtype=ml_dtypes.bfloat16)
    for c in range(32):
        oh32[c, c * 128:(c + 1) * 128] = 1.0
    oh16s = np.zeros((16, 16 * 128), dtype=ml_dtypes.bfloat16)
    for c in range(16):
        oh16s[c, c * 128:(c + 1) * 128] = 0.05
    return dict(
        ident_f=np.eye(128, dtype=np.float32),
        ident_b=np.eye(128, dtype=ml_dtypes.bfloat16),
        ones_f=np.ones((128, 128), dtype=np.float32),
        triu_neg=np.where(jj <= ii, 0.0, -1e30).astype(np.float32),
        oh8=oh8, oh32=oh32, oh16s=oh16s)


_BUILD_CACHE = {}


def get_build(dbg_names=()):
    key = tuple(sorted(dbg_names))
    if key not in _BUILD_CACHE:
        kb = KB(dbg_names)
        kb.build()
        _BUILD_CACHE[key] = kb
    return _BUILD_CACHE[key]


def make_in_maps(inputs, dbg_names=()):
    consts = make_consts()
    x = np.ascontiguousarray(inputs["L0_attn_u"].reshape(8192, D))
    nw_b = np.ascontiguousarray(
        np.broadcast_to(inputs["L0_attn_norm_weight"][None, :], (128, D_IN)))
    zeros_x = np.zeros((T, D), dtype=np.float32)
    zeros_cb = np.zeros((CONVD,), dtype=np.float32)
    in_maps = []
    for core in range(8):
        b = core // 2
        hh = core % 2
        rows = slice(b * 2048 + hh * T, b * 2048 + (hh + 1) * T)
        fr_rows = slice(b * 2048, b * 2048 + T)
        m = dict(
            x_in=np.ascontiguousarray(x[rows]),
            x_front=(np.ascontiguousarray(x[fr_rows]) if hh == 1 else zeros_x),
            fr_conv_b=(inputs["L0_attn_conv_b"] if hh == 1 else zeros_cb),
            w_in=inputs["L0_attn_W_in"],
            conv_w=inputs["L0_attn_conv_w"],
            conv_b=inputs["L0_attn_conv_b"],
            dt_bias=inputs["L0_attn_dt_bias"],
            a_log=inputs["L0_attn_A_log"],
            d_vec=inputs["L0_attn_D"],
            w_out=inputs["L0_attn_W_out"],
            ffn_gamma=inputs["L0_ffn_gamma"],
            ffn_w=inputs["L0_ffn_W"],
            l1_B=inputs["L1_attn_B"],
            l1_C=inputs["L1_attn_C"],
            l1_D=inputs["L1_attn_D"],
            gate_w=inputs["L1_ffn_gate_w"],
            moe_wg=inputs["L1_ffn_Wg"],
            moe_wu=inputs["L1_ffn_Wu"],
            moe_wd=inputs["L1_ffn_Wd"],
            cwg=inputs["L1_ffn_cWg"],
            cwu=inputs["L1_ffn_cWu"],
            cwd=inputs["L1_ffn_cWd"],
            nw_bcast=nw_b,
            **consts,
        )
        in_maps.append(m)
    return in_maps


def kernel(**inputs):
    kb = get_build()
    in_maps = make_in_maps(inputs)
    res = run_bass_kernel_spmd(kb.nc, in_maps, core_ids=list(range(8)))
    out = np.empty((8192, D), dtype=np.float32)
    for core in range(8):
        b = core // 2
        hh = core % 2
        out[b * 2048 + hh * T:b * 2048 + (hh + 1) * T] = res.results[core]["out"]
    return out.reshape(4, 2048, D)
